# version 05
# baseline (speedup 1.0000x reference)
import numpy as np
import ml_dtypes
from contextlib import ExitStack

import concourse.bass as bass
import concourse.mybir as mybir
from concourse.bass_utils import run_bass_kernel_spmd

F32 = mybir.dt.float32
BF16 = mybir.dt.bfloat16
AF = mybir.ActivationFunctionType
ALU = mybir.AluOpType
AX = mybir.AxisListType

T = 4096
D = 1024
DI = 2048
DST = 16
DTR = 64
DFF = 4096
NT = T // 128
ND = D // 128
NDI = DI // 128
NDF = DFF // 128
IT = 512
JT = 512
MC = 512
NMC = T // MC
EPS_LN = 1e-5
EPS_RMS = 1e-12
NCORES = 8
CVALS = [EPS_LN, EPS_RMS, 1.0 / D, 1.0 / 32.0, 1.0, -1.0, 0.0, 2.0**-9, 7.5]


class Sched:
    ENGINES = ("sync", "gpsimd", "vector", "scalar", "tensor")

    def __init__(self):
        self.steps = {e: [] for e in self.ENGINES}
        self.counts = {}
        self.phase = "init"
        self.nc = None

    pad_emitters = {}

    def step(self, eng, fn, wait=(), inc=None, inc_n=1, dma_n=0, pad=False):
        waits = tuple((c, self.counts[c]) for c in wait if self.counts.get(c, 0) > 0)
        if inc is not None and dma_n:
            self.counts[inc] = self.counts.get(inc, 0) + 16 * dma_n
            self.steps[eng].append((fn, waits, None, 0, False, self.phase))
        elif inc is not None:
            self.counts[inc] = self.counts.get(inc, 0) + inc_n
            self.steps[eng].append((fn, waits, inc, inc_n, pad, self.phase))
        else:
            self.steps[eng].append((fn, waits, None, 0, False, self.phase))

    def emit(self, eng_obj, eng_name, sems):
        last_wait = {}
        for fn, waits, inc, inc_n, pad, phase in self.steps[eng_name]:
            with self.nc.named_scope(phase):
                for chn, val in waits:
                    if last_wait.get(chn, -1) < val:
                        eng_obj.wait_ge(sems[chn], val)
                        last_wait[chn] = val
                instr = fn(eng_obj)
                if pad and eng_name in self.pad_emitters:
                    instr = self.pad_emitters[eng_name](eng_obj)
                if inc is not None:
                    assert instr is not None, f"step on {eng_name} returned None"
                    instr.then_inc(sems[inc], inc_n)


def _host_consts():
    identf = np.eye(128, dtype=np.float32)
    identb = np.eye(128).astype(ml_dtypes.bfloat16)
    onesf = np.ones((1, 128), dtype=np.float32)
    tril = np.zeros((4, 128, JT), dtype=np.float32)
    jj = np.arange(JT)[None, :]
    pp = np.arange(128)[:, None]
    for k in range(4):
        tril[k] = (jj <= 128 * k + pp).astype(np.float32)
    probe = np.array([[2.0**-30, 1.0, 2.0**30, 2.0**49, 2.0**20,
                       1.5 * 2.0**52, 1.5 * 2.0**23,
                       float(np.sqrt(2.0) * 2.0**-27), float(2.0**-12),
                       0.0, float(T * T), float(T * T - 1), 1.25,
                       0.0, 0.0, 0.0]], dtype=np.float32)
    cvals = np.broadcast_to(np.array(CVALS, dtype=np.float32)[None, :],
                            (128, len(CVALS))).copy()
    return dict(identf_h=identf, identb_h=identb, onesf_h=onesf,
                tril_h=np.ascontiguousarray(tril.reshape(4 * 128, JT)),
                probe_h=probe, cvals_h=cvals)


def build_nc(dbg=()):
    nc = bass.Bass("TRN2", target_bir_lowering=False,
                   detect_race_conditions=False)
    S = Sched()
    S.nc = nc
    ctx = ExitStack()

    def din(name, shape, dt=F32):
        return nc.dram_tensor(name, list(shape), dt, kind="ExternalInput").ap()

    x_in = din("x_in", (T, D))
    Wq = din("L0_attn_Wq", (D, D)); bq = din("L0_attn_bq", (1, D))
    Wk = din("L0_attn_Wk", (D, D)); bk = din("L0_attn_bk", (1, D))
    Wv = din("L0_attn_Wv", (D, D)); bv = din("L0_attn_bv", (1, D))
    Wo = din("L0_attn_Wo", (D, D)); bo = din("L0_attn_bo", (1, D))
    Wtq = din("L0_attn_Wtq", (2, D)); btq = din("L0_attn_btq", (1, 2))
    Wfq = din("L0_attn_Wfq", (D, 2)); bfq = din("L0_attn_bfq", (1, D))
    w8 = din("L0_ffn_weight_fp8", (D, D)); b8 = din("L0_ffn_bias", (1, D))
    s8in = din("L0_ffn_input_scale", (1, 1)); s8w = din("L0_ffn_weight_scale", (1, 1))
    inpw = din("L1_attn_in_proj_w", (2 * DI, D))
    convw = din("L1_attn_conv_w", (DI, 4)); convb = din("L1_attn_conv_b", (1, DI))
    xprw = din("L1_attn_x_proj_w", (DTR + 2 * DST, DI))
    dtw = din("L1_attn_dt_proj_w", (DI, DTR)); dtb = din("L1_attn_dt_proj_b", (1, DI))
    alog = din("L1_attn_A_log", (DI, DST)); dD = din("L1_attn_D", (1, DI))
    outw = din("L1_attn_out_proj_w", (D, DI))
    nw = din("L1_attn_norm_w", (1, D))
    w1 = din("L1_attn_ffn_w1", (DFF, D)); b1 = din("L1_attn_ffn_b1", (1, DFF))
    w2 = din("L1_attn_ffn_w2", (D, DFF)); b2 = din("L1_attn_ffn_b2", (1, D))
    fnw = din("L1_attn_ffn_norm_w", (1, D))
    wL1 = din("L1_ffn_weight", (D, D)); bL1 = din("L1_ffn_bias", (1, D))
    identf_h = din("identf_h", (128, 128))
    identb_h = din("identb_h", (128, 128), BF16)
    cvals_h = din("cvals_h", (128, len(CVALS)))
    onesf_h = din("onesf_h", (1, 128))
    tril_h = din("tril_h", (4 * 128, JT))
    probe_h = din("probe_h", (1, 16))

    out = nc.dram_tensor("out", [T, D], F32, kind="ExternalOutput").ap()
    dbg_out = {}
    for name, shape, dt in dbg:
        dbg_out[name] = nc.dram_tensor("dbg_" + name, list(shape), dt,
                                       kind="ExternalOutput").ap()

    def dsc(name, shape, dt):
        return nc.dram_tensor(name, list(shape), dt).ap()

    qT_d = dsc("qT_d", (D, T), BF16)
    kT_d = dsc("kT_d", (D, T), BF16)
    V_d = dsc("V_d", (T, D), BF16)
    ffn0_d = dsc("ffn0_d", (T, D), F32)
    x1_d = dsc("x1_d", (T, D), F32)
    xn1_d = dsc("xn1_d", (T, D), F32)
    xn1T_d = dsc("xn1T_d", (D, T), BF16)
    xdT_d = dsc("xdT_d", (D, T), BF16)
    u_d = dsc("u_d", (DI, T), BF16)
    dl_d = dsc("dl_d", (DI, T), BF16)
    wsc_d = dsc("wsc_d", (DI, T), BF16)
    sz_d = dsc("sz_d", (DI, T), BF16)
    hm_d = dsc("hm_d", (D, T), BF16)
    ffn1_d = dsc("ffn1_d", (T, D), F32)
    xn_dbg = dsc("xn_dbg", (T, D), F32)
    stats_dbg = dsc("stats_dbg", (128, 96), F32)
    qpx_d = dsc("qpx_d", (1, T), F32)
    qpy_d = dsc("qpy_d", (1, T), F32)
    bm_d = dsc("bm_d", (DST, T), F32)
    cm_d = dsc("cm_d", (DST, T), F32)
    pT_d = dsc("pT_d", (T, IT), BF16)

    scopes = [ctx]

    def sb(name, shape, dt=F32):
        return scopes[-1].enter_context(nc.sbuf_tensor(name, list(shape), dt))

    def push_scope():
        scopes.append(ExitStack())

    def pop_scope():
        scopes.pop().close()

    def barrier():
        for e in Sched.ENGINES:
            S.step(e, lambda eng: None, wait=("cL", "cG", "cT", "cV", "cS"))

    def psum(name, shape, dt=F32):
        return ctx.enter_context(nc.psum_tensor(name, list(shape), dt))

    sems = {}
    for chn in ("cL", "cG", "cT", "cV", "cS"):
        sems[chn] = ctx.enter_context(nc.semaphore(chn))

    identf = sb("identf", (128, 128))
    identb = sb("identb", (128, 128), BF16)
    onesf = sb("onesf", (1, 128))
    probe_sb = sb("probe_sb", (1, 16))
    CST = sb("CST", (128, 16))
    W_ = sb("W_", (1, 32))
    stat_r = sb("stat_r", (128, NT))
    stat_mb = sb("stat_mb", (128, NT))
    stat_tmp = sb("stat_tmp", (128, NT))
    stat_nm = sb("stat_nm", (128, NT))
    stat_ve = sb("stat_ve", (128, NT))
    stat_s = sb("stat_s", (128, NT))
    qpN = sb("qpN", (128, NT * 2))
    rembN = sb("rembN", (128, NT * 2))
    bqc = sb("bqc", (128, ND)); bkc = sb("bkc", (128, ND))
    btq_sb = sb("btq_sb", (2, 1))
    b1c = sb("b1c", (128, NDF))
    dtbc = sb("dtbc", (128, NDI))
    convbc = sb("convbc", (128, NDI))
    convwc = sb("convwc", (128, NDI * 4))
    Dc = sb("Dc", (128, NDI))
    Ac = sb("Ac", (128, NDI * DST))
    scan_st = sb("scan_st", (128, NDI * DST))
    wld_buf = sb("wld_buf", (128, 4096))
    brow = sb("brow", (1, D))
    cvals_sb = sb("cvals_sb", (128, len(CVALS)))
    vdump = sb("vdump", (128, 128))
    sdump = sb("sdump", (128, 128))
    S.pad_emitters = {
        "vector": lambda eng: eng.tensor_scalar_add(vdump[:], identf[:], 0.0),
        "scalar": lambda eng: eng.activation(out=sdump[:], in_=identf[:],
                                             func=AF.Identity),
    }

    def vgap(eng):
        return eng.tensor_scalar_add(vdump[:], identf[:], 0.0)
    for _i, _v in enumerate(CVALS):
        nc.const_aps.register(F32, float(_v), cvals_sb[:, _i:_i + 1])
    PS = [psum(f"ps{i}", (128, 512)) for i in range(6)]
    PSB = psum("psb", (128, 1024), BF16)
    PS7 = psum("ps7", (128, 512))

    def mk_bcast(src_ap, dst_sb, scl):
        def ldr(eng, src=src_ap):
            eng.dma_start(out=brow[:], in_=src).then_inc(sems["cL"], 16)
            return None
        S.step("sync", ldr, wait=("cT",), inc="cL", dma_n=1)

        def bc_t(eng):
            eng.matmul(PS[1][:, :], onesf[:], brow[:, 0:512], start=True, stop=True)
            return eng.matmul(PS[2][:, :], onesf[:], brow[:, 512:1024],
                              start=True, stop=True)
        S.step("tensor", bc_t, wait=("cL", "cV"), inc="cT")

        def bc_v(eng, dst=dst_sb, scl=scl):
            eng.tensor_scalar_mul(dst[:, 0:512], PS[1][:, :], scl)
            return eng.tensor_scalar_mul(dst[:, 512:1024], PS[2][:, :], scl)
        S.step("vector", bc_v, wait=("cT",), inc="cV")

    S.phase = "C"
    def ldc(eng):
        eng.dma_start(out=identf[:], in_=identf_h).then_inc(sems["cL"], 16)
        eng.dma_start(out=identb[:], in_=identb_h).then_inc(sems["cL"], 16)
        eng.dma_start(out=onesf[:], in_=onesf_h).then_inc(sems["cL"], 16)
        eng.dma_start(out=probe_sb[:], in_=probe_h).then_inc(sems["cL"], 16)
        eng.dma_start(out=cvals_sb[:], in_=cvals_h).then_inc(sems["cL"], 16)
        eng.dma_start(out=btq_sb[:], in_=btq.rearrange("o t -> t o")
                      ).then_inc(sems["cL"], 16)
        eng.dma_start(out=W_[:, 20:21], in_=s8in).then_inc(sems["cL"], 16)
        eng.dma_start(out=W_[:, 21:22], in_=s8w).then_inc(sems["cL"], 16)
        eng.dma_start(out=bqc[:], in_=bq.rearrange("o (j p) -> p (o j)", p=128)
                      ).then_inc(sems["cL"], 16)
        eng.dma_start(out=bkc[:], in_=bk.rearrange("o (j p) -> p (o j)", p=128)
                      ).then_inc(sems["cL"], 16)
        eng.dma_start(out=b1c[:], in_=b1.rearrange("o (j p) -> p (o j)", p=128)
                      ).then_inc(sems["cL"], 16)
        eng.dma_start(out=dtbc[:], in_=dtb.rearrange("o (j p) -> p (o j)", p=128)
                      ).then_inc(sems["cL"], 16)
        eng.dma_start(out=convbc[:], in_=convb.rearrange("o (j p) -> p (o j)", p=128)
                      ).then_inc(sems["cL"], 16)
        eng.dma_start(out=Dc[:], in_=dD.rearrange("o (j p) -> p (o j)", p=128)
                      ).then_inc(sems["cL"], 16)
        eng.dma_start(out=convwc.rearrange("p (j k) -> p j k", k=4),
                      in_=convw.rearrange("(j p) k -> p j k", p=128)
                      ).then_inc(sems["cL"], 16)
        eng.dma_start(out=Ac.rearrange("p (j n) -> p j n", n=DST),
                      in_=alog.rearrange("(j p) n -> p j n", p=128)
                      ).then_inc(sems["cL"], 16)
        return None
    S.step("sync", ldc, inc="cL", dma_n=16)

    def consts_v(eng):
        pc = probe_sb
        nop8 = lambda: vgap(eng)
        eng.tensor_add(W_[:, 0:1], pc[:, 0:1], pc[:, 1:2]); nop8()
        eng.tensor_sub(W_[:, 1:2], W_[:, 0:1], pc[:, 1:2]); nop8()
        eng.tensor_mul(W_[:, 2:3], W_[:, 1:2], pc[:, 2:3]); nop8()
        eng.tensor_sub(W_[:, 3:4], pc[:, 1:2], W_[:, 2:3]); nop8()
        eng.tensor_mul(W_[:, 4:5], W_[:, 2:3], pc[:, 3:4])
        eng.tensor_mul(W_[:, 5:6], W_[:, 3:4], pc[:, 4:5]); nop8()
        eng.tensor_add(W_[:, 6:7], W_[:, 4:5], W_[:, 5:6]); nop8()
        eng.tensor_add(W_[:, 7:8], W_[:, 6:7], pc[:, 1:2])
        eng.tensor_mul(W_[:, 8:9], W_[:, 2:3], pc[:, 5:6])
        eng.tensor_mul(W_[:, 9:10], W_[:, 3:4], pc[:, 6:7]); nop8()
        eng.tensor_add(W_[:, 10:11], W_[:, 8:9], W_[:, 9:10]); nop8()
        eng.tensor_sub(W_[:, 11:12], pc[:, 9:10], W_[:, 10:11])
        eng.tensor_mul(W_[:, 12:13], W_[:, 2:3], pc[:, 7:8])
        eng.tensor_mul(W_[:, 13:14], W_[:, 3:4], pc[:, 8:9]); nop8()
        eng.tensor_add(W_[:, 14:15], W_[:, 12:13], W_[:, 13:14]); nop8()
        eng.reciprocal(W_[:, 22:23], W_[:, 20:21]); nop8()
        return eng.tensor_mul(W_[:, 23:24], W_[:, 20:21], W_[:, 21:22])
    S.step("vector", consts_v, wait=("cL",), inc="cV", pad=True)

    cst_srcs = [(W_[:, 7:8], 0), (W_[:, 10:11], 1), (W_[:, 11:12], 2),
                (W_[:, 14:15], 3), (W_[:, 22:23], 4), (W_[:, 23:24], 5)]

    def cst_bc_t(eng):
        last = None
        for i, (src, col) in enumerate(cst_srcs):
            last = eng.matmul(PS[0][:, i:i + 1], onesf[:], src,
                              start=True, stop=True)
        return last
    S.step("tensor", cst_bc_t, wait=("cV",), inc="cT")

    def cst_bc_v(eng):
        last = None
        for i, (src, col) in enumerate(cst_srcs):
            last = eng.tensor_copy(CST[:, col:col + 1], PS[0][:, i:i + 1])
        return last
    S.step("vector", cst_bc_v, wait=("cT",), inc="cV", pad=True)

    def r75_v(eng):
        return eng.reciprocal(CST[:, 7:8], cvals_sb[:, 8:9])
    S.step("vector", r75_v, wait=("cL",), inc="cV", pad=True)

    atmp = sb("atmp", (128, NDI * DST))

    def a_exp(eng):
        return eng.activation(out=atmp[:], in_=Ac[:], func=AF.Exp)
    S.step("scalar", a_exp, wait=("cL",), inc="cS")

    def a_neg(eng):
        return eng.tensor_scalar_mul(Ac[:], atmp[:], -1.0)
    S.step("vector", a_neg, wait=("cS",), inc="cV")

    def st_zero(eng):
        return eng.memset(scan_st[:], 0.0)
    S.step("vector", st_zero, inc="cV")

    def weight_T(wap, R, C, dst_sb):
        nr, ncl = R // 128, C // 128
        for rt in range(nr):
            def ldw(eng, rt=rt, wap=wap, C=C):
                eng.dma_start(out=wld_buf[:, 0:C],
                              in_=wap[rt * 128:(rt + 1) * 128, :]
                              ).then_inc(sems["cL"], 16)
                return None
            S.step("sync", ldw, wait=("cV",), inc="cL", dma_n=1)
            for cg in range(0, ncl, 4):
                ng = min(4, ncl - cg)

                def trn_g(eng, cg=cg, ng=ng):
                    last = None
                    for i in range(ng):
                        last = eng.transpose(PS7[:, i * 128:(i + 1) * 128],
                                             wld_buf[:, (cg + i) * 128:(cg + i + 1) * 128],
                                             identf[:])
                    return last
                S.step("tensor", trn_g, wait=("cL", "cV"), inc="cT")

                def ev_g(eng, cg=cg, ng=ng, rt=rt, dst_sb=dst_sb, R=R):
                    last = None
                    for i in range(ng):
                        c = cg + i
                        last = eng.tensor_copy(
                            dst_sb[:, c * R + rt * 128:c * R + (rt + 1) * 128],
                            PS7[:, i * 128:(i + 1) * 128])
                    return last
                S.step("vector", ev_g, wait=("cT",), inc="cV")

    S.phase = "P1"
    push_scope()
    bvc_bc = sb("bvc_bc", (128, D))
    b8c_bc = sb("b8c_bc", (128, D))
    mk_bcast(bv, bvc_bc, 1.0)
    mk_bcast(b8, b8c_bc, 1.0)
    WqTs = sb("WqTs", (128, D * ND), BF16)
    WkTs = sb("WkTs", (128, D * ND), BF16)
    WvTs = sb("WvTs", (128, D * ND), BF16)
    w8Ts = sb("w8Ts", (128, D * ND), BF16)
    WtqT = sb("WtqT", (128, ND * 2))

    weight_T(Wq, D, D, WqTs)
    weight_T(Wk, D, D, WkTs)
    weight_T(Wv, D, D, WvTs)
    weight_T(w8, D, D, w8Ts)

    def ldwtq(eng):
        for c in range(2):
            eng.dma_start(out=WtqT.rearrange("p (j c) -> p j c", c=2)[:, :, c],
                          in_=Wtq.rearrange("c (j p) -> c p j", p=128)[c]
                          ).then_inc(sems["cL"], 16)
        return None
    S.step("sync", ldwtq, inc="cL", dma_n=2)

    x_buf = [sb(f"x_buf{i}", (128, D)) for i in range(2)]
    ln_sc = [sb(f"ln_sc{i}", (128, D)) for i in range(3)]
    q8_sc = [sb(f"q8_sc{i}", (128, D)) for i in range(5)]
    aug = sb("aug", (128, 4))
    aug_sq = sb("aug_sq", (128, 2))

    def aug_ones(eng):
        return eng.memset(aug[:, 3:4], 1.0)
    S.step("vector", aug_ones, inc="cV")
    qp2 = sb("qp2", (2, 512))
    xnT_c = sb("xnT_c", (128, ND * 512), BF16)
    xnTf_c = sb("xnTf_c", (128, ND * 512))
    xdqT_c = sb("xdqT_c", (128, ND * 512), BF16)
    proj_ev = [sb(f"proj_ev{i}", (128, 512), BF16) for i in range(4)]
    vfe_ev = [sb(f"vfe_ev{i}", (128, D)) for i in range(2)]
    vfe_bf = [sb(f"vfe_bf{i}", (128, D), BF16) for i in range(2)]

    def p1_tile(tt):
        xb = x_buf[tt % 2]

        def ldx(eng, tt=tt, xb=xb):
            eng.dma_start(out=xb[:], in_=x_in[tt * 128:(tt + 1) * 128, :]
                          ).then_inc(sems["cL"], 16)
            return None
        S.step("sync", ldx, wait=("cS",), inc="cL", dma_n=1)

        def red(eng, tt=tt, xb=xb):
            eng.reduce_sum(stat_tmp[:, tt:tt + 1], xb[:], axis=AX.X,
                           negate=True)
            vgap(eng)
            return eng.tensor_scalar_mul(stat_nm[:, tt:tt + 1],
                                         stat_tmp[:, tt:tt + 1], 1.0 / D)
        S.step("vector", red, wait=("cL",), inc="cV", pad=True)

        def ln_a(eng, tt=tt, xb=xb):
            eng.activation(out=ln_sc[0][:], in_=xb[:], func=AF.Identity,
                           bias=stat_nm[:, tt:tt + 1])
            return eng.activation(out=ln_sc[1][:], in_=ln_sc[0][:],
                                  func=AF.Square,
                                  accum_out=stat_tmp[:, tt:tt + 1])
        S.step("scalar", ln_a, wait=("cV",), inc="cS", pad=True)

        def ln_v2(eng, tt=tt):
            return eng.tensor_scalar(out=stat_ve[:, tt:tt + 1],
                                     in0=stat_tmp[:, tt:tt + 1],
                                     scalar1=1.0 / D, scalar2=EPS_LN,
                                     op0=ALU.mult, op1=ALU.add)
        S.step("vector", ln_v2, wait=("cS",), inc="cV", pad=True)

        def ln_s2(eng, tt=tt):
            return eng.activation(out=stat_s[:, tt:tt + 1],
                                  in_=stat_ve[:, tt:tt + 1], func=AF.Sqrt)
        S.step("scalar", ln_s2, wait=("cV",), inc="cS", pad=True)

        def ln_v3(eng, tt=tt):
            eng.reciprocal(stat_r[:, tt:tt + 1], stat_s[:, tt:tt + 1])
            vgap(eng)
            eng.tensor_mul(stat_ve[:, tt:tt + 1], stat_nm[:, tt:tt + 1],
                           stat_r[:, tt:tt + 1])
            vgap(eng)
            return eng.tensor_scalar_mul(stat_mb[:, tt:tt + 1],
                                         stat_ve[:, tt:tt + 1], 1.0)
        S.step("vector", ln_v3, wait=("cS",), inc="cV", pad=True)

        def ln_b(eng, tt=tt, xb=xb):
            return eng.activation(out=ln_sc[2][:], in_=xb[:], func=AF.Identity,
                                  scale=stat_r[:, tt:tt + 1],
                                  bias=stat_mb[:, tt:tt + 1])
        S.step("scalar", ln_b, wait=("cV", "cG"), inc="cS")

        xn = ln_sc[2]

        if dbg:
            def st_xn_dbg(eng, tt=tt, xn=xn):
                eng.dma_start(out=xn_dbg[tt * 128:(tt + 1) * 128, :], in_=xn[:]
                              ).then_inc(sems["cG"], 16)
                return None
            S.step("gpsimd", st_xn_dbg, wait=("cS",), inc="cG", dma_n=1)

        def q8(eng, xn=xn):
            Cv, M0 = CST[:, 0:1], CST[:, 1:2]
            rsin = CST[:, 4:5]
            w1_, w2_, w3_, w4_, w5_ = q8_sc
            eng.tensor_scalar(out=w1_[:], in0=xn[:], scalar1=rsin, scalar2=448.0,
                              op0=ALU.mult, op1=ALU.min)
            eng.tensor_scalar(out=w5_[:], in0=w1_[:], scalar1=-448.0, scalar2=None,
                              op0=ALU.max)
            eng.tensor_scalar(out=w2_[:], in0=w5_[:], scalar1=Cv, scalar2=None,
                              op0=ALU.mult)
            eng.tensor_sub(w3_[:], w2_[:], w5_[:])
            eng.tensor_sub(w4_[:], w2_[:], w3_[:])
            eng.tensor_scalar(out=w2_[:], in0=w5_[:], scalar1=512.0, scalar2=None,
                              op0=ALU.mult)
            eng.tensor_scalar(out=w3_[:], in0=w2_[:], scalar1=M0, scalar2=None,
                              op0=ALU.add)
            eng.tensor_scalar(out=w2_[:], in0=w3_[:], scalar1=M0,
                              scalar2=float(2.0**-9),
                              op0=ALU.subtract, op1=ALU.mult)
            eng.tensor_scalar(out=w1_[:], in0=w5_[:], scalar1=-1.0, scalar2=None,
                              op0=ALU.mult)
            eng.tensor_max(w3_[:], w1_[:], w5_[:])
            eng.tensor_single_scalar(out=w1_[:], in_=w3_[:], scalar=float(2.0**-6),
                                     op=ALU.is_lt)
            eng.tensor_sub(w3_[:], w2_[:], w4_[:])
            eng.tensor_mul(w2_[:], w3_[:], w1_[:])
            return eng.tensor_add(w5_[:], w2_[:], w4_[:])
        S.step("vector", q8, wait=("cS",), inc="cV")
        q8v = q8_sc[4]

        ci = tt % 4
        for d0 in range(0, ND, 2):
            def tr2(eng, d0=d0, xn=xn):
                eng.transpose(PS[4][:, 0:128], xn[:, d0 * 128:(d0 + 1) * 128],
                              identf[:])
                return eng.transpose(PS[5][:, 0:128],
                                     xn[:, (d0 + 1) * 128:(d0 + 2) * 128],
                                     identf[:])
            S.step("tensor", tr2, wait=("cV", "cS"), inc="cT")

            def ev2(eng, d0=d0, ci=ci):
                c0 = d0 * 512 + ci * 128
                c1 = (d0 + 1) * 512 + ci * 128
                eng.tensor_copy(xnT_c[:, c0:c0 + 128], PS[4][:, 0:128])
                eng.tensor_copy(xnTf_c[:, c0:c0 + 128], PS[4][:, 0:128])
                eng.tensor_copy(xnT_c[:, c1:c1 + 128], PS[5][:, 0:128])
                return eng.tensor_copy(xnTf_c[:, c1:c1 + 128], PS[5][:, 0:128])
            S.step("vector", ev2, wait=("cT",), inc="cV")

        for d0 in range(0, ND, 2):
            def tr2q(eng, d0=d0, q8v=q8v):
                eng.transpose(PS[4][:, 0:128], q8v[:, d0 * 128:(d0 + 1) * 128],
                              identf[:])
                return eng.transpose(PS[5][:, 0:128],
                                     q8v[:, (d0 + 1) * 128:(d0 + 2) * 128],
                                     identf[:])
            S.step("tensor", tr2q, wait=("cV",), inc="cT")

            def ev2q(eng, d0=d0, ci=ci):
                c0 = d0 * 512 + ci * 128
                c1 = (d0 + 1) * 512 + ci * 128
                eng.tensor_copy(xdqT_c[:, c0:c0 + 128], PS[4][:, 0:128])
                return eng.tensor_copy(xdqT_c[:, c1:c1 + 128], PS[5][:, 0:128])
            S.step("vector", ev2q, wait=("cT",), inc="cV")

        def mkaug(eng, tt=tt):
            eng.memset(aug[:, 3:4], 1.0)
            return eng.tensor_copy(aug[:, 0:2], qpN_hold[tt % 2][:, 0:2]) \
                if False else eng.tensor_copy(aug[:, 0:2], ln_sc[0][:, 0:2])

    qpN_dummy = None

    def p1_chunk_proj(cc):
        ch0 = cc * 512

        def qp_mm(eng):
            last = None
            for d in range(ND):
                last = eng.matmul(PS[0][0:2, 0:512],
                                  WtqT[:, d * 2:(d + 1) * 2],
                                  xnTf_c[:, d * 512:(d + 1) * 512],
                                  start=(d == 0), stop=(d == ND - 1))
            return last
        S.step("tensor", qp_mm, wait=("cV", "cS"), inc="cT")

        def qp_ev(eng, ch0=ch0):
            return eng.activation(out=qp2[:], in_=PS[0][0:2, 0:512],
                                  func=AF.Identity, bias=btq_sb[:])
        S.step("scalar", qp_ev, wait=("cT", "cG"), inc="cS")

        def qp_rows(eng, ch0=ch0):
            eng.dma_start(out=qpx_d[:, ch0:ch0 + 512], in_=qp2[0:1, :]
                          ).then_inc(sems["cG"], 16)
            eng.dma_start(out=qpy_d[:, ch0:ch0 + 512], in_=qp2[1:2, :]
                          ).then_inc(sems["cG"], 16)
            return None
        S.step("gpsimd", qp_rows, wait=("cS",), inc="cG", dma_n=2)

        for si in range(4):
            tt = cc * 4 + si

            def qp_tr(eng, si=si):
                return eng.transpose(PS[4][0:128, 0:2],
                                     qp2[:, si * 128:(si + 1) * 128],
                                     identf[0:2, 0:2])
            S.step("tensor", qp_tr, wait=("cS", "cV"), inc="cT")

            def qp_trev(eng, tt=tt):
                eng.tensor_copy(qpN[:, tt * 2:tt * 2 + 2], PS[4][0:128, 0:2])
                return eng.tensor_copy(aug[:, 0:2], PS[4][0:128, 0:2])
            S.step("vector", qp_trev, wait=("cT",), inc="cV", pad=True)

            def sqa(eng, tt=tt):
                return eng.activation(out=aug_sq[:], in_=qpN[:, tt * 2:tt * 2 + 2],
                                      func=AF.Square)
            S.step("scalar", sqa, wait=("cV",), inc="cS", pad=True)

            def reda(eng):
                return eng.reduce_sum(aug[:, 2:3], aug_sq[:], axis=AX.X)
            S.step("vector", reda, wait=("cS",), inc="cV", pad=True)

            def gmm(eng, tt=tt):
                return eng.matmul(PS7[0:4, 0:4], aug[:], aug[:],
                                  start=(tt == 0), stop=(tt == NT - 1),
                                  skip_group_check=True)
            S.step("tensor", gmm, wait=("cV",), inc="cT")

        for name, Wsl, bcol, dst in (("k", WkTs, bkc, kT_d), ("q", WqTs, bqc, qT_d)):
            for m in range(ND):
                bank = PS[0] if m % 2 == 0 else PS[1]

                def mm(eng, Wsl=Wsl, m=m, bank=bank):
                    last = None
                    for c in range(ND):
                        last = eng.matmul(bank[:, :],
                                          Wsl[:, c * D + m * 128:c * D + (m + 1) * 128],
                                          xnT_c[:, c * 512:(c + 1) * 512],
                                          start=(c == 0), stop=(c == ND - 1))
                    return last
                S.step("tensor", mm, wait=("cV", "cS", "cG"), inc="cT")

                def ev(eng, m=m, bank=bank, bcol=bcol):
                    return eng.activation(out=proj_ev[m % 4][:], in_=bank[:, :],
                                          func=AF.Identity, bias=bcol[:, m:m + 1])
                S.step("scalar", ev, wait=("cT", "cG"), inc="cS")

                def st(eng, m=m, dst=dst, ch0=ch0):
                    eng.dma_start(out=dst[m * 128:(m + 1) * 128, ch0:ch0 + 512],
                                  in_=proj_ev[m % 4][:]).then_inc(sems["cG"], 16)
                    return None
                S.step("gpsimd", st, wait=("cS",), inc="cG", dma_n=1)

        for tsub in range(4):
            tt = cc * 4 + tsub
            for half in range(2):
                bank = PS[2] if half == 0 else PS[3]

                def mmv(eng, tsub=tsub, half=half, bank=bank):
                    last = None
                    for c in range(ND):
                        last = eng.matmul(
                            bank[:, :],
                            xnT_c[:, c * 512 + tsub * 128:c * 512 + (tsub + 1) * 128],
                            WvTs[:, c * D + half * 512:c * D + (half + 1) * 512],
                            start=(c == 0), stop=(c == ND - 1))
                    return last
                S.step("tensor", mmv, wait=("cV", "cG"), inc="cT")

            def evv(eng, tsub=tsub):
                eng.tensor_add(vfe_ev[0][:, 0:512], PS[2][:, :], bvc_bc[:, 0:512])
                eng.tensor_add(vfe_ev[0][:, 512:1024], PS[3][:, :], bvc_bc[:, 512:1024])
                return eng.tensor_copy(vfe_bf[0][:], vfe_ev[0][:])
            S.step("vector", evv, wait=("cT", "cG"), inc="cV")

            def stv(eng, tt=tt):
                eng.dma_start(out=V_d[tt * 128:(tt + 1) * 128, :], in_=vfe_bf[0][:]
                              ).then_inc(sems["cG"], 16)
                return None
            S.step("gpsimd", stv, wait=("cV",), inc="cG", dma_n=1)

            for half in range(2):
                bank = PS[2] if half == 0 else PS[3]

                def mmf(eng, tsub=tsub, half=half, bank=bank):
                    last = None
                    for c in range(ND):
                        last = eng.matmul(
                            bank[:, :],
                            xdqT_c[:, c * 512 + tsub * 128:c * 512 + (tsub + 1) * 128],
                            w8Ts[:, c * D + half * 512:c * D + (half + 1) * 512],
                            start=(c == 0), stop=(c == ND - 1))
                    return last
                S.step("tensor", mmf, wait=("cV", "cG"), inc="cT")

            def evf(eng, tsub=tsub):
                s8p = CST[:, 5:6]
                eng.tensor_scalar(out=vfe_ev[0][:, 0:512], in0=PS[2][:, :],
                                  scalar1=s8p, scalar2=None, op0=ALU.mult)
                eng.tensor_scalar(out=vfe_ev[0][:, 512:1024], in0=PS[3][:, :],
                                  scalar1=s8p, scalar2=None, op0=ALU.mult)
                return eng.tensor_add(vfe_ev[1][:], vfe_ev[0][:], b8c_bc[:])
            S.step("vector", evf, wait=("cT", "cG"), inc="cV")

            def stf(eng, tt=tt):
                eng.dma_start(out=ffn0_d[tt * 128:(tt + 1) * 128, :], in_=vfe_ev[1][:]
                              ).then_inc(sems["cG"], 16)
                return None
            S.step("gpsimd", stf, wait=("cV",), inc="cG", dma_n=1)

    for cc in range(NMC):
        for si in range(4):
            p1_tile(cc * 4 + si)
        p1_chunk_proj(cc)

    S.phase = "TH"
    G_sb = sb("G_sb", (4, 4))
    G9 = sb("G9", (1, 16))

    def g_ev(eng):
        return eng.tensor_copy(G_sb[:], PS7[0:4, 0:4])
    S.step("vector", g_ev, wait=("cT",), inc="cV", pad=True)

    def g_gather(eng):
        for i in range(4):
            eng.dma_start(out=G9[0:1, 4 * i:4 * i + 4], in_=G_sb[i:i + 1, :]
                          ).then_inc(sems["cG"], 16)
        return None
    S.step("gpsimd", g_gather, wait=("cV",), inc="cG", dma_n=4)

    def thr_v(eng):
        g = lambda i: G9[:, i:i + 1]
        nop8 = lambda: vgap(eng)
        pc = probe_sb
        eng.tensor_mul(W_[:, 0:1], g(14), g(15))
        eng.tensor_mul(W_[:, 1:2], g(3), g(3))
        eng.tensor_mul(W_[:, 2:3], g(7), g(7)); nop8()
        eng.tensor_scalar_mul(W_[:, 3:4], W_[:, 0:1], 2.0)
        eng.tensor_add(W_[:, 4:5], W_[:, 1:2], W_[:, 2:3]); nop8()
        eng.tensor_scalar_mul(W_[:, 5:6], W_[:, 4:5], 2.0); nop8()
        eng.tensor_sub(W_[:, 6:7], W_[:, 3:4], W_[:, 5:6]); nop8()
        eng.tensor_scalar_mul(W_[:, 7:8], W_[:, 6:7], float(2.0**-24))
        eng.tensor_mul(W_[:, 8:9], g(15), g(10))
        eng.tensor_mul(W_[:, 9:10], g(14), g(14))
        eng.tensor_mul(W_[:, 10:11], g(2), g(3))
        eng.tensor_mul(W_[:, 11:12], g(6), g(7))
        eng.tensor_mul(W_[:, 12:13], g(0), g(0))
        eng.tensor_mul(W_[:, 13:14], g(1), g(1))
        eng.tensor_mul(W_[:, 14:15], g(5), g(5)); nop8()
        eng.tensor_scalar_mul(W_[:, 15:16], W_[:, 8:9], 2.0)
        eng.tensor_scalar_mul(W_[:, 16:17], W_[:, 9:10], 2.0)
        eng.tensor_add(W_[:, 17:18], W_[:, 10:11], W_[:, 11:12])
        eng.tensor_scalar_mul(W_[:, 18:19], W_[:, 13:14], 2.0); nop8()
        eng.tensor_scalar_mul(W_[:, 19:20], W_[:, 17:18], -8.0)
        eng.tensor_add(W_[:, 24:25], W_[:, 12:13], W_[:, 18:19]); nop8()
        eng.tensor_add(W_[:, 25:26], W_[:, 24:25], W_[:, 14:15]); nop8()
        eng.tensor_scalar_mul(W_[:, 26:27], W_[:, 25:26], 4.0)
        eng.tensor_add(W_[:, 27:28], W_[:, 15:16], W_[:, 16:17]); nop8()
        eng.tensor_add(W_[:, 28:29], W_[:, 27:28], W_[:, 19:20]); nop8()
        eng.tensor_add(W_[:, 29:30], W_[:, 28:29], W_[:, 26:27]); nop8()
        eng.tensor_mul(W_[:, 24:25], W_[:, 6:7], W_[:, 6:7]); nop8()
        eng.tensor_scalar_mul(W_[:, 25:26], W_[:, 24:25], float(2.0**-24)); nop8()
        eng.tensor_sub(W_[:, 26:27], W_[:, 29:30], W_[:, 25:26])
        eng.reciprocal(W_[:, 27:28], pc[:, 11:12]); nop8()
        return eng.tensor_mul(W_[:, 28:29], W_[:, 26:27], W_[:, 27:28])
    S.step("vector", thr_v, wait=("cG",), inc="cV", pad=True)

    def thr_s(eng):
        return eng.activation(out=W_[:, 29:30], in_=W_[:, 28:29], func=AF.Sqrt)
    S.step("scalar", thr_s, wait=("cV",), inc="cS", pad=True)

    def thr_v2(eng):
        nop8 = lambda: vgap(eng)
        eng.tensor_scalar_mul(W_[:, 30:31], W_[:, 29:30], 1.25); nop8()
        return eng.tensor_add(W_[:, 31:32], W_[:, 30:31], W_[:, 7:8])
    S.step("vector", thr_v2, wait=("cS",), inc="cV", pad=True)

    def thr_bc_t(eng):
        return eng.matmul(PS[0][:, 0:1], onesf[:], W_[:, 31:32],
                          start=True, stop=True)
    S.step("tensor", thr_bc_t, wait=("cV",), inc="cT")

    def thr_bc_v(eng):
        return eng.tensor_copy(CST[:, 6:7], PS[0][:, 0:1])
    S.step("vector", thr_bc_v, wait=("cT",), inc="cV")

    rn_a = sb("rn_a", (128, NT))
    rn_b = sb("rn_b", (128, NT))

    for tt in range(NT):
        def sq_t(eng, tt=tt):
            return eng.activation(out=aug_sq[:], in_=qpN[:, 2 * tt:2 * tt + 2],
                                  func=AF.Square)
        S.step("scalar", sq_t, wait=("cV",), inc="cS", pad=True)

        def red_t(eng, tt=tt):
            return eng.reduce_sum(rn_a[:, tt:tt + 1], aug_sq[:], axis=AX.X)
        S.step("vector", red_t, wait=("cS",), inc="cV", pad=True)

    def rn_sqrt(eng):
        return eng.activation(out=rn_b[:], in_=rn_a[:], func=AF.Sqrt)
    S.step("scalar", rn_sqrt, wait=("cV",), inc="cS", pad=True)

    def rn_max(eng):
        return eng.tensor_scalar_max(rn_a[:], rn_b[:], 1e-12)
    S.step("vector", rn_max, wait=("cS",), inc="cV")

    def rn_rec(eng):
        vgap(eng)
        return eng.reciprocal(rn_b[:], rn_a[:])
    S.step("vector", rn_rec, inc="cV", pad=True)

    for tt in range(NT):
        def remb_mk(eng, tt=tt):
            return eng.tensor_scalar(out=rembN[:, 2 * tt:2 * tt + 2],
                                     in0=qpN[:, 2 * tt:2 * tt + 2],
                                     scalar1=rn_b[:, tt:tt + 1], scalar2=None,
                                     op0=ALU.mult)
        S.step("vector", remb_mk, wait=("cV",), inc="cV", pad=True)

    if dbg:
        def st_stats_dbg(eng):
            eng.dma_start(out=stats_dbg[:, 0:32], in_=stat_r[:]
                          ).then_inc(sems["cG"], 16)
            eng.dma_start(out=stats_dbg[:, 32:64], in_=stat_mb[:]
                          ).then_inc(sems["cG"], 16)
            eng.dma_start(out=stats_dbg[:, 64:96], in_=stat_tmp[:]
                          ).then_inc(sems["cG"], 16)
            return None
        S.step("gpsimd", st_stats_dbg, wait=("cS", "cV"), inc="cG", dma_n=3)

    barrier()
    pop_scope()

    S.phase = "P3"
    push_scope()
    boc_bc = sb("boc_bc", (128, D))
    bfqc_bc = sb("bfqc_bc", (128, D))
    mk_bcast(bo, boc_bc, 1.0)
    mk_bcast(bfq, bfqc_bc, 0.1)
    tril = sb("tril", (128, 4 * JT), BF16)

    def ldtril(eng):
        eng.dma_start(out=tril.rearrange("p (k j) -> p k j", j=JT),
                      in_=tril_h.rearrange("(k p) j -> p k j", p=128)
                      ).then_inc(sems["cG"], 16)
        return None
    S.step("gpsimd", ldtril, inc="cG", dma_n=1)

    WoTs = sb("WoTs", (128, D * ND), BF16)
    weight_T(Wo, D, D, WoTs)
    WfqT = sb("WfqT", (2, D))

    def ldwfq(eng):
        eng.dma_start(out=WfqT[:], in_=Wfq.rearrange("r c -> c r")
                      ).then_inc(sems["cL"], 16)
        return None
    S.step("sync", ldwfq, inc="cL", dma_n=1)

    QPXR = sb("QPXR", (128, T))
    QPYR = sb("QPYR", (128, T))
    push_scope()
    qprow = sb("qprow", (1, 1024))

    for ch in range(8):
        def ld_qrow(eng, ch=ch):
            eng.dma_start(out=qprow[:, 0:512], in_=qpx_d[:, ch * 512:(ch + 1) * 512]
                          ).then_inc(sems["cL"], 16)
            eng.dma_start(out=qprow[:, 512:1024], in_=qpy_d[:, ch * 512:(ch + 1) * 512]
                          ).then_inc(sems["cL"], 16)
            return None
        S.step("sync", ld_qrow, wait=("cT",), inc="cL", dma_n=2)

        def qpr_t(eng, ch=ch):
            eng.matmul(PS[0][:, :], onesf[:], qprow[:, 0:512],
                       start=True, stop=True)
            return eng.matmul(PS[1][:, :], onesf[:], qprow[:, 512:1024],
                              start=True, stop=True)
        S.step("tensor", qpr_t, wait=("cL", "cV"), inc="cT")

        def qpr_v(eng, ch=ch):
            eng.tensor_copy(QPXR[:, ch * 512:(ch + 1) * 512], PS[0][:, :])
            return eng.tensor_copy(QPYR[:, ch * 512:(ch + 1) * 512], PS[1][:, :])
        S.step("vector", qpr_v, wait=("cT",), inc="cV")

    barrier()
    pop_scope()
    qTi = sb("qTi", (128, ND * 512), BF16)
    kTj = [sb(f"kTj{i}", (128, ND * 512), BF16) for i in range(2)]
    PTev = sb("PTev", (128, 4 * 512), BF16)
    PTrd = [sb(f"PTrd{i}", (128, 512), BF16) for i in range(2)]
    remb2 = sb("remb2", (2, 128))
    ms = [sb(f"ms{i}", (128, 512)) for i in range(6)]
    P0t = sb("P0t", (128, 512), BF16)
    Pmt = sb("Pmt", (128, 512), BF16)
    rs_parts = sb("rs_parts", (128, 32))
    rs4 = sb("rs4", (128, 4))
    rr4 = sb("rr4", (128, 4))
    Vbuf = [sb(f"Vbuf{i}", (128, 512), BF16) for i in range(2)]
    ctxT = sb("ctxT", (128, ND * 512), BF16)
    xq_buf = [sb("xq_buf0", (128, D))] * 2
    fq_buf = [sb("fq_buf0", (128, D))] * 2
    cb = [sb(f"cb{i}", (128, D)) for i in range(4)]
    q6s = [sb(f"q6s{i}", (128, D)) for i in range(5)]
    q6sc = [sb(f"q6sc{i}", (128, 32)) for i in range(4)]
    pev3 = sb("pev3", (128, 256), BF16)
    xdN = sb("xdN", (128, D), BF16)
    x1N = sb("x1N", (128, D))
    xn1N = sb("xn1N", (128, D))
    ln1_sc = [sb("ln1_sc0", (128, D))]
    l1r = sb("l1r", (128, 4))
    l1tmp = sb("l1tmp", (128, 4))
    l1mb = sb("l1mb", (128, 4))

    def flash_itile(it):
        i0 = it * IT

        def ld_q(eng, i0=i0):
            eng.dma_start(out=qTi.rearrange("p (d j) -> p d j", j=512),
                          in_=qT_d.rearrange("(d p) t -> p d t", p=128)[:, :, i0:i0 + 512]
                          ).then_inc(sems["cL"], 16)
            return None
        S.step("sync", ld_q, wait=("cT",), inc="cL", dma_n=1)

        for jt in range(it + 1):
            j0 = jt * JT
            kb = kTj[jt % 2]

            def ld_k(eng, j0=j0, kb=kb):
                eng.dma_start(out=kb.rearrange("p (d j) -> p d j", j=512),
                              in_=kT_d.rearrange("(d p) t -> p d t", p=128)[:, :, j0:j0 + 512]
                              ).then_inc(sems["cL"], 16)
                return None
            S.step("sync", ld_k, wait=("cT",), inc="cL", dma_n=1)

            for isub in range(4):
                ti = it * 4 + isub
                sbank = PS[0] if isub % 2 == 0 else PS[1]

                def s_mm(eng, isub=isub, kb=kb, sbank=sbank):
                    last = None
                    for dd in range(ND):
                        last = eng.matmul(
                            sbank[:, :],
                            qTi[:, dd * 512 + isub * 128:dd * 512 + (isub + 1) * 128],
                            kb[:, dd * 512:(dd + 1) * 512],
                            start=(dd == 0), stop=(dd == ND - 1))
                    return last
                S.step("tensor", s_mm, wait=("cL", "cS", "cV"), inc="cT")

                def m_exp(eng, sbank=sbank):
                    return eng.activation(out=P0t[:], in_=sbank[:, :], func=AF.Exp,
                                          scale=1.0 / 32.0)
                S.step("scalar", m_exp, wait=("cT",), inc="cS")

                def m_sc(eng, ti=ti, j0=j0):
                    eng.tensor_scalar(out=ms[0][:], in0=QPXR[:, j0:j0 + 512],
                                      scalar1=qpN[:, 2 * ti:2 * ti + 1], scalar2=None,
                                      op0=ALU.subtract)
                    return eng.tensor_scalar(out=ms[1][:], in0=QPYR[:, j0:j0 + 512],
                                             scalar1=qpN[:, 2 * ti + 1:2 * ti + 2],
                                             scalar2=None, op0=ALU.subtract)
                S.step("vector", m_sc, wait=("cV", "cT", "cS"), inc="cV")

                def m_sq(eng):
                    eng.activation(out=ms[2][:], in_=ms[0][:], func=AF.Square)
                    eng.activation(out=ms[3][:], in_=ms[1][:], func=AF.Square)
                    return eng.activation(out=ms[4][:], in_=ms[0][:], func=AF.Identity,
                                          scale=CST[:, 3:4])
                S.step("scalar", m_sq, wait=("cV",), inc="cS")

                def m_neg(eng):
                    return eng.activation(out=ms[0][:], in_=ms[1][:], func=AF.Identity,
                                          scale=-1.0)
                S.step("scalar", m_neg, inc="cS")

                def m_v(eng, it=it, jt=jt, isub=isub):
                    eng.tensor_add(ms[5][:], ms[2][:], ms[3][:])
                    eng.tensor_max(ms[2][:], ms[0][:], ms[1][:])
                    eng.tensor_scalar(out=ms[3][:], in0=ms[5][:],
                                      scalar1=CST[:, 6:7], scalar2=None,
                                      op0=ALU.is_le)
                    eng.tensor_tensor(ms[5][:], ms[4][:], ms[2][:], ALU.is_gt)
                    eng.tensor_mul(ms[0][:], ms[3][:], ms[5][:])
                    eng.tensor_sub(ms[1][:], ms[3][:], ms[0][:])
                    if jt == it:
                        eng.tensor_mul(ms[2][:], ms[1][:],
                                       tril[:, isub * JT:(isub + 1) * JT])
                        mfin = ms[2]
                    else:
                        mfin = ms[1]
                    return eng.scalar_tensor_tensor(
                        out=Pmt[:], in0=mfin[:], scalar=1.0, in1=P0t[:],
                        op0=ALU.mult, op1=ALU.mult,
                        accum_out=rs_parts[:, isub * 8 + jt:isub * 8 + jt + 1])
                S.step("vector", m_v, wait=("cS", "cT"), inc="cV")

                for jj4 in range(4):
                    def p_tr(eng, jj4=jj4):
                        return eng.transpose(PSB[:, 0:128],
                                             Pmt[:, jj4 * 128:(jj4 + 1) * 128],
                                             identb[:])
                    S.step("tensor", p_tr, wait=("cV", "cG"), inc="cT")

                    def p_ev(eng, jj4=jj4, isub=isub):
                        return eng.tensor_copy(
                            PTev[:, jj4 * 512 + isub * 128:jj4 * 512 + (isub + 1) * 128],
                            PSB[:, 0:128])
                    S.step("vector", p_ev, wait=("cT", "cG"), inc="cV")

            def st_pt(eng, jt=jt):
                for jj4 in range(4):
                    js = jt * 4 + jj4
                    eng.dma_start(out=pT_d[js * 128:(js + 1) * 128, :],
                                  in_=PTev[:, jj4 * 512:(jj4 + 1) * 512]
                                  ).then_inc(sems["cG"], 16)
                return None
            S.step("gpsimd", st_pt, wait=("cV",), inc="cG", dma_n=4)

        def rs_red(eng, it=it):
            last = None
            for isub in range(4):
                last = eng.reduce_sum(rs4[:, isub:isub + 1],
                                      rs_parts[:, isub * 8:isub * 8 + it + 1],
                                      axis=AX.X)
            return last
        S.step("vector", rs_red, wait=("cV",), inc="cV")

        def rs_rec(eng):
            vgap(eng)
            return eng.reciprocal(rr4[:], rs4[:])
        S.step("vector", rs_rec, inc="cV", pad=True)

        njs = (it + 1) * 4
        for dhalf in range(2):
            for js in range(njs):
                vb = Vbuf[js % 2]
                pb = PTrd[js % 2]

                def ld_v(eng, js=js, dhalf=dhalf, vb=vb, pb=pb):
                    eng.dma_start(out=vb[:],
                                  in_=V_d[js * 128:(js + 1) * 128,
                                          dhalf * 512:(dhalf + 1) * 512]
                                  ).then_inc(sems["cL"], 16)
                    eng.dma_start(out=pb[:], in_=pT_d[js * 128:(js + 1) * 128, :]
                                  ).then_inc(sems["cL"], 16)
                    return None
                S.step("sync", ld_v, wait=("cT", "cG"), inc="cL", dma_n=2)

                def c_mm(eng, js=js, vb=vb, pb=pb, first=(js == 0),
                         last_js=(js == njs - 1)):
                    lasti = None
                    for ds in range(4):
                        lasti = eng.matmul(
                            PS[2 + ds][:, :],
                            vb[:, ds * 128:(ds + 1) * 128],
                            pb[:],
                            start=first, stop=last_js)
                    return lasti
                S.step("tensor", c_mm, wait=("cL", "cV"), inc="cT")

            def c_ev(eng, dhalf=dhalf):
                lasti = None
                for ds in range(4):
                    dd = dhalf * 4 + ds
                    lasti = eng.tensor_copy(ctxT[:, dd * 512:(dd + 1) * 512],
                                            PS[2 + ds][:, :])
                return lasti
            S.step("vector", c_ev, wait=("cT",), inc="cV")

        for isub in range(4):
            ti = it * 4 + isub

            def ao_mm(eng, isub=isub):
                lasti = None
                for half in range(2):
                    for dd in range(ND):
                        lasti = eng.matmul(
                            PS[0 + half][:, :],
                            ctxT[:, dd * 512 + isub * 128:dd * 512 + (isub + 1) * 128],
                            WoTs[:, dd * D + half * 512:dd * D + (half + 1) * 512],
                            start=(dd == 0), stop=(dd == ND - 1))
                return lasti
            S.step("tensor", ao_mm, wait=("cV", "cS"), inc="cT")

            def qo_tr(eng, ti=ti):
                return eng.transpose(PS[4][0:2, 0:128],
                                     rembN[:, 2 * ti:2 * ti + 2], identf[:])
            S.step("tensor", qo_tr, wait=("cV",), inc="cT")

            def qo_ev(eng):
                return eng.tensor_copy(remb2[:], PS[4][0:2, 0:128])
            S.step("vector", qo_ev, wait=("cT",), inc="cV", pad=True)

            def qo_mm(eng, ti=ti):
                eng.matmul(PS[2][:, :], remb2[:], WfqT[:, 0:512],
                           start=True, stop=True)
                return eng.matmul(PS[3][:, :], remb2[:], WfqT[:, 512:1024],
                                  start=True, stop=True)
            S.step("tensor", qo_mm, wait=("cV",), inc="cT")

            def ld_xf(eng, ti=ti, isub=isub):
                eng.dma_start(out=xq_buf[isub % 2][:],
                              in_=x_in[ti * 128:(ti + 1) * 128, :]
                              ).then_inc(sems["cL"], 16)
                eng.dma_start(out=fq_buf[isub % 2][:],
                              in_=ffn0_d[ti * 128:(ti + 1) * 128, :]
                              ).then_inc(sems["cL"], 16)
                return None
            S.step("sync", ld_xf, wait=("cV",), inc="cL", dma_n=2)

            def comb_v(eng, ti=ti, isub=isub):
                eng.tensor_scalar(out=cb[0][:, 0:512], in0=PS[0][:, :],
                                  scalar1=rr4[:, isub:isub + 1], scalar2=None,
                                  op0=ALU.mult)
                eng.tensor_scalar(out=cb[0][:, 512:1024], in0=PS[1][:, :],
                                  scalar1=rr4[:, isub:isub + 1], scalar2=None,
                                  op0=ALU.mult)
                eng.tensor_add(cb[1][:], cb[0][:], boc_bc[:])
                eng.scalar_tensor_tensor(out=cb[2][:, 0:512], in0=PS[2][:, :],
                                         scalar=0.1, in1=cb[1][:, 0:512],
                                         op0=ALU.mult, op1=ALU.add)
                eng.scalar_tensor_tensor(out=cb[2][:, 512:1024], in0=PS[3][:, :],
                                         scalar=0.1, in1=cb[1][:, 512:1024],
                                         op0=ALU.mult, op1=ALU.add)
                return eng.tensor_add(cb[0][:], cb[2][:], bfqc_bc[:])
            S.step("vector", comb_v, wait=("cT", "cL"), inc="cV")

            def comb_s(eng, ti=ti, isub=isub):
                return eng.activation(out=cb[3][:], in_=xq_buf[isub % 2][:],
                                      func=AF.Identity, scale=stat_r[:, ti:ti + 1],
                                      bias=stat_mb[:, ti:ti + 1])
            S.step("scalar", comb_s, wait=("cL",), inc="cS")

            def comb_v2(eng, ti=ti, isub=isub):
                eng.tensor_add(cb[1][:], cb[0][:], cb[3][:])
                eng.tensor_add(cb[2][:], cb[1][:], xq_buf[isub % 2][:])
                return eng.tensor_add(x1N[:], cb[2][:], fq_buf[isub % 2][:])
            S.step("vector", comb_v2, wait=("cS", "cV", "cG"), inc="cV")

            def st_x1(eng, ti=ti):
                eng.dma_start(out=x1_d[ti * 128:(ti + 1) * 128, :], in_=x1N[:]
                              ).then_inc(sems["cG"], 16)
                return None
            S.step("gpsimd", st_x1, wait=("cV",), inc="cG", dma_n=1)

            ic = isub % 4

            def l1_red(eng, ic=ic):
                nop8 = lambda: vgap(eng)
                eng.reduce_sum(l1tmp[:, ic:ic + 1], x1N[:], axis=AX.X, negate=True)
                nop8()
                return eng.tensor_scalar_mul(l1mb[:, ic:ic + 1],
                                             l1tmp[:, ic:ic + 1], 1.0 / D)
            S.step("vector", l1_red, wait=("cV",), inc="cV", pad=True)

            def l1_a(eng, ic=ic):
                eng.activation(out=ln1_sc[0][:], in_=x1N[:], func=AF.Identity,
                               bias=l1mb[:, ic:ic + 1])
                return eng.activation(out=q6s[0][:], in_=ln1_sc[0][:],
                                      func=AF.Square,
                                      accum_out=l1tmp[:, ic:ic + 1])
            S.step("scalar", l1_a, wait=("cV",), inc="cS", pad=True)

            def l1_v2(eng, ic=ic):
                return eng.tensor_scalar(out=l1r[:, ic:ic + 1],
                                         in0=l1tmp[:, ic:ic + 1],
                                         scalar1=1.0 / D, scalar2=EPS_LN,
                                         op0=ALU.mult, op1=ALU.add)
            S.step("vector", l1_v2, wait=("cS",), inc="cV", pad=True)

            def l1_s2(eng, ic=ic):
                return eng.activation(out=l1tmp[:, ic:ic + 1],
                                      in_=l1r[:, ic:ic + 1], func=AF.Sqrt)
            S.step("scalar", l1_s2, wait=("cV",), inc="cS", pad=True)

            def l1_v3(eng, ic=ic):
                nop8 = lambda: vgap(eng)
                eng.reciprocal(l1r[:, ic:ic + 1], l1tmp[:, ic:ic + 1])
                nop8()
                eng.tensor_mul(l1tmp[:, ic:ic + 1], l1mb[:, ic:ic + 1],
                               l1r[:, ic:ic + 1])
                nop8()
                return eng.tensor_scalar_mul(l1mb[:, ic:ic + 1],
                                             l1tmp[:, ic:ic + 1], 1.0)
            S.step("vector", l1_v3, wait=("cS",), inc="cV", pad=True)

            def l1_b(eng, ic=ic):
                return eng.activation(out=xn1N[:], in_=x1N[:], func=AF.Identity,
                                      scale=l1r[:, ic:ic + 1],
                                      bias=l1mb[:, ic:ic + 1])
            S.step("scalar", l1_b, wait=("cV",), inc="cS")

            def st_xn1(eng, ti=ti):
                eng.dma_start(out=xn1_d[ti * 128:(ti + 1) * 128, :], in_=xn1N[:]
                              ).then_inc(sems["cG"], 16)
                return None
            S.step("gpsimd", st_xn1, wait=("cS",), inc="cG", dma_n=1)

            def q6a(eng):
                nop8 = lambda: vgap(eng)
                eng.tensor_reduce(out=q6sc[0].rearrange("p (g o) -> p g o", o=1),
                                  in_=xn1N.rearrange("p (g k) -> p g k", k=32),
                                  axis=AX.X, op=ALU.max, apply_absolute_value=True)
                nop8()
                eng.tensor_scalar(out=q6sc[1][:], in0=q6sc[0][:],
                                  scalar1=CST[:, 7:8], scalar2=None, op0=ALU.mult)
                nop8()
                eng.tensor_scalar_max(q6sc[2][:], q6sc[1][:], 1e-30)
                nop8()
                eng.reciprocal(q6sc[3][:], q6sc[2][:])
                nop8()
                return eng.tensor_mul(
                    q6s[0][:],
                    xn1N.rearrange("p (g k) -> p g k", k=32),
                    q6sc[3].rearrange("p (g o) -> p g o", o=1).broadcast_to((128, 32, 32)))
            S.step("vector", q6a, wait=("cS",), inc="cV")

            def q6b(eng):
                M0 = CST[:, 1:2]
                w1_, w2_, w3_, w4_ = q6s[1], q6s[2], q6s[3], q6s[4]
                q = q6s[0]
                eng.tensor_scalar(out=w1_[:], in0=q[:], scalar1=-1.0, scalar2=None,
                                  op0=ALU.mult)
                eng.tensor_max(w2_[:], w1_[:], q[:])
                eng.tensor_single_scalar(out=w1_[:], in_=w2_[:], scalar=2.0,
                                         op=ALU.is_lt)
                eng.tensor_single_scalar(out=w3_[:], in_=w2_[:], scalar=4.0,
                                         op=ALU.is_lt)
                eng.tensor_scalar(out=w4_[:], in0=w3_[:], scalar1=2.0, scalar2=2.0,
                                  op0=ALU.mult, op1=ALU.add)
                eng.scalar_tensor_tensor(out=w2_[:], in0=w1_[:], scalar=4.0,
                                         in1=w4_[:], op0=ALU.mult, op1=ALU.add)
                eng.tensor_scalar(out=w4_[:], in0=w3_[:], scalar1=-0.25, scalar2=0.5,
                                  op0=ALU.mult, op1=ALU.add)
                eng.scalar_tensor_tensor(out=w3_[:], in0=w1_[:], scalar=-0.125,
                                         in1=w4_[:], op0=ALU.mult, op1=ALU.add)
                eng.tensor_mul(w1_[:], q[:], w2_[:])
                eng.tensor_scalar(out=w4_[:], in0=w1_[:], scalar1=M0, scalar2=None,
                                  op0=ALU.add)
                eng.tensor_scalar(out=w2_[:], in0=w4_[:], scalar1=M0, scalar2=None,
                                  op0=ALU.subtract)
                eng.tensor_sub(w4_[:], w2_[:], w1_[:])
                eng.tensor_single_scalar(out=w1_[:], in_=w4_[:], scalar=0.5,
                                         op=ALU.is_equal)
                eng.tensor_sub(w4_[:], w2_[:], w1_[:])
                eng.tensor_mul(q6s[0][:], w4_[:], w3_[:])
                return eng.tensor_mul(
                    xdN.rearrange("p (g k) -> p g k", k=32),
                    q6s[0].rearrange("p (g k) -> p g k", k=32),
                    q6sc[2].rearrange("p (g o) -> p g o", o=1).broadcast_to((128, 32, 32)))
            S.step("vector", q6b, inc="cV")

            for d0 in range(0, ND, 2):
                def t1(eng, d0=d0):
                    eng.transpose(PS[4][:, 0:128], xn1N[:, d0 * 128:(d0 + 1) * 128],
                                  identf[:])
                    return eng.transpose(PS[5][:, 0:128],
                                         xn1N[:, (d0 + 1) * 128:(d0 + 2) * 128],
                                         identf[:])
                S.step("tensor", t1, wait=("cV", "cS", "cG"), inc="cT")

                def e1(eng, d0=d0):
                    eng.tensor_copy(pev3[:, 0:128], PS[4][:, 0:128])
                    return eng.tensor_copy(pev3[:, 128:256], PS[5][:, 0:128])
                S.step("vector", e1, wait=("cT",), inc="cV")

                def s1(eng, d0=d0, ti=ti):
                    eng.dma_start(out=xn1T_d[d0 * 128:(d0 + 1) * 128,
                                             ti * 128:(ti + 1) * 128],
                                  in_=pev3[:, 0:128]).then_inc(sems["cG"], 16)
                    eng.dma_start(out=xn1T_d[(d0 + 1) * 128:(d0 + 2) * 128,
                                             ti * 128:(ti + 1) * 128],
                                  in_=pev3[:, 128:256]).then_inc(sems["cG"], 16)
                    return None
                S.step("gpsimd", s1, wait=("cV",), inc="cG", dma_n=2)

                def t2(eng, d0=d0):
                    eng.transpose(PSB[:, 0:128], xdN[:, d0 * 128:(d0 + 1) * 128],
                                  identb[:])
                    return eng.transpose(PSB[:, 128:256],
                                         xdN[:, (d0 + 1) * 128:(d0 + 2) * 128],
                                         identb[:])
                S.step("tensor", t2, wait=("cV", "cG"), inc="cT")

                def e2(eng, d0=d0):
                    eng.tensor_copy(pev3[:, 0:128], PSB[:, 0:128])
                    return eng.tensor_copy(pev3[:, 128:256], PSB[:, 128:256])
                S.step("vector", e2, wait=("cT",), inc="cV")

                def s2(eng, d0=d0, ti=ti):
                    eng.dma_start(out=xdT_d[d0 * 128:(d0 + 1) * 128,
                                            ti * 128:(ti + 1) * 128],
                                  in_=pev3[:, 0:128]).then_inc(sems["cG"], 16)
                    eng.dma_start(out=xdT_d[(d0 + 1) * 128:(d0 + 2) * 128,
                                            ti * 128:(ti + 1) * 128],
                                  in_=pev3[:, 128:256]).then_inc(sems["cG"], 16)
                    return None
                S.step("gpsimd", s2, wait=("cV",), inc="cG", dma_n=2)

    for it in range(T // IT):
        flash_itile(it)

    barrier()
    pop_scope()

    S.phase = "P4"
    push_scope()
    bL1c_bc = sb("bL1c_bc", (128, D))
    mk_bcast(bL1, bL1c_bc, 1.0)
    wdT = sb("wdT", (128, ND * D), BF16)
    wqN = [sb(f"wqN{i}", (128, D)) for i in range(2)]
    wqb = sb("wqb", (128, D), BF16)
    p4sc = [sb(f"p4sc{i}", (128, D)) for i in range(4)]
    p4bk = [sb(f"p4bk{i}", (128, 32)) for i in range(4)]
    xdTb = [sb(f"xdTb{i}", (128, ND * 128), BF16) for i in range(2)]
    f1ev = [sb(f"f1ev{i}", (128, D)) for i in range(2)]

    def q6_pipeline(src, dstbf, sc, bk, eng):
        nop8 = lambda: vgap(eng)
        M0 = CST[:, 1:2]
        eng.tensor_reduce(out=bk[0].rearrange("p (g o) -> p g o", o=1),
                          in_=src.rearrange("p (g k) -> p g k", k=32),
                          axis=AX.X, op=ALU.max, apply_absolute_value=True)
        nop8()
        eng.tensor_scalar(out=bk[1][:], in0=bk[0][:],
                          scalar1=CST[:, 7:8], scalar2=None, op0=ALU.mult)
        nop8()
        eng.tensor_scalar_max(bk[2][:], bk[1][:], 1e-30)
        nop8()
        eng.reciprocal(bk[3][:], bk[2][:])
        nop8()
        eng.tensor_mul(sc[0][:],
                       src.rearrange("p (g k) -> p g k", k=32),
                       bk[3].rearrange("p (g o) -> p g o", o=1).broadcast_to((128, 32, 32)))
        eng.tensor_scalar(out=sc[1][:], in0=sc[0][:], scalar1=-1.0, scalar2=None,
                          op0=ALU.mult)
        eng.tensor_max(sc[2][:], sc[1][:], sc[0][:])
        eng.tensor_single_scalar(out=sc[1][:], in_=sc[2][:], scalar=2.0, op=ALU.is_lt)
        eng.tensor_single_scalar(out=sc[3][:], in_=sc[2][:], scalar=4.0, op=ALU.is_lt)
        eng.tensor_scalar(out=sc[2][:], in0=sc[3][:], scalar1=2.0, scalar2=2.0,
                          op0=ALU.mult, op1=ALU.add)
        eng.scalar_tensor_tensor(out=sc[2][:], in0=sc[1][:], scalar=4.0, in1=sc[2][:],
                                 op0=ALU.mult, op1=ALU.add) \
            if False else eng.scalar_tensor_tensor(
                out=p4_k_tmp[:], in0=sc[1][:], scalar=4.0, in1=sc[2][:],
                op0=ALU.mult, op1=ALU.add)
        eng.tensor_scalar(out=sc[2][:], in0=sc[3][:], scalar1=-0.25, scalar2=0.5,
                          op0=ALU.mult, op1=ALU.add)
        eng.scalar_tensor_tensor(out=sc[3][:], in0=sc[1][:], scalar=-0.125,
                                 in1=sc[2][:], op0=ALU.mult, op1=ALU.add)
        eng.tensor_mul(sc[1][:], sc[0][:], p4_k_tmp[:])
        eng.tensor_scalar(out=sc[2][:], in0=sc[1][:], scalar1=M0, scalar2=None,
                          op0=ALU.add)
        eng.tensor_scalar(out=sc[0][:], in0=sc[2][:], scalar1=M0, scalar2=None,
                          op0=ALU.subtract)
        eng.tensor_sub(sc[2][:], sc[0][:], sc[1][:])
        eng.tensor_single_scalar(out=sc[1][:], in_=sc[2][:], scalar=0.5,
                                 op=ALU.is_equal)
        eng.tensor_sub(sc[2][:], sc[0][:], sc[1][:])
        eng.tensor_mul(sc[0][:], sc[2][:], sc[3][:])
        return eng.tensor_mul(
            dstbf.rearrange("p (g k) -> p g k", k=32),
            sc[0].rearrange("p (g k) -> p g k", k=32),
            bk[2].rearrange("p (g o) -> p g o", o=1).broadcast_to((128, 32, 32)))

    p4_k_tmp = sb("p4_k_tmp", (128, D))

    for rt in range(ND):
        def ldw1(eng, rt=rt):
            eng.dma_start(out=wqN[rt % 2][:], in_=wL1[rt * 128:(rt + 1) * 128, :]
                          ).then_inc(sems["cL"], 16)
            return None
        S.step("sync", ldw1, wait=("cT",), inc="cL", dma_n=1)

        def q6w(eng, rt=rt):
            return q6_pipeline(wqN[rt % 2], wqb, p4sc, p4bk, eng)
        S.step("vector", q6w, wait=("cL", "cT"), inc="cV")

        for d0 in range(0, ND, 2):
            def trw(eng, d0=d0):
                eng.transpose(PSB[:, 0:128], wqb[:, d0 * 128:(d0 + 1) * 128],
                              identb[:])
                return eng.transpose(PSB[:, 128:256],
                                     wqb[:, (d0 + 1) * 128:(d0 + 2) * 128],
                                     identb[:])
            S.step("tensor", trw, wait=("cV",), inc="cT")

            def evw(eng, d0=d0, rt=rt):
                eng.tensor_copy(wdT[:, d0 * D + rt * 128:d0 * D + (rt + 1) * 128],
                                PSB[:, 0:128])
                return eng.tensor_copy(
                    wdT[:, (d0 + 1) * D + rt * 128:(d0 + 1) * D + (rt + 1) * 128],
                    PSB[:, 128:256])
            S.step("vector", evw, wait=("cT",), inc="cV")

    for tt in range(NT):
        xb = xdTb[tt % 2]

        def ldxd(eng, tt=tt, xb=xb):
            eng.dma_start(out=xb.rearrange("p (d j) -> p d j", j=128),
                          in_=xdT_d.rearrange("(d p) t -> p d t", p=128)
                          [:, :, tt * 128:(tt + 1) * 128]).then_inc(sems["cL"], 16)
            return None
        S.step("sync", ldxd, wait=("cT",), inc="cL", dma_n=1)

        for half in range(2):
            def mmf1(eng, xb=xb, half=half):
                last = None
                for dd in range(ND):
                    last = eng.matmul(
                        PS[half][:, :],
                        xb[:, dd * 128:(dd + 1) * 128],
                        wdT[:, dd * D + half * 512:dd * D + (half + 1) * 512],
                        start=(dd == 0), stop=(dd == ND - 1))
                return last
            S.step("tensor", mmf1, wait=("cL", "cV", "cG"), inc="cT")

        def evf1(eng, tt=tt):
            eng.tensor_copy(f1ev[0][:, 0:512], PS[0][:, :])
            eng.tensor_copy(f1ev[0][:, 512:1024], PS[1][:, :])
            return eng.tensor_add(f1ev[1][:], f1ev[0][:], bL1c_bc[:])
        S.step("vector", evf1, wait=("cT", "cG"), inc="cV")

        def stf1(eng, tt=tt):
            eng.dma_start(out=ffn1_d[tt * 128:(tt + 1) * 128, :], in_=f1ev[1][:]
                          ).then_inc(sems["cG"], 16)
            return None
        S.step("gpsimd", stf1, wait=("cV",), inc="cG", dma_n=1)

    barrier()
    pop_scope()

    S.phase = "M1"
    push_scope()
    inpT = sb("inpT", (128, ND * 2 * DI), BF16)
    weight_T(inpw, 2 * DI, D, inpT)
    xprT = sb("xprT", (128, NDI * 96), BF16)
    ddtT = sb("ddtT", (64, DI), BF16)

    def ldxpr(eng):
        eng.dma_start(out=wld_buf[0:96, 0:DI], in_=xprw[:, :]).then_inc(sems["cL"], 16)
        return None
    S.step("sync", ldxpr, wait=("cT", "cV"), inc="cL", dma_n=1)

    for c in range(NDI):
        def trxp(eng, c=c):
            return eng.transpose(PS7[:, 0:96],
                                 wld_buf[0:96, c * 128:(c + 1) * 128],
                                 identf[0:96, 0:96])
        S.step("tensor", trxp, wait=("cL", "cV"), inc="cT")

        def evxp(eng, c=c):
            return eng.tensor_copy(xprT[:, c * 96:(c + 1) * 96], PS7[:, 0:96])
        S.step("vector", evxp, wait=("cT",), inc="cV", pad=True)

    def lddtw(eng):
        eng.dma_start(out=wld_buf[:, 0:DTR * NDI].rearrange("p (j r) -> p j r", r=DTR),
                      in_=dtw.rearrange("(j p) r -> p j r", p=128)
                      ).then_inc(sems["cL"], 16)
        return None
    S.step("sync", lddtw, wait=("cT", "cV"), inc="cL", dma_n=1)

    for c in range(NDI):
        def trdt(eng, c=c):
            return eng.transpose(PS7[0:64, 0:128],
                                 wld_buf[:, c * DTR:(c + 1) * DTR], identf[:])
        S.step("tensor", trdt, wait=("cL", "cV"), inc="cT")

        def evdt(eng, c=c):
            return eng.tensor_copy(ddtT[:, c * 128:(c + 1) * 128], PS7[0:64, 0:128])
        S.step("vector", evdt, wait=("cT",), inc="cV", pad=True)

    xnTb = [sb(f"xnTb{i}", (128, ND * MC), BF16) for i in range(2)]
    u_raw = sb("u_raw", (128, NDI * MC))
    ustage = sb("ustage", (128, NDI * MC), BF16)
    u_ext = sb("u_ext", (128, MC + 3))
    cv_t = [sb(f"cv_t{i}", (128, MC)) for i in range(2)]
    halo = sb("halo", (128, NDI * 3))
    zev = sb("zev", (128, MC), BF16)
    xdbl = sb("xdbl", (96, MC))
    dtT_bf = sb("dtT_bf", (64, MC), BF16)
    dl_f = [sb(f"dl_f{i}", (128, MC)) for i in range(4)]
    dl_b = sb("dl_b", (128, MC), BF16)
    w_b = sb("w_b", (128, MC), BF16)

    def halo_z(eng):
        return eng.memset(halo[:], 0.0)
    S.step("vector", halo_z, inc="cV")

    for cc in range(NMC):
        ch0 = cc * MC
        xb = xnTb[cc % 2]

        def ldxn1(eng, ch0=ch0, xb=xb):
            eng.dma_start(out=xb.rearrange("p (d j) -> p d j", j=MC),
                          in_=xn1T_d.rearrange("(d p) t -> p d t", p=128)
                          [:, :, ch0:ch0 + MC]).then_inc(sems["cL"], 16)
            return None
        S.step("sync", ldxn1, wait=("cT",), inc="cL", dma_n=1)

        for m in range(32):
            bank = PS[m % 2]

            def mmxz(eng, m=m, xb=xb, bank=bank):
                last = None
                for c in range(ND):
                    last = eng.matmul(
                        bank[:, :],
                        inpT[:, c * (2 * DI) + m * 128:c * (2 * DI) + (m + 1) * 128],
                        xb[:, c * MC:(c + 1) * MC],
                        start=(c == 0), stop=(c == ND - 1))
                return last
            S.step("tensor", mmxz, wait=("cL", "cV", "cS", "cG"), inc="cT")

            if m < 16:
                def evu(eng, m=m, bank=bank):
                    return eng.tensor_copy(u_raw[:, m * MC:(m + 1) * MC], bank[:, :])
                S.step("vector", evu, wait=("cT",), inc="cV")
            else:
                def evz(eng, bank=bank):
                    return eng.activation(out=zev[:], in_=bank[:, :], func=AF.Silu)
                S.step("scalar", evz, wait=("cT", "cG"), inc="cS")

                def stz(eng, m=m, ch0=ch0):
                    dt = m - 16
                    eng.dma_start(out=sz_d[dt * 128:(dt + 1) * 128, ch0:ch0 + MC],
                                  in_=zev[:]).then_inc(sems["cG"], 16)
                    return None
                S.step("gpsimd", stz, wait=("cS",), inc="cG", dma_n=1)

        for dt in range(NDI):
            def cva(eng, dt=dt):
                eng.tensor_copy(u_ext[:, 0:3], halo[:, dt * 3:(dt + 1) * 3])
                eng.tensor_copy(u_ext[:, 3:MC + 3], u_raw[:, dt * MC:(dt + 1) * MC])
                eng.tensor_copy(halo[:, dt * 3:(dt + 1) * 3],
                                u_raw[:, dt * MC + MC - 3:dt * MC + MC])
                eng.tensor_scalar(out=cv_t[0][:], in0=u_ext[:, 0:MC],
                                  scalar1=convwc[:, dt * 4:dt * 4 + 1], scalar2=None,
                                  op0=ALU.mult)
                eng.scalar_tensor_tensor(out=cv_t[1][:], in0=u_ext[:, 1:MC + 1],
                                         scalar=convwc[:, dt * 4 + 1:dt * 4 + 2],
                                         in1=cv_t[0][:], op0=ALU.mult, op1=ALU.add)
                eng.scalar_tensor_tensor(out=cv_t[0][:], in0=u_ext[:, 2:MC + 2],
                                         scalar=convwc[:, dt * 4 + 2:dt * 4 + 3],
                                         in1=cv_t[1][:], op0=ALU.mult, op1=ALU.add)
                return eng.scalar_tensor_tensor(
                    out=cv_t[1][:], in0=u_ext[:, 3:MC + 3],
                    scalar=convwc[:, dt * 4 + 3:dt * 4 + 4],
                    in1=cv_t[0][:], op0=ALU.mult, op1=ALU.add)
            S.step("vector", cva, wait=("cS", "cG", "cV"), inc="cV")

            def cvs(eng, dt=dt):
                return eng.activation(out=ustage[:, dt * MC:(dt + 1) * MC],
                                      in_=cv_t[1][:], func=AF.Silu,
                                      bias=convbc[:, dt:dt + 1])
            S.step("scalar", cvs, wait=("cV",), inc="cS")

            def stu(eng, dt=dt, ch0=ch0):
                eng.dma_start(out=u_d[dt * 128:(dt + 1) * 128, ch0:ch0 + MC],
                              in_=ustage[:, dt * MC:(dt + 1) * MC]
                              ).then_inc(sems["cG"], 16)
                return None
            S.step("gpsimd", stu, wait=("cS",), inc="cG", dma_n=1)

        def mmxd(eng):
            last = None
            for c in range(NDI):
                last = eng.matmul(PS[2][0:96, :], xprT[:, c * 96:(c + 1) * 96],
                                  ustage[:, c * MC:(c + 1) * MC],
                                  start=(c == 0), stop=(c == NDI - 1))
            return last
        S.step("tensor", mmxd, wait=("cS", "cV"), inc="cT")

        def evxd(eng):
            eng.tensor_copy(xdbl[:], PS[2][0:96, :])
            return eng.tensor_copy(dtT_bf[:], PS[2][0:64, :])
        S.step("vector", evxd, wait=("cT", "cG"), inc="cV", pad=True)

        def stbc(eng, ch0=ch0):
            eng.dma_start(out=bm_d[:, ch0:ch0 + MC], in_=xdbl[64:80, :]
                          ).then_inc(sems["cG"], 16)
            eng.dma_start(out=cm_d[:, ch0:ch0 + MC], in_=xdbl[80:96, :]
                          ).then_inc(sems["cG"], 16)
            return None
        S.step("gpsimd", stbc, wait=("cV",), inc="cG", dma_n=2)

        for dt in range(NDI):
            def mmdt(eng, dt=dt):
                return eng.matmul(PS[3][:, :], ddtT[:, dt * 128:(dt + 1) * 128],
                                  dtT_bf[:], start=True, stop=True)
            S.step("tensor", mmdt, wait=("cV", "cS"), inc="cT")

            def spl(eng, dt=dt):
                eng.activation(out=dl_f[0][:], in_=PS[3][:, :], func=AF.Relu,
                               bias=dtbc[:, dt:dt + 1])
                eng.activation(out=dl_f[1][:], in_=PS[3][:, :], func=AF.Abs,
                               bias=dtbc[:, dt:dt + 1])
                eng.activation(out=dl_f[2][:], in_=dl_f[1][:], func=AF.Exp,
                               scale=-1.0)
                return eng.activation(out=dl_f[3][:], in_=dl_f[2][:], func=AF.Ln,
                                      bias=1.0)
            S.step("scalar", spl, wait=("cT", "cV"), inc="cS")

            def dl_v(eng, dt=dt):
                eng.tensor_add(dl_f[1][:], dl_f[0][:], dl_f[3][:])
                eng.tensor_copy(dl_b[:], dl_f[1][:])
                return eng.tensor_mul(w_b[:], dl_f[1][:],
                                      ustage[:, dt * MC:(dt + 1) * MC])
            S.step("vector", dl_v, wait=("cS", "cG"), inc="cV")

            def stdl(eng, dt=dt, ch0=ch0):
                eng.dma_start(out=dl_d[dt * 128:(dt + 1) * 128, ch0:ch0 + MC],
                              in_=dl_b[:]).then_inc(sems["cG"], 16)
                eng.dma_start(out=wsc_d[dt * 128:(dt + 1) * 128, ch0:ch0 + MC],
                              in_=w_b[:]).then_inc(sems["cG"], 16)
                return None
            S.step("gpsimd", stdl, wait=("cV",), inc="cG", dma_n=2)

    barrier()
    pop_scope()

    S.phase = "M2"
    push_scope()
    M2C = 1024
    NM2 = T // M2C
    outT = sb("outT", (128, NDI * D), BF16)
    weight_T(outw, D, DI, outT)
    BBs = sb("BBs", (128, DST * M2C), BF16)
    CBs = sb("CBs", (128, DST * M2C), BF16)
    brow2 = sb("brow2", (1, M2C))
    crow2 = sb("crow2", (1, M2C))
    dlb2 = [sb(f"dlb2{i}", (128, M2C), BF16) for i in range(2)]
    wb2 = [sb(f"wb2{i}", (128, M2C), BF16) for i in range(2)]
    ub2 = [sb("ub20", (128, M2C), BF16)] * 2
    szb2 = [sb("szb20", (128, M2C), BF16)] * 2
    a_ring = [sb(f"a_ring{i}", (128, M2C), BF16) for i in range(4)]
    b_ring = [sb(f"b_ring{i}", (128, M2C), BF16) for i in range(4)]
    h_buf = [sb("h_buf0", (128, M2C), BF16)] * 2
    p_buf = [sb("p_buf0", (128, M2C), BF16)] * 2
    yacc = [sb(f"yacc{i}", (128, M2C), BF16) for i in range(2)]
    ud_f = sb("ud_f", (128, M2C), BF16)
    y3_f = sb("y3_f", (128, M2C), BF16)
    y2st = sb("y2st", (128, NDI * M2C), BF16)
    hm_ev = sb("hm_ev", (128, M2C), BF16)

    for cc in range(NM2):
        ch0 = cc * M2C

        for n in range(DST):
            def ldrow(eng, n=n, ch0=ch0):
                eng.dma_start(out=brow2[:], in_=bm_d[n:n + 1, ch0:ch0 + M2C]
                              ).then_inc(sems["cL"], 16)
                eng.dma_start(out=crow2[:], in_=cm_d[n:n + 1, ch0:ch0 + M2C]
                              ).then_inc(sems["cL"], 16)
                return None
            S.step("sync", ldrow, wait=("cS",), inc="cL", dma_n=2)

            def bbmm(eng, n=n):
                eng.matmul(PS[0][:, :], onesf[:], brow2[:, 0:512],
                           start=True, stop=True)
                eng.matmul(PS[1][:, :], onesf[:], brow2[:, 512:1024],
                           start=True, stop=True)
                eng.matmul(PS[2][:, :], onesf[:], crow2[:, 0:512],
                           start=True, stop=True)
                return eng.matmul(PS[3][:, :], onesf[:], crow2[:, 512:1024],
                                  start=True, stop=True)
            S.step("tensor", bbmm, wait=("cL", "cS"), inc="cT")

            def bbev(eng, n=n):
                eng.activation(out=BBs[:, n * M2C:n * M2C + 512], in_=PS[0][:, :],
                               func=AF.Identity)
                eng.activation(out=BBs[:, n * M2C + 512:(n + 1) * M2C],
                               in_=PS[1][:, :], func=AF.Identity)
                eng.activation(out=CBs[:, n * M2C:n * M2C + 512], in_=PS[2][:, :],
                               func=AF.Identity)
                return eng.activation(out=CBs[:, n * M2C + 512:(n + 1) * M2C],
                                      in_=PS[3][:, :], func=AF.Identity)
            S.step("scalar", bbev, wait=("cT",), inc="cS")

        for dt in range(NDI):
            dlb, wb = dlb2[dt % 2], wb2[dt % 2]

            def lddw(eng, dt=dt, ch0=ch0, dlb=dlb, wb=wb):
                eng.dma_start(out=dlb[:], in_=dl_d[dt * 128:(dt + 1) * 128,
                                                  ch0:ch0 + M2C]
                              ).then_inc(sems["cL"], 16)
                eng.dma_start(out=wb[:], in_=wsc_d[dt * 128:(dt + 1) * 128,
                                                   ch0:ch0 + M2C]
                              ).then_inc(sems["cL"], 16)
                return None
            S.step("sync", lddw, wait=("cV", "cS"), inc="cL", dma_n=2)

            for g4 in range(4):
                def a_g(eng, dt=dt, g4=g4, dlb=dlb):
                    last = None
                    for i in range(4):
                        n = g4 * 4 + i
                        last = eng.activation(out=a_ring[i][:], in_=dlb[:],
                                              func=AF.Exp,
                                              scale=Ac[:, dt * DST + n:dt * DST + n + 1])
                    return last
                S.step("scalar", a_g, wait=("cL", "cV"), inc="cS")

                def b_g(eng, dt=dt, g4=g4, wb=wb):
                    last = None
                    for i in range(4):
                        n = g4 * 4 + i
                        last = eng.tensor_mul(b_ring[i][:], wb[:],
                                              BBs[:, n * M2C:(n + 1) * M2C])
                    return last
                S.step("gpsimd", b_g, wait=("cL", "cS", "cV"), inc="cG")

                def scan_g(eng, dt=dt, g4=g4):
                    last = None
                    for i in range(4):
                        n = g4 * 4 + i
                        col = dt * DST + n
                        eng.tensor_tensor_scan(
                            out=h_buf[i % 2][:], data0=a_ring[i][:],
                            data1=b_ring[i][:],
                            initial=scan_st[:, col:col + 1],
                            op0=ALU.mult, op1=ALU.add)
                        eng.tensor_mul(p_buf[i % 2][:], h_buf[i % 2][:],
                                       CBs[:, n * M2C:(n + 1) * M2C])
                        if n == 0:
                            eng.tensor_copy(yacc[1][:], p_buf[i % 2][:])
                        else:
                            eng.tensor_add(yacc[(n + 1) % 2][:], yacc[n % 2][:],
                                           p_buf[i % 2][:])
                        last = eng.tensor_copy(scan_st[:, col:col + 1],
                                               h_buf[i % 2][:, M2C - 1:M2C])
                    return last
                S.step("vector", scan_g, wait=("cS", "cG"), inc="cV")

            def ldusz(eng, dt=dt, ch0=ch0):
                eng.dma_start(out=ub2[dt % 2][:], in_=u_d[dt * 128:(dt + 1) * 128,
                                                         ch0:ch0 + M2C]
                              ).then_inc(sems["cL"], 16)
                eng.dma_start(out=szb2[dt % 2][:], in_=sz_d[dt * 128:(dt + 1) * 128,
                                                           ch0:ch0 + M2C]
                              ).then_inc(sems["cL"], 16)
                return None
            S.step("sync", ldusz, wait=("cV", "cS"), inc="cL", dma_n=2)

            def gate_s(eng, dt=dt):
                return eng.activation(out=ud_f[:], in_=ub2[dt % 2][:], func=AF.Identity,
                                      scale=Dc[:, dt:dt + 1])
            S.step("scalar", gate_s, wait=("cL", "cV"), inc="cS")

            def gate_v(eng, dt=dt):
                eng.tensor_add(y3_f[:], yacc[0][:], ud_f[:])
                return eng.tensor_mul(y2st[:, dt * M2C:(dt + 1) * M2C], y3_f[:],
                                      szb2[dt % 2][:])
            S.step("vector", gate_v, wait=("cS",), inc="cV")

        for m in range(ND):
            bank2 = (PS[2], PS[3]) if m % 2 == 0 else (PS[4], PS[5])

            def mmop(eng, m=m, bank2=bank2):
                last = None
                for k in range(NDI):
                    for half in range(2):
                        last = eng.matmul(
                            bank2[half][:, :],
                            outT[:, k * D + m * 128:k * D + (m + 1) * 128],
                            y2st[:, k * M2C + half * 512:k * M2C + (half + 1) * 512],
                            start=(k == 0), stop=(k == NDI - 1))
                return last
            S.step("tensor", mmop, wait=("cV", "cS", "cG"), inc="cT")

            def evop(eng, m=m, bank2=bank2):
                eng.tensor_copy(hm_ev[:, 0:512], bank2[0][:, :])
                return eng.tensor_copy(hm_ev[:, 512:1024], bank2[1][:, :])
            S.step("vector", evop, wait=("cT", "cG"), inc="cV")

            def stop_(eng, m=m, ch0=ch0):
                eng.dma_start(out=hm_d[m * 128:(m + 1) * 128, ch0:ch0 + M2C],
                              in_=hm_ev[:]).then_inc(sems["cG"], 16)
                return None
            S.step("gpsimd", stop_, wait=("cV",), inc="cG", dma_n=1)

    barrier()
    pop_scope()

    S.phase = "M3"
    push_scope()
    b2c_bc = sb("b2c_bc", (128, D))
    nw_bc = sb("nw_bc", (128, D))
    fnw_bc = sb("fnw_bc", (128, D))
    mk_bcast(b2, b2c_bc, 1.0)
    mk_bcast(nw, nw_bc, 1.0)
    mk_bcast(fnw, fnw_bc, 1.0)
    w2T_dd = dsc("w2T_dd", (DFF, D), BF16)
    w1Ts = sb("w1Ts", (128, ND * DFF), BF16)
    weight_T(w1, DFF, D, w1Ts)

    w2ev = sb("w2ev", (128, 128), BF16)
    for rt in range(ND):
        def ldw2(eng, rt=rt):
            eng.dma_start(out=wld_buf[:, 0:DFF], in_=w2[rt * 128:(rt + 1) * 128, :]
                          ).then_inc(sems["cL"], 16)
            return None
        S.step("sync", ldw2, wait=("cT",), inc="cL", dma_n=1)
        for c in range(NDF):
            def trw2(eng, c=c):
                return eng.transpose(PS7[:, 0:128],
                                     wld_buf[:, c * 128:(c + 1) * 128], identf[:])
            S.step("tensor", trw2, wait=("cL", "cG"), inc="cT")

            def evw2(eng, c=c):
                return eng.tensor_copy(w2ev[:], PS7[:, 0:128])
            S.step("vector", evw2, wait=("cT", "cG"), inc="cV")

            def stw2(eng, c=c, rt=rt):
                eng.dma_start(out=w2T_dd[c * 128:(c + 1) * 128,
                                         rt * 128:(rt + 1) * 128],
                              in_=w2ev[:]).then_inc(sems["cG"], 16)
                return None
            S.step("gpsimd", stw2, wait=("cV",), inc="cG", dma_n=1)

    M3C = 256
    NS3 = M3C // 128
    hTb = [sb(f"hTb{i}", (128, ND * M3C), BF16) for i in range(2)]
    xn1b = [sb(f"xn1b{i}", (128, D)) for i in range(2)]
    vN = sb("vN", (128, D))
    h2a = sb("h2a", (128, D))
    h2N = sb("h2N", (128, NS3 * D))
    h2bf = sb("h2bf", (128, D), BF16)
    h2T = sb("h2T", (128, ND * M3C), BF16)
    gm = [sb(f"gm{i}", (128, M3C), BF16) for i in range(2)]
    w2Tb = [sb(f"w2Tb{i}", (128, D), BF16) for i in range(2)]
    m3r = sb("m3r", (128, 8))
    ffv = [sb(f"ffv{i}", (128, D)) for i in range(3)]
    x1b = [sb(f"x1b{i}", (128, D)) for i in range(2)]
    f1b = [sb(f"f1b{i}", (128, D)) for i in range(2)]
    x2o = sb("x2o", (128, D))

    for cc in range(T // M3C):
        ch0 = cc * M3C
        hb = hTb[cc % 2]

        def ldh(eng, ch0=ch0, hb=hb):
            eng.dma_start(out=hb.rearrange("p (d j) -> p d j", j=M3C),
                          in_=hm_d.rearrange("(d p) t -> p d t", p=128)
                          [:, :, ch0:ch0 + M3C]).then_inc(sems["cL"], 16)
            return None
        S.step("sync", ldh, wait=("cT",), inc="cL", dma_n=1)

        for tsub in range(NS3):
            ti = cc * NS3 + tsub

            def ldxn1c(eng, ti=ti, tsub=tsub):
                eng.dma_start(out=xn1b[tsub % 2][:],
                              in_=xn1_d[ti * 128:(ti + 1) * 128, :]
                              ).then_inc(sems["cL"], 16)
                return None
            S.step("sync", ldxn1c, wait=("cV",), inc="cL", dma_n=1)

            for d0 in range(0, ND, 2):
                def trh(eng, d0=d0, tsub=tsub, hb=hb):
                    eng.transpose(PSB[:, 0:128],
                                  hb[:, d0 * M3C + tsub * 128:d0 * M3C + (tsub + 1) * 128],
                                  identb[:])
                    return eng.transpose(
                        PSB[:, 128:256],
                        hb[:, (d0 + 1) * M3C + tsub * 128:(d0 + 1) * M3C + (tsub + 1) * 128],
                        identb[:])
                S.step("tensor", trh, wait=("cL", "cV"), inc="cT")

                def evh(eng, d0=d0, tsub=tsub):
                    eng.tensor_add(vN[:, d0 * 128:(d0 + 1) * 128], PSB[:, 0:128],
                                   xn1b[tsub % 2][:, d0 * 128:(d0 + 1) * 128])
                    return eng.tensor_add(vN[:, (d0 + 1) * 128:(d0 + 2) * 128],
                                          PSB[:, 128:256],
                                          xn1b[tsub % 2][:, (d0 + 1) * 128:(d0 + 2) * 128])
                S.step("vector", evh, wait=("cT", "cL"), inc="cV")

            def rms1_s(eng, tsub=tsub):
                return eng.activation(out=h2a[:], in_=vN[:], func=AF.Square,
                                      accum_out=m3r[:, tsub:tsub + 1])
            S.step("scalar", rms1_s, wait=("cV",), inc="cS", pad=True)

            def rms1_v2(eng, tsub=tsub):
                return eng.tensor_scalar(out=m3r[:, tsub + 4:tsub + 5],
                                         in0=m3r[:, tsub:tsub + 1],
                                         scalar1=1.0 / D, scalar2=EPS_RMS,
                                         op0=ALU.mult, op1=ALU.add)
            S.step("vector", rms1_v2, wait=("cS",), inc="cV", pad=True)

            def rms1_sq(eng, tsub=tsub):
                return eng.activation(out=m3r[:, tsub:tsub + 1],
                                      in_=m3r[:, tsub + 4:tsub + 5], func=AF.Sqrt)
            S.step("scalar", rms1_sq, wait=("cV",), inc="cS", pad=True)

            def rms1_r(eng, tsub=tsub):
                [eng.engine_nop() for _ in range(8)]
                return eng.reciprocal(m3r[:, tsub + 4:tsub + 5],
                                      m3r[:, tsub:tsub + 1])
            S.step("vector", rms1_r, wait=("cS",), inc="cV", pad=True)

            def rms1_f(eng, tsub=tsub):
                return eng.activation(out=h2a[:], in_=vN[:], func=AF.Identity,
                                      scale=m3r[:, tsub + 4:tsub + 5])
            S.step("scalar", rms1_f, wait=("cV",), inc="cS")

            def rms1_v(eng, tsub=tsub):
                eng.tensor_mul(h2N[:, tsub * D:(tsub + 1) * D], h2a[:], nw_bc[:])
                return eng.tensor_copy(h2bf[:], h2N[:, tsub * D:(tsub + 1) * D])
            S.step("vector", rms1_v, wait=("cS",), inc="cV")

            for d0 in range(0, ND, 2):
                def trh2(eng, d0=d0):
                    eng.transpose(PSB[:, 0:128], h2bf[:, d0 * 128:(d0 + 1) * 128],
                                  identb[:])
                    return eng.transpose(PSB[:, 128:256],
                                         h2bf[:, (d0 + 1) * 128:(d0 + 2) * 128],
                                         identb[:])
                S.step("tensor", trh2, wait=("cV",), inc="cT")

                def evh2(eng, d0=d0, tsub=tsub):
                    eng.tensor_copy(
                        h2T[:, d0 * M3C + tsub * 128:d0 * M3C + (tsub + 1) * 128],
                        PSB[:, 0:128])
                    return eng.tensor_copy(
                        h2T[:, (d0 + 1) * M3C + tsub * 128:(d0 + 1) * M3C + (tsub + 1) * 128],
                        PSB[:, 128:256])
                S.step("vector", evh2, wait=("cT",), inc="cV")

        for m in range(NDF):
            gb = gm[m % 2]

            def ldw2b(eng, m=m):
                eng.dma_start(out=w2Tb[m % 2][:], in_=w2T_dd[m * 128:(m + 1) * 128, :]
                              ).then_inc(sems["cL"], 16)
                return None
            S.step("sync", ldw2b, wait=("cT",), inc="cL", dma_n=1)

            def mmg(eng, m=m):
                last = None
                for k in range(ND):
                    last = eng.matmul(PS[4 + m % 2][:, 0:M3C],
                                      w1Ts[:, k * DFF + m * 128:k * DFF + (m + 1) * 128],
                                      h2T[:, k * M3C:(k + 1) * M3C],
                                      start=(k == 0), stop=(k == ND - 1))
                return last
            S.step("tensor", mmg, wait=("cV", "cS"), inc="cT")

            def evg(eng, m=m, gb=gb):
                return eng.activation(out=gb[:], in_=PS[4 + m % 2][:, 0:M3C],
                                      func=AF.Gelu, bias=b1c[:, m:m + 1])
            S.step("scalar", evg, wait=("cT",), inc="cS")

            def mmff(eng, m=m, gb=gb):
                last = None
                for tsub in range(NS3):
                    for half in range(2):
                        last = eng.matmul(
                            PS[tsub * 2 + half][:, :],
                            gb[:, tsub * 128:(tsub + 1) * 128],
                            w2Tb[m % 2][:, half * 512:(half + 1) * 512],
                            start=(m == 0), stop=(m == NDF - 1))
                return last
            S.step("tensor", mmff,
                   wait=("cS", "cL", "cV") if m < 2 else ("cS", "cL"), inc="cT")

        for tsub in range(NS3):
            ti = cc * NS3 + tsub

            def ldx1f(eng, ti=ti, tsub=tsub):
                eng.dma_start(out=x1b[tsub % 2][:],
                              in_=x1_d[ti * 128:(ti + 1) * 128, :]
                              ).then_inc(sems["cL"], 16)
                eng.dma_start(out=f1b[tsub % 2][:],
                              in_=ffn1_d[ti * 128:(ti + 1) * 128, :]
                              ).then_inc(sems["cL"], 16)
                return None
            S.step("sync", ldx1f, wait=("cG",), inc="cL", dma_n=2)

            def ffev(eng, tsub=tsub):
                eng.tensor_add(ffv[0][:, 0:512], PS[tsub * 2][:, :], b2c_bc[:, 0:512])
                eng.tensor_add(ffv[0][:, 512:1024], PS[tsub * 2 + 1][:, :],
                               b2c_bc[:, 512:1024])
                return eng.tensor_add(ffv[1][:], ffv[0][:],
                                      h2N[:, tsub * D:(tsub + 1) * D])
            S.step("vector", ffev, wait=("cT",), inc="cV")

            def rms2_s(eng, tsub=tsub):
                return eng.activation(out=ffv[2][:], in_=ffv[1][:], func=AF.Square,
                                      accum_out=m3r[:, tsub:tsub + 1])
            S.step("scalar", rms2_s, wait=("cV", "cG"), inc="cS", pad=True)

            def rms2_v2(eng, tsub=tsub):
                return eng.tensor_scalar(out=m3r[:, tsub + 4:tsub + 5],
                                         in0=m3r[:, tsub:tsub + 1],
                                         scalar1=1.0 / D, scalar2=EPS_RMS,
                                         op0=ALU.mult, op1=ALU.add)
            S.step("vector", rms2_v2, wait=("cS",), inc="cV", pad=True)

            def rms2_sq(eng, tsub=tsub):
                return eng.activation(out=m3r[:, tsub:tsub + 1],
                                      in_=m3r[:, tsub + 4:tsub + 5], func=AF.Sqrt)
            S.step("scalar", rms2_sq, wait=("cV",), inc="cS", pad=True)

            def rms2_r(eng, tsub=tsub):
                [eng.engine_nop() for _ in range(8)]
                return eng.reciprocal(m3r[:, tsub + 4:tsub + 5],
                                      m3r[:, tsub:tsub + 1])
            S.step("vector", rms2_r, wait=("cS",), inc="cV", pad=True)

            def rms2_f(eng, tsub=tsub):
                return eng.activation(out=ffv[2][:], in_=ffv[1][:], func=AF.Identity,
                                      scale=m3r[:, tsub + 4:tsub + 5])
            S.step("scalar", rms2_f, wait=("cV",), inc="cS")

            def fin_v(eng, tsub=tsub):
                eng.tensor_mul(ffv[0][:], ffv[2][:], fnw_bc[:])
                eng.tensor_add(x2o[:], x1b[tsub % 2][:], ffv[0][:])
                return eng.tensor_add(ffv[2][:], x2o[:], f1b[tsub % 2][:])
            S.step("vector", fin_v, wait=("cS", "cL", "cG"), inc="cV")

            def st_out(eng, ti=ti):
                eng.dma_start(out=out[ti * 128:(ti + 1) * 128, :], in_=ffv[2][:]
                              ).then_inc(sems["cG"], 16)
                return None
            S.step("gpsimd", st_out, wait=("cV",), inc="cG", dma_n=1)

    barrier()
    pop_scope()

    S.phase = "DBG"
    scratch_map = dict(xn_dbg=xn_dbg, stats_dbg=stats_dbg, qpx_d=qpx_d, qpy_d=qpy_d,
                       qT_d=qT_d, kT_d=kT_d, V_d=V_d, ffn0_d=ffn0_d, x1_d=x1_d,
                       xn1_d=xn1_d, xn1T_d=xn1T_d, xdT_d=xdT_d, u_d=u_d,
                       dl_d=dl_d, wsc_d=wsc_d, sz_d=sz_d, hm_d=hm_d,
                       ffn1_d=ffn1_d)
    for name, shape, dt in dbg:
        src = scratch_map[name]

        def dmp(eng, src=src, dst=dbg_out[name]):
            eng.dma_start(out=dst, in_=src).then_inc(sems["cG"], 16)
            return None
        S.step("gpsimd", dmp, inc="cG", dma_n=1)
    if dbg:
        barrier()

    with nc.allow_non_contiguous_dma(reason="small one-time bias/const gathers"), \
            nc.Block() as block:
        @block.sync
        def _(eng):
            S.emit(eng, "sync", sems)

        @block.gpsimd
        def _(eng):
            S.emit(eng, "gpsimd", sems)

        @block.vector
        def _(eng):
            S.emit(eng, "vector", sems)

        @block.scalar
        def _(eng):
            S.emit(eng, "scalar", sems)

        @block.tensor
        def _(eng):
            S.emit(eng, "tensor", sems)

    ctx.close()
    return nc


def _shard_inputs(inputs):
    hc = _host_consts()
    x = np.ascontiguousarray(np.asarray(inputs["L0_attn_hidden_states"],
                                        dtype=np.float32))
    common = {}
    for k, v in inputs.items():
        if k == "L0_attn_hidden_states":
            continue
        a = np.ascontiguousarray(np.asarray(v, dtype=np.float32))
        if a.ndim == 1:
            a = a[None, :]
        common[k] = a
    common.update(hc)
    return [dict(common, x_in=np.ascontiguousarray(x[b])) for b in range(NCORES)]


def kernel(**inputs):
    maps = _shard_inputs(inputs)
    nc = build_nc()
    res = run_bass_kernel_spmd(nc, maps, core_ids=list(range(NCORES)))
    return np.stack([np.asarray(r["out"]) for r in res.results], axis=0)


# version 08
# speedup vs baseline: 1.1069x; 1.1069x over previous
import numpy as np
import ml_dtypes
from contextlib import ExitStack

import concourse.bass as bass
import concourse.mybir as mybir
from concourse.bass_utils import run_bass_kernel_spmd

F32 = mybir.dt.float32
BF16 = mybir.dt.bfloat16
AF = mybir.ActivationFunctionType
ALU = mybir.AluOpType
AX = mybir.AxisListType

T = 4096
D = 1024
DI = 2048
DST = 16
DTR = 64
DFF = 4096
NT = T // 128
ND = D // 128
NDI = DI // 128
NDF = DFF // 128
IT = 512
JT = 512
MC = 512
NMC = T // MC
EPS_LN = 1e-5
EPS_RMS = 1e-12
NCORES = 8
CVALS = [EPS_LN, EPS_RMS, 1.0 / D, 1.0 / 32.0, 1.0, -1.0, 0.0, 2.0**-9, 7.5]


class Sched:
    ENGINES = ("sync", "gpsimd", "vector", "scalar", "tensor")

    def __init__(self):
        self.steps = {e: [] for e in self.ENGINES}
        self.counts = {}

    pad_emitters = {}

    def step(self, eng, fn, wait=(), inc=None, inc_n=1, dma_n=0, pad=False,
             wait_at=None):
        waits = tuple((c, (wait_at or {}).get(c, self.counts.get(c, 0)))
                      for c in wait
                      if (wait_at or {}).get(c, self.counts.get(c, 0)) > 0)
        if inc is not None and dma_n:
            self.counts[inc] = self.counts.get(inc, 0) + 16 * dma_n
            self.steps[eng].append((fn, waits, None, 0, False))
        elif inc is not None:
            self.counts[inc] = self.counts.get(inc, 0) + inc_n
            self.steps[eng].append((fn, waits, inc, inc_n, pad))
        else:
            self.steps[eng].append((fn, waits, None, 0, False))

    def emit(self, eng_obj, eng_name, sems):
        last_wait = {}
        for fn, waits, inc, inc_n, pad in self.steps[eng_name]:
            for chn, val in waits:
                if last_wait.get(chn, -1) < val:
                    eng_obj.wait_ge(sems[chn], val)
                    last_wait[chn] = val
            instr = fn(eng_obj)
            if pad and eng_name in self.pad_emitters:
                instr = self.pad_emitters[eng_name](eng_obj)
            if inc is not None:
                assert instr is not None, f"step on {eng_name} returned None"
                instr.then_inc(sems[inc], inc_n)


def _host_consts():
    identf = np.eye(128, dtype=np.float32)
    identb = np.eye(128).astype(ml_dtypes.bfloat16)
    onesf = np.ones((1, 128), dtype=np.float32)
    tril = np.zeros((4, 128, JT), dtype=np.float32)
    jj = np.arange(JT)[None, :]
    pp = np.arange(128)[:, None]
    for k in range(4):
        tril[k] = (jj <= 128 * k + pp).astype(np.float32)
    probe = np.array([[2.0**-30, 1.0, 2.0**30, 2.0**49, 2.0**20,
                       1.5 * 2.0**52, 1.5 * 2.0**23,
                       float(np.sqrt(2.0) * 2.0**-27), float(2.0**-12),
                       0.0, float(T * T), float(T * T - 1), 1.25,
                       0.0, 0.0, 0.0]], dtype=np.float32)
    cvals = np.broadcast_to(np.array(CVALS, dtype=np.float32)[None, :],
                            (128, len(CVALS))).copy()
    return dict(identf_h=identf, identb_h=identb, onesf_h=onesf,
                tril_h=np.ascontiguousarray(tril.reshape(4 * 128, JT)),
                probe_h=probe, cvals_h=cvals)


def build_nc(dbg=()):
    nc = bass.Bass("TRN2", target_bir_lowering=False,
                   detect_race_conditions=False)
    S = Sched()
    ctx = ExitStack()

    def din(name, shape, dt=F32):
        return nc.dram_tensor(name, list(shape), dt, kind="ExternalInput").ap()

    x_in = din("x_in", (T, D))
    Wq = din("L0_attn_Wq", (D, D)); bq = din("L0_attn_bq", (1, D))
    Wk = din("L0_attn_Wk", (D, D)); bk = din("L0_attn_bk", (1, D))
    Wv = din("L0_attn_Wv", (D, D)); bv = din("L0_attn_bv", (1, D))
    Wo = din("L0_attn_Wo", (D, D)); bo = din("L0_attn_bo", (1, D))
    Wtq = din("L0_attn_Wtq", (2, D)); btq = din("L0_attn_btq", (1, 2))
    Wfq = din("L0_attn_Wfq", (D, 2)); bfq = din("L0_attn_bfq", (1, D))
    w8 = din("L0_ffn_weight_fp8", (D, D)); b8 = din("L0_ffn_bias", (1, D))
    s8in = din("L0_ffn_input_scale", (1, 1)); s8w = din("L0_ffn_weight_scale", (1, 1))
    inpw = din("L1_attn_in_proj_w", (2 * DI, D))
    convw = din("L1_attn_conv_w", (DI, 4)); convb = din("L1_attn_conv_b", (1, DI))
    xprw = din("L1_attn_x_proj_w", (DTR + 2 * DST, DI))
    dtw = din("L1_attn_dt_proj_w", (DI, DTR)); dtb = din("L1_attn_dt_proj_b", (1, DI))
    alog = din("L1_attn_A_log", (DI, DST)); dD = din("L1_attn_D", (1, DI))
    outw = din("L1_attn_out_proj_w", (D, DI))
    nw = din("L1_attn_norm_w", (1, D))
    w1 = din("L1_attn_ffn_w1", (DFF, D)); b1 = din("L1_attn_ffn_b1", (1, DFF))
    w2 = din("L1_attn_ffn_w2", (D, DFF)); b2 = din("L1_attn_ffn_b2", (1, D))
    fnw = din("L1_attn_ffn_norm_w", (1, D))
    wL1 = din("L1_ffn_weight", (D, D)); bL1 = din("L1_ffn_bias", (1, D))
    identf_h = din("identf_h", (128, 128))
    identb_h = din("identb_h", (128, 128), BF16)
    cvals_h = din("cvals_h", (128, len(CVALS)))
    onesf_h = din("onesf_h", (1, 128))
    tril_h = din("tril_h", (4 * 128, JT))
    probe_h = din("probe_h", (1, 16))

    out = nc.dram_tensor("out", [T, D], F32, kind="ExternalOutput").ap()
    dbg_out = {}
    for name, shape, dt in dbg:
        dbg_out[name] = nc.dram_tensor("dbg_" + name, list(shape), dt,
                                       kind="ExternalOutput").ap()

    def dsc(name, shape, dt):
        return nc.dram_tensor(name, list(shape), dt).ap()

    qT_d = dsc("qT_d", (D, T), BF16)
    kT_d = dsc("kT_d", (D, T), BF16)
    V_d = dsc("V_d", (T, D), BF16)
    ffn0_d = dsc("ffn0_d", (T, D), F32)
    x1_d = dsc("x1_d", (T, D), F32)
    xn1_d = dsc("xn1_d", (T, D), F32)
    xn1T_d = dsc("xn1T_d", (D, T), BF16)
    xdT_d = dsc("xdT_d", (D, T), BF16)
    u_d = dsc("u_d", (DI, T), BF16)
    dl_d = dsc("dl_d", (DI, T), BF16)
    wsc_d = dsc("wsc_d", (DI, T), BF16)
    sz_d = dsc("sz_d", (DI, T), BF16)
    hm_d = dsc("hm_d", (D, T), BF16)
    ffn1_d = dsc("ffn1_d", (T, D), F32)
    xn_dbg = dsc("xn_dbg", (T, D), F32)
    stats_dbg = dsc("stats_dbg", (128, 96), F32)
    qpx_d = dsc("qpx_d", (1, T), F32)
    qpy_d = dsc("qpy_d", (1, T), F32)
    bm_d = dsc("bm_d", (DST, T), F32)
    cm_d = dsc("cm_d", (DST, T), F32)
    pT_d = dsc("pT_d", (T, IT), BF16)

    scopes = [ctx]

    def sb(name, shape, dt=F32):
        return scopes[-1].enter_context(nc.sbuf_tensor(name, list(shape), dt))

    def push_scope():
        scopes.append(ExitStack())

    def pop_scope():
        scopes.pop().close()

    def barrier():
        for e in Sched.ENGINES:
            S.step(e, lambda eng: None, wait=("cL", "cG", "cT", "cV", "cS"))

    def psum(name, shape, dt=F32):
        return ctx.enter_context(nc.psum_tensor(name, list(shape), dt))

    sems = {}
    for chn in ("cL", "cG", "cT", "cV", "cS"):
        sems[chn] = ctx.enter_context(nc.semaphore(chn))

    identf = sb("identf", (128, 128))
    identb = sb("identb", (128, 128), BF16)
    onesf = sb("onesf", (1, 128))
    probe_sb = sb("probe_sb", (1, 16))
    CST = sb("CST", (128, 16))
    W_ = sb("W_", (1, 32))
    stat_r = sb("stat_r", (128, NT))
    stat_mb = sb("stat_mb", (128, NT))
    stat_tmp = sb("stat_tmp", (128, NT))
    stat_nm = sb("stat_nm", (128, NT))
    stat_ve = sb("stat_ve", (128, NT))
    stat_s = sb("stat_s", (128, NT))
    qpN = sb("qpN", (128, NT * 2))
    rembN = sb("rembN", (128, NT * 2))
    bqc = sb("bqc", (128, ND)); bkc = sb("bkc", (128, ND))
    btq_sb = sb("btq_sb", (2, 1))
    b1c = sb("b1c", (128, NDF))
    dtbc = sb("dtbc", (128, NDI))
    convbc = sb("convbc", (128, NDI))
    convwc = sb("convwc", (128, NDI * 4))
    Dc = sb("Dc", (128, NDI))
    Ac = sb("Ac", (128, NDI * DST))
    scan_st = sb("scan_st", (128, NDI * DST))
    wld_buf = sb("wld_buf", (128, 4096))
    brow = sb("brow", (1, D))
    cvals_sb = sb("cvals_sb", (128, len(CVALS)))
    vdump = sb("vdump", (128, 128))
    sdump = sb("sdump", (128, 128))
    S.pad_emitters = {
        "vector": lambda eng: eng.tensor_scalar_add(vdump[:], identf[:], 0.0),
        "scalar": lambda eng: eng.activation(out=sdump[:], in_=identf[:],
                                             func=AF.Identity),
    }

    def vgap(eng):
        return eng.tensor_scalar_add(vdump[:], identf[:], 0.0)
    for _i, _v in enumerate(CVALS):
        nc.const_aps.register(F32, float(_v), cvals_sb[:, _i:_i + 1])
    PS = [psum(f"ps{i}", (128, 512)) for i in range(6)]
    PSB = psum("psb", (128, 1024), BF16)
    PS7 = psum("ps7", (128, 512))

    def mk_bcast(src_ap, dst_sb, scl):
        def ldr(eng, src=src_ap):
            eng.dma_start(out=brow[:], in_=src).then_inc(sems["cL"], 16)
            return None
        S.step("sync", ldr, wait=("cT",), inc="cL", dma_n=1)

        def bc_t(eng):
            eng.matmul(PS[1][:, :], onesf[:], brow[:, 0:512], start=True, stop=True)
            return eng.matmul(PS[2][:, :], onesf[:], brow[:, 512:1024],
                              start=True, stop=True)
        S.step("tensor", bc_t, wait=("cL", "cV"), inc="cT")

        def bc_v(eng, dst=dst_sb, scl=scl):
            eng.tensor_scalar_mul(dst[:, 0:512], PS[1][:, :], scl)
            return eng.tensor_scalar_mul(dst[:, 512:1024], PS[2][:, :], scl)
        S.step("vector", bc_v, wait=("cT",), inc="cV")

    def ldc(eng):
        eng.dma_start(out=identf[:], in_=identf_h).then_inc(sems["cL"], 16)
        eng.dma_start(out=identb[:], in_=identb_h).then_inc(sems["cL"], 16)
        eng.dma_start(out=onesf[:], in_=onesf_h).then_inc(sems["cL"], 16)
        eng.dma_start(out=probe_sb[:], in_=probe_h).then_inc(sems["cL"], 16)
        eng.dma_start(out=cvals_sb[:], in_=cvals_h).then_inc(sems["cL"], 16)
        eng.dma_start(out=btq_sb[:], in_=btq.rearrange("o t -> t o")
                      ).then_inc(sems["cL"], 16)
        eng.dma_start(out=W_[:, 20:21], in_=s8in).then_inc(sems["cL"], 16)
        eng.dma_start(out=W_[:, 21:22], in_=s8w).then_inc(sems["cL"], 16)
        eng.dma_start(out=bqc[:], in_=bq.rearrange("o (j p) -> p (o j)", p=128)
                      ).then_inc(sems["cL"], 16)
        eng.dma_start(out=bkc[:], in_=bk.rearrange("o (j p) -> p (o j)", p=128)
                      ).then_inc(sems["cL"], 16)
        eng.dma_start(out=b1c[:], in_=b1.rearrange("o (j p) -> p (o j)", p=128)
                      ).then_inc(sems["cL"], 16)
        eng.dma_start(out=dtbc[:], in_=dtb.rearrange("o (j p) -> p (o j)", p=128)
                      ).then_inc(sems["cL"], 16)
        eng.dma_start(out=convbc[:], in_=convb.rearrange("o (j p) -> p (o j)", p=128)
                      ).then_inc(sems["cL"], 16)
        eng.dma_start(out=Dc[:], in_=dD.rearrange("o (j p) -> p (o j)", p=128)
                      ).then_inc(sems["cL"], 16)
        eng.dma_start(out=convwc.rearrange("p (j k) -> p j k", k=4),
                      in_=convw.rearrange("(j p) k -> p j k", p=128)
                      ).then_inc(sems["cL"], 16)
        eng.dma_start(out=Ac.rearrange("p (j n) -> p j n", n=DST),
                      in_=alog.rearrange("(j p) n -> p j n", p=128)
                      ).then_inc(sems["cL"], 16)
        return None
    S.step("sync", ldc, inc="cL", dma_n=16)

    def consts_v(eng):
        pc = probe_sb
        nop8 = lambda: vgap(eng)
        eng.tensor_add(W_[:, 0:1], pc[:, 0:1], pc[:, 1:2]); nop8()
        eng.tensor_sub(W_[:, 1:2], W_[:, 0:1], pc[:, 1:2]); nop8()
        eng.tensor_mul(W_[:, 2:3], W_[:, 1:2], pc[:, 2:3]); nop8()
        eng.tensor_sub(W_[:, 3:4], pc[:, 1:2], W_[:, 2:3]); nop8()
        eng.tensor_mul(W_[:, 4:5], W_[:, 2:3], pc[:, 3:4])
        eng.tensor_mul(W_[:, 5:6], W_[:, 3:4], pc[:, 4:5]); nop8()
        eng.tensor_add(W_[:, 6:7], W_[:, 4:5], W_[:, 5:6]); nop8()
        eng.tensor_add(W_[:, 7:8], W_[:, 6:7], pc[:, 1:2])
        eng.tensor_mul(W_[:, 8:9], W_[:, 2:3], pc[:, 5:6])
        eng.tensor_mul(W_[:, 9:10], W_[:, 3:4], pc[:, 6:7]); nop8()
        eng.tensor_add(W_[:, 10:11], W_[:, 8:9], W_[:, 9:10]); nop8()
        eng.tensor_sub(W_[:, 11:12], pc[:, 9:10], W_[:, 10:11])
        eng.tensor_mul(W_[:, 12:13], W_[:, 2:3], pc[:, 7:8])
        eng.tensor_mul(W_[:, 13:14], W_[:, 3:4], pc[:, 8:9]); nop8()
        eng.tensor_add(W_[:, 14:15], W_[:, 12:13], W_[:, 13:14]); nop8()
        eng.reciprocal(W_[:, 22:23], W_[:, 20:21]); nop8()
        return eng.tensor_mul(W_[:, 23:24], W_[:, 20:21], W_[:, 21:22])
    S.step("vector", consts_v, wait=("cL",), inc="cV", pad=True)

    cst_srcs = [(W_[:, 7:8], 0), (W_[:, 10:11], 1), (W_[:, 11:12], 2),
                (W_[:, 14:15], 3), (W_[:, 22:23], 4), (W_[:, 23:24], 5)]

    def cst_bc_t(eng):
        last = None
        for i, (src, col) in enumerate(cst_srcs):
            last = eng.matmul(PS[0][:, i:i + 1], onesf[:], src,
                              start=True, stop=True)
        return last
    S.step("tensor", cst_bc_t, wait=("cV",), inc="cT")

    def cst_bc_v(eng):
        last = None
        for i, (src, col) in enumerate(cst_srcs):
            last = eng.tensor_copy(CST[:, col:col + 1], PS[0][:, i:i + 1])
        return last
    S.step("vector", cst_bc_v, wait=("cT",), inc="cV", pad=True)

    def r75_v(eng):
        return eng.reciprocal(CST[:, 7:8], cvals_sb[:, 8:9])
    S.step("vector", r75_v, wait=("cL",), inc="cV", pad=True)

    atmp = sb("atmp", (128, NDI * DST))

    def a_exp(eng):
        return eng.activation(out=atmp[:], in_=Ac[:], func=AF.Exp)
    S.step("scalar", a_exp, wait=("cL",), inc="cS")

    def a_neg(eng):
        return eng.tensor_scalar_mul(Ac[:], atmp[:], -1.0)
    S.step("vector", a_neg, wait=("cS",), inc="cV")

    def st_zero(eng):
        return eng.memset(scan_st[:], 0.0)
    S.step("vector", st_zero, inc="cV")

    def weight_T(wap, R, C, dst_sb):
        nr, ncl = R // 128, C // 128
        for rt in range(nr):
            def ldw(eng, rt=rt, wap=wap, C=C):
                eng.dma_start(out=wld_buf[:, 0:C],
                              in_=wap[rt * 128:(rt + 1) * 128, :]
                              ).then_inc(sems["cL"], 16)
                return None
            S.step("sync", ldw, wait=("cV",), inc="cL", dma_n=1)
            for cg in range(0, ncl, 4):
                ng = min(4, ncl - cg)

                def trn_g(eng, cg=cg, ng=ng):
                    last = None
                    for i in range(ng):
                        last = eng.transpose(PS7[:, i * 128:(i + 1) * 128],
                                             wld_buf[:, (cg + i) * 128:(cg + i + 1) * 128],
                                             identf[:])
                    return last
                S.step("tensor", trn_g, wait=("cL", "cV"), inc="cT")

                def ev_g(eng, cg=cg, ng=ng, rt=rt, dst_sb=dst_sb, R=R):
                    last = None
                    for i in range(ng):
                        c = cg + i
                        last = eng.tensor_copy(
                            dst_sb[:, c * R + rt * 128:c * R + (rt + 1) * 128],
                            PS7[:, i * 128:(i + 1) * 128])
                    return last
                S.step("vector", ev_g, wait=("cT",), inc="cV")

    push_scope()
    bvc_bc = sb("bvc_bc", (128, D))
    b8c_bc = sb("b8c_bc", (128, D))
    mk_bcast(bv, bvc_bc, 1.0)
    mk_bcast(b8, b8c_bc, 1.0)
    WqTs = sb("WqTs", (128, D * ND), BF16)
    WkTs = sb("WkTs", (128, D * ND), BF16)
    WvTs = sb("WvTs", (128, D * ND), BF16)
    w8Ts = sb("w8Ts", (128, D * ND), BF16)
    WtqT = sb("WtqT", (128, ND * 2))

    weight_T(Wq, D, D, WqTs)
    weight_T(Wk, D, D, WkTs)
    weight_T(Wv, D, D, WvTs)
    weight_T(w8, D, D, w8Ts)

    def ldwtq(eng):
        for c in range(2):
            eng.dma_start(out=WtqT.rearrange("p (j c) -> p j c", c=2)[:, :, c],
                          in_=Wtq.rearrange("c (j p) -> c p j", p=128)[c]
                          ).then_inc(sems["cL"], 16)
        return None
    S.step("sync", ldwtq, inc="cL", dma_n=2)

    x_buf = [sb(f"x_buf{i}", (128, D)) for i in range(2)]
    ln_sc = [sb(f"ln_sc{i}", (128, D)) for i in range(3)]
    q8_sc = [sb(f"q8_sc{i}", (128, D)) for i in range(5)]
    aug = sb("aug", (128, 4))
    aug_sq = sb("aug_sq", (128, 2))

    def aug_ones(eng):
        return eng.memset(aug[:, 3:4], 1.0)
    S.step("vector", aug_ones, inc="cV")
    qp2 = sb("qp2", (2, 512))
    xnT_c = sb("xnT_c", (128, ND * 512), BF16)
    xnTf_c = sb("xnTf_c", (128, ND * 512))
    xdqT_c = sb("xdqT_c", (128, ND * 512), BF16)
    proj_ev = [sb(f"proj_ev{i}", (128, 512), BF16) for i in range(4)]
    vfe_ev = [sb(f"vfe_ev{i}", (128, D)) for i in range(2)]
    vfe_bf = [sb(f"vfe_bf{i}", (128, D), BF16) for i in range(2)]

    def p1_tile(tt):
        xb = x_buf[tt % 2]

        def ldx(eng, tt=tt, xb=xb):
            eng.dma_start(out=xb[:], in_=x_in[tt * 128:(tt + 1) * 128, :]
                          ).then_inc(sems["cL"], 16)
            return None
        S.step("sync", ldx, wait=("cS",), inc="cL", dma_n=1)

        def red(eng, tt=tt, xb=xb):
            eng.reduce_sum(stat_tmp[:, tt:tt + 1], xb[:], axis=AX.X,
                           negate=True)
            vgap(eng)
            return eng.tensor_scalar_mul(stat_nm[:, tt:tt + 1],
                                         stat_tmp[:, tt:tt + 1], 1.0 / D)
        S.step("vector", red, wait=("cL",), inc="cV", pad=True)

        def ln_a(eng, tt=tt, xb=xb):
            eng.activation(out=ln_sc[0][:], in_=xb[:], func=AF.Identity,
                           bias=stat_nm[:, tt:tt + 1])
            return eng.activation(out=ln_sc[1][:], in_=ln_sc[0][:],
                                  func=AF.Square,
                                  accum_out=stat_tmp[:, tt:tt + 1])
        S.step("scalar", ln_a, wait=("cV",), inc="cS", pad=True)

        def ln_v2(eng, tt=tt):
            return eng.tensor_scalar(out=stat_ve[:, tt:tt + 1],
                                     in0=stat_tmp[:, tt:tt + 1],
                                     scalar1=1.0 / D, scalar2=EPS_LN,
                                     op0=ALU.mult, op1=ALU.add)
        S.step("vector", ln_v2, wait=("cS",), inc="cV", pad=True)

        def ln_s2(eng, tt=tt):
            return eng.activation(out=stat_s[:, tt:tt + 1],
                                  in_=stat_ve[:, tt:tt + 1], func=AF.Sqrt)
        S.step("scalar", ln_s2, wait=("cV",), inc="cS", pad=True)

        def ln_v3(eng, tt=tt):
            eng.reciprocal(stat_r[:, tt:tt + 1], stat_s[:, tt:tt + 1])
            vgap(eng)
            eng.tensor_mul(stat_ve[:, tt:tt + 1], stat_nm[:, tt:tt + 1],
                           stat_r[:, tt:tt + 1])
            vgap(eng)
            return eng.tensor_scalar_mul(stat_mb[:, tt:tt + 1],
                                         stat_ve[:, tt:tt + 1], 1.0)
        S.step("vector", ln_v3, wait=("cS",), inc="cV", pad=True)

        def ln_b(eng, tt=tt, xb=xb):
            return eng.activation(out=ln_sc[2][:], in_=xb[:], func=AF.Identity,
                                  scale=stat_r[:, tt:tt + 1],
                                  bias=stat_mb[:, tt:tt + 1])
        S.step("scalar", ln_b, wait=("cV", "cG"), inc="cS")

        xn = ln_sc[2]

        if dbg:
            def st_xn_dbg(eng, tt=tt, xn=xn):
                eng.dma_start(out=xn_dbg[tt * 128:(tt + 1) * 128, :], in_=xn[:]
                              ).then_inc(sems["cG"], 16)
                return None
            S.step("gpsimd", st_xn_dbg, wait=("cS",), inc="cG", dma_n=1)

        def q8(eng, xn=xn):
            Cv, M0 = CST[:, 0:1], CST[:, 1:2]
            rsin = CST[:, 4:5]
            w1_, w2_, w3_, w4_, w5_ = q8_sc
            eng.tensor_scalar(out=w1_[:], in0=xn[:], scalar1=rsin, scalar2=448.0,
                              op0=ALU.mult, op1=ALU.min)
            eng.tensor_scalar(out=w5_[:], in0=w1_[:], scalar1=-448.0, scalar2=None,
                              op0=ALU.max)
            eng.tensor_scalar(out=w2_[:], in0=w5_[:], scalar1=Cv, scalar2=None,
                              op0=ALU.mult)
            eng.tensor_sub(w3_[:], w2_[:], w5_[:])
            eng.tensor_sub(w4_[:], w2_[:], w3_[:])
            eng.tensor_scalar(out=w2_[:], in0=w5_[:], scalar1=512.0, scalar2=None,
                              op0=ALU.mult)
            eng.tensor_scalar(out=w3_[:], in0=w2_[:], scalar1=M0, scalar2=None,
                              op0=ALU.add)
            eng.tensor_scalar(out=w2_[:], in0=w3_[:], scalar1=M0,
                              scalar2=float(2.0**-9),
                              op0=ALU.subtract, op1=ALU.mult)
            eng.tensor_scalar(out=w1_[:], in0=w5_[:], scalar1=-1.0, scalar2=None,
                              op0=ALU.mult)
            eng.tensor_max(w3_[:], w1_[:], w5_[:])
            eng.tensor_single_scalar(out=w1_[:], in_=w3_[:], scalar=float(2.0**-6),
                                     op=ALU.is_lt)
            eng.tensor_sub(w3_[:], w2_[:], w4_[:])
            eng.tensor_mul(w2_[:], w3_[:], w1_[:])
            return eng.tensor_add(w5_[:], w2_[:], w4_[:])
        S.step("vector", q8, wait=("cS",), inc="cV")
        q8v = q8_sc[4]

        ci = tt % 4
        for d0 in range(0, ND, 2):
            def tr2(eng, d0=d0, xn=xn):
                eng.transpose(PS[4][:, 0:128], xn[:, d0 * 128:(d0 + 1) * 128],
                              identf[:])
                return eng.transpose(PS[5][:, 0:128],
                                     xn[:, (d0 + 1) * 128:(d0 + 2) * 128],
                                     identf[:])
            S.step("tensor", tr2, wait=("cV", "cS"), inc="cT")

            def ev2(eng, d0=d0, ci=ci):
                c0 = d0 * 512 + ci * 128
                c1 = (d0 + 1) * 512 + ci * 128
                eng.tensor_copy(xnT_c[:, c0:c0 + 128], PS[4][:, 0:128])
                eng.tensor_copy(xnTf_c[:, c0:c0 + 128], PS[4][:, 0:128])
                eng.tensor_copy(xnT_c[:, c1:c1 + 128], PS[5][:, 0:128])
                return eng.tensor_copy(xnTf_c[:, c1:c1 + 128], PS[5][:, 0:128])
            S.step("vector", ev2, wait=("cT",), inc="cV")

        for d0 in range(0, ND, 2):
            def tr2q(eng, d0=d0, q8v=q8v):
                eng.transpose(PS[4][:, 0:128], q8v[:, d0 * 128:(d0 + 1) * 128],
                              identf[:])
                return eng.transpose(PS[5][:, 0:128],
                                     q8v[:, (d0 + 1) * 128:(d0 + 2) * 128],
                                     identf[:])
            S.step("tensor", tr2q, wait=("cV",), inc="cT")

            def ev2q(eng, d0=d0, ci=ci):
                c0 = d0 * 512 + ci * 128
                c1 = (d0 + 1) * 512 + ci * 128
                eng.tensor_copy(xdqT_c[:, c0:c0 + 128], PS[4][:, 0:128])
                return eng.tensor_copy(xdqT_c[:, c1:c1 + 128], PS[5][:, 0:128])
            S.step("vector", ev2q, wait=("cT",), inc="cV")

        def mkaug(eng, tt=tt):
            eng.memset(aug[:, 3:4], 1.0)
            return eng.tensor_copy(aug[:, 0:2], qpN_hold[tt % 2][:, 0:2]) \
                if False else eng.tensor_copy(aug[:, 0:2], ln_sc[0][:, 0:2])

    qpN_dummy = None

    def p1_chunk_proj(cc):
        ch0 = cc * 512

        def qp_mm(eng):
            last = None
            for d in range(ND):
                last = eng.matmul(PS[0][0:2, 0:512],
                                  WtqT[:, d * 2:(d + 1) * 2],
                                  xnTf_c[:, d * 512:(d + 1) * 512],
                                  start=(d == 0), stop=(d == ND - 1))
            return last
        S.step("tensor", qp_mm, wait=("cV", "cS"), inc="cT")

        def qp_ev(eng, ch0=ch0):
            return eng.activation(out=qp2[:], in_=PS[0][0:2, 0:512],
                                  func=AF.Identity, bias=btq_sb[:])
        S.step("scalar", qp_ev, wait=("cT", "cG"), inc="cS")

        def qp_rows(eng, ch0=ch0):
            eng.dma_start(out=qpx_d[:, ch0:ch0 + 512], in_=qp2[0:1, :]
                          ).then_inc(sems["cG"], 16)
            eng.dma_start(out=qpy_d[:, ch0:ch0 + 512], in_=qp2[1:2, :]
                          ).then_inc(sems["cG"], 16)
            return None
        S.step("gpsimd", qp_rows, wait=("cS",), inc="cG", dma_n=2)

        for si in range(4):
            tt = cc * 4 + si

            def qp_tr(eng, si=si):
                return eng.transpose(PS[4][0:128, 0:2],
                                     qp2[:, si * 128:(si + 1) * 128],
                                     identf[0:2, 0:2])
            S.step("tensor", qp_tr, wait=("cS", "cV"), inc="cT")

            def qp_trev(eng, tt=tt):
                eng.tensor_copy(qpN[:, tt * 2:tt * 2 + 2], PS[4][0:128, 0:2])
                return eng.tensor_copy(aug[:, 0:2], PS[4][0:128, 0:2])
            S.step("vector", qp_trev, wait=("cT",), inc="cV", pad=True)

            def sqa(eng, tt=tt):
                return eng.activation(out=aug_sq[:], in_=qpN[:, tt * 2:tt * 2 + 2],
                                      func=AF.Square)
            S.step("scalar", sqa, wait=("cV",), inc="cS", pad=True)

            def reda(eng):
                return eng.reduce_sum(aug[:, 2:3], aug_sq[:], axis=AX.X)
            S.step("vector", reda, wait=("cS",), inc="cV", pad=True)

            def gmm(eng, tt=tt):
                return eng.matmul(PS7[0:4, 0:4], aug[:], aug[:],
                                  start=(tt == 0), stop=(tt == NT - 1),
                                  skip_group_check=True)
            S.step("tensor", gmm, wait=("cV",), inc="cT")

        for name, Wsl, bcol, dst in (("k", WkTs, bkc, kT_d), ("q", WqTs, bqc, qT_d)):
            for m in range(ND):
                bank = PS[0] if m % 2 == 0 else PS[1]

                def mm(eng, Wsl=Wsl, m=m, bank=bank):
                    last = None
                    for c in range(ND):
                        last = eng.matmul(bank[:, :],
                                          Wsl[:, c * D + m * 128:c * D + (m + 1) * 128],
                                          xnT_c[:, c * 512:(c + 1) * 512],
                                          start=(c == 0), stop=(c == ND - 1))
                    return last
                S.step("tensor", mm, wait=("cV", "cS", "cG"), inc="cT")

                def ev(eng, m=m, bank=bank, bcol=bcol):
                    return eng.activation(out=proj_ev[m % 4][:], in_=bank[:, :],
                                          func=AF.Identity, bias=bcol[:, m:m + 1])
                S.step("scalar", ev, wait=("cT", "cG"), inc="cS")

                def st(eng, m=m, dst=dst, ch0=ch0):
                    eng.dma_start(out=dst[m * 128:(m + 1) * 128, ch0:ch0 + 512],
                                  in_=proj_ev[m % 4][:]).then_inc(sems["cG"], 16)
                    return None
                S.step("gpsimd", st, wait=("cS",), inc="cG", dma_n=1)

        for tsub in range(4):
            tt = cc * 4 + tsub
            for half in range(2):
                bank = PS[2] if half == 0 else PS[3]

                def mmv(eng, tsub=tsub, half=half, bank=bank):
                    last = None
                    for c in range(ND):
                        last = eng.matmul(
                            bank[:, :],
                            xnT_c[:, c * 512 + tsub * 128:c * 512 + (tsub + 1) * 128],
                            WvTs[:, c * D + half * 512:c * D + (half + 1) * 512],
                            start=(c == 0), stop=(c == ND - 1))
                    return last
                S.step("tensor", mmv, wait=("cV", "cG"), inc="cT")

            def evv(eng, tsub=tsub):
                eng.tensor_add(vfe_ev[0][:, 0:512], PS[2][:, :], bvc_bc[:, 0:512])
                eng.tensor_add(vfe_ev[0][:, 512:1024], PS[3][:, :], bvc_bc[:, 512:1024])
                return eng.tensor_copy(vfe_bf[0][:], vfe_ev[0][:])
            S.step("vector", evv, wait=("cT", "cG"), inc="cV")

            def stv(eng, tt=tt):
                eng.dma_start(out=V_d[tt * 128:(tt + 1) * 128, :], in_=vfe_bf[0][:]
                              ).then_inc(sems["cG"], 16)
                return None
            S.step("gpsimd", stv, wait=("cV",), inc="cG", dma_n=1)

            for half in range(2):
                bank = PS[2] if half == 0 else PS[3]

                def mmf(eng, tsub=tsub, half=half, bank=bank):
                    last = None
                    for c in range(ND):
                        last = eng.matmul(
                            bank[:, :],
                            xdqT_c[:, c * 512 + tsub * 128:c * 512 + (tsub + 1) * 128],
                            w8Ts[:, c * D + half * 512:c * D + (half + 1) * 512],
                            start=(c == 0), stop=(c == ND - 1))
                    return last
                S.step("tensor", mmf, wait=("cV", "cG"), inc="cT")

            def evf(eng, tsub=tsub):
                s8p = CST[:, 5:6]
                eng.tensor_scalar(out=vfe_ev[0][:, 0:512], in0=PS[2][:, :],
                                  scalar1=s8p, scalar2=None, op0=ALU.mult)
                eng.tensor_scalar(out=vfe_ev[0][:, 512:1024], in0=PS[3][:, :],
                                  scalar1=s8p, scalar2=None, op0=ALU.mult)
                return eng.tensor_add(vfe_ev[1][:], vfe_ev[0][:], b8c_bc[:])
            S.step("vector", evf, wait=("cT", "cG"), inc="cV")

            def stf(eng, tt=tt):
                eng.dma_start(out=ffn0_d[tt * 128:(tt + 1) * 128, :], in_=vfe_ev[1][:]
                              ).then_inc(sems["cG"], 16)
                return None
            S.step("gpsimd", stf, wait=("cV",), inc="cG", dma_n=1)

    for cc in range(NMC):
        for si in range(4):
            p1_tile(cc * 4 + si)
        p1_chunk_proj(cc)

    G_sb = sb("G_sb", (4, 4))
    G9 = sb("G9", (1, 16))

    def g_ev(eng):
        return eng.tensor_copy(G_sb[:], PS7[0:4, 0:4])
    S.step("vector", g_ev, wait=("cT",), inc="cV", pad=True)

    def g_gather(eng):
        for i in range(4):
            eng.dma_start(out=G9[0:1, 4 * i:4 * i + 4], in_=G_sb[i:i + 1, :]
                          ).then_inc(sems["cG"], 16)
        return None
    S.step("gpsimd", g_gather, wait=("cV",), inc="cG", dma_n=4)

    def thr_v(eng):
        g = lambda i: G9[:, i:i + 1]
        nop8 = lambda: vgap(eng)
        pc = probe_sb
        eng.tensor_mul(W_[:, 0:1], g(14), g(15))
        eng.tensor_mul(W_[:, 1:2], g(3), g(3))
        eng.tensor_mul(W_[:, 2:3], g(7), g(7)); nop8()
        eng.tensor_scalar_mul(W_[:, 3:4], W_[:, 0:1], 2.0)
        eng.tensor_add(W_[:, 4:5], W_[:, 1:2], W_[:, 2:3]); nop8()
        eng.tensor_scalar_mul(W_[:, 5:6], W_[:, 4:5], 2.0); nop8()
        eng.tensor_sub(W_[:, 6:7], W_[:, 3:4], W_[:, 5:6]); nop8()
        eng.tensor_scalar_mul(W_[:, 7:8], W_[:, 6:7], float(2.0**-24))
        eng.tensor_mul(W_[:, 8:9], g(15), g(10))
        eng.tensor_mul(W_[:, 9:10], g(14), g(14))
        eng.tensor_mul(W_[:, 10:11], g(2), g(3))
        eng.tensor_mul(W_[:, 11:12], g(6), g(7))
        eng.tensor_mul(W_[:, 12:13], g(0), g(0))
        eng.tensor_mul(W_[:, 13:14], g(1), g(1))
        eng.tensor_mul(W_[:, 14:15], g(5), g(5)); nop8()
        eng.tensor_scalar_mul(W_[:, 15:16], W_[:, 8:9], 2.0)
        eng.tensor_scalar_mul(W_[:, 16:17], W_[:, 9:10], 2.0)
        eng.tensor_add(W_[:, 17:18], W_[:, 10:11], W_[:, 11:12])
        eng.tensor_scalar_mul(W_[:, 18:19], W_[:, 13:14], 2.0); nop8()
        eng.tensor_scalar_mul(W_[:, 19:20], W_[:, 17:18], -8.0)
        eng.tensor_add(W_[:, 24:25], W_[:, 12:13], W_[:, 18:19]); nop8()
        eng.tensor_add(W_[:, 25:26], W_[:, 24:25], W_[:, 14:15]); nop8()
        eng.tensor_scalar_mul(W_[:, 26:27], W_[:, 25:26], 4.0)
        eng.tensor_add(W_[:, 27:28], W_[:, 15:16], W_[:, 16:17]); nop8()
        eng.tensor_add(W_[:, 28:29], W_[:, 27:28], W_[:, 19:20]); nop8()
        eng.tensor_add(W_[:, 29:30], W_[:, 28:29], W_[:, 26:27]); nop8()
        eng.tensor_mul(W_[:, 24:25], W_[:, 6:7], W_[:, 6:7]); nop8()
        eng.tensor_scalar_mul(W_[:, 25:26], W_[:, 24:25], float(2.0**-24)); nop8()
        eng.tensor_sub(W_[:, 26:27], W_[:, 29:30], W_[:, 25:26])
        eng.reciprocal(W_[:, 27:28], pc[:, 11:12]); nop8()
        return eng.tensor_mul(W_[:, 28:29], W_[:, 26:27], W_[:, 27:28])
    S.step("vector", thr_v, wait=("cG",), inc="cV", pad=True)

    def thr_s(eng):
        return eng.activation(out=W_[:, 29:30], in_=W_[:, 28:29], func=AF.Sqrt)
    S.step("scalar", thr_s, wait=("cV",), inc="cS", pad=True)

    def thr_v2(eng):
        nop8 = lambda: vgap(eng)
        eng.tensor_scalar_mul(W_[:, 30:31], W_[:, 29:30], 1.25); nop8()
        return eng.tensor_add(W_[:, 31:32], W_[:, 30:31], W_[:, 7:8])
    S.step("vector", thr_v2, wait=("cS",), inc="cV", pad=True)

    def thr_bc_t(eng):
        return eng.matmul(PS[0][:, 0:1], onesf[:], W_[:, 31:32],
                          start=True, stop=True)
    S.step("tensor", thr_bc_t, wait=("cV",), inc="cT")

    def thr_bc_v(eng):
        return eng.tensor_copy(CST[:, 6:7], PS[0][:, 0:1])
    S.step("vector", thr_bc_v, wait=("cT",), inc="cV")

    rn_a = sb("rn_a", (128, NT))
    rn_b = sb("rn_b", (128, NT))

    for tt in range(NT):
        def sq_t(eng, tt=tt):
            return eng.activation(out=aug_sq[:], in_=qpN[:, 2 * tt:2 * tt + 2],
                                  func=AF.Square)
        S.step("scalar", sq_t, wait=("cV",), inc="cS", pad=True)

        def red_t(eng, tt=tt):
            return eng.reduce_sum(rn_a[:, tt:tt + 1], aug_sq[:], axis=AX.X)
        S.step("vector", red_t, wait=("cS",), inc="cV", pad=True)

    def rn_sqrt(eng):
        return eng.activation(out=rn_b[:], in_=rn_a[:], func=AF.Sqrt)
    S.step("scalar", rn_sqrt, wait=("cV",), inc="cS", pad=True)

    def rn_max(eng):
        return eng.tensor_scalar_max(rn_a[:], rn_b[:], 1e-12)
    S.step("vector", rn_max, wait=("cS",), inc="cV")

    def rn_rec(eng):
        vgap(eng)
        return eng.reciprocal(rn_b[:], rn_a[:])
    S.step("vector", rn_rec, inc="cV", pad=True)

    for tt in range(NT):
        def remb_mk(eng, tt=tt):
            return eng.tensor_scalar(out=rembN[:, 2 * tt:2 * tt + 2],
                                     in0=qpN[:, 2 * tt:2 * tt + 2],
                                     scalar1=rn_b[:, tt:tt + 1], scalar2=None,
                                     op0=ALU.mult)
        S.step("vector", remb_mk, wait=("cV",), inc="cV", pad=True)

    if dbg:
        def st_stats_dbg(eng):
            eng.dma_start(out=stats_dbg[:, 0:32], in_=stat_r[:]
                          ).then_inc(sems["cG"], 16)
            eng.dma_start(out=stats_dbg[:, 32:64], in_=stat_mb[:]
                          ).then_inc(sems["cG"], 16)
            eng.dma_start(out=stats_dbg[:, 64:96], in_=stat_tmp[:]
                          ).then_inc(sems["cG"], 16)
            return None
        S.step("gpsimd", st_stats_dbg, wait=("cS", "cV"), inc="cG", dma_n=3)

    barrier()
    pop_scope()

    push_scope()
    boc_bc = sb("boc_bc", (128, D))
    bfqc_bc = sb("bfqc_bc", (128, D))
    mk_bcast(bo, boc_bc, 1.0)
    mk_bcast(bfq, bfqc_bc, 0.1)
    tril = sb("tril", (128, 4 * JT), BF16)

    def ldtril(eng):
        eng.dma_start(out=tril.rearrange("p (k j) -> p k j", j=JT),
                      in_=tril_h.rearrange("(k p) j -> p k j", p=128)
                      ).then_inc(sems["cG"], 16)
        return None
    S.step("gpsimd", ldtril, inc="cG", dma_n=1)

    WoTs = sb("WoTs", (128, D * ND), BF16)
    weight_T(Wo, D, D, WoTs)
    WfqT = sb("WfqT", (2, D))

    def ldwfq(eng):
        eng.dma_start(out=WfqT[:], in_=Wfq.rearrange("r c -> c r")
                      ).then_inc(sems["cL"], 16)
        return None
    S.step("sync", ldwfq, inc="cL", dma_n=1)

    QPXR = sb("QPXR", (128, T))
    QPYR = sb("QPYR", (128, T))
    push_scope()
    qprow = sb("qprow", (1, 1024))

    for ch in range(8):
        def ld_qrow(eng, ch=ch):
            eng.dma_start(out=qprow[:, 0:512], in_=qpx_d[:, ch * 512:(ch + 1) * 512]
                          ).then_inc(sems["cL"], 16)
            eng.dma_start(out=qprow[:, 512:1024], in_=qpy_d[:, ch * 512:(ch + 1) * 512]
                          ).then_inc(sems["cL"], 16)
            return None
        S.step("sync", ld_qrow, wait=("cT",), inc="cL", dma_n=2)

        def qpr_t(eng, ch=ch):
            eng.matmul(PS[0][:, :], onesf[:], qprow[:, 0:512],
                       start=True, stop=True)
            return eng.matmul(PS[1][:, :], onesf[:], qprow[:, 512:1024],
                              start=True, stop=True)
        S.step("tensor", qpr_t, wait=("cL", "cV"), inc="cT")

        def qpr_v(eng, ch=ch):
            eng.tensor_copy(QPXR[:, ch * 512:(ch + 1) * 512], PS[0][:, :])
            return eng.tensor_copy(QPYR[:, ch * 512:(ch + 1) * 512], PS[1][:, :])
        S.step("vector", qpr_v, wait=("cT",), inc="cV")

    barrier()
    pop_scope()
    qTi = sb("qTi", (128, ND * 512), BF16)
    kTj = [sb(f"kTj{i}", (128, ND * 512), BF16) for i in range(2)]
    PTev = sb("PTev", (128, 4 * 512), BF16)
    PTrd = [sb(f"PTrd{i}", (128, 512), BF16) for i in range(2)]
    remb2 = sb("remb2", (2, 128))
    ms = [sb(f"ms{i}", (128, 512)) for i in range(6)]
    P0t = sb("P0t", (128, 512), BF16)
    Pmt = sb("Pmt", (128, 512), BF16)
    rs_parts = sb("rs_parts", (128, 32))
    rs4 = sb("rs4", (128, 4))
    rr4 = sb("rr4", (128, 4))
    Vbuf = [sb(f"Vbuf{i}", (128, 512), BF16) for i in range(2)]
    ctxT = sb("ctxT", (128, ND * 512), BF16)
    xq_buf = [sb("xq_buf0", (128, D))] * 2
    fq_buf = [sb("fq_buf0", (128, D))] * 2
    cb = [sb(f"cb{i}", (128, D)) for i in range(4)]
    q6s = [sb(f"q6s{i}", (128, D)) for i in range(5)]
    q6sc = [sb(f"q6sc{i}", (128, 32)) for i in range(4)]
    pev3 = sb("pev3", (128, 256), BF16)
    xdN = sb("xdN", (128, D), BF16)
    x1N = sb("x1N", (128, D))
    xn1N = sb("xn1N", (128, D))
    ln1_sc = [sb("ln1_sc0", (128, D))]
    l1r = sb("l1r", (128, 4))
    l1tmp = sb("l1tmp", (128, 4))
    l1mb = sb("l1mb", (128, 4))

    def flash_itile(it):
        i0 = it * IT

        def ld_q(eng, i0=i0):
            eng.dma_start(out=qTi.rearrange("p (d j) -> p d j", j=512),
                          in_=qT_d.rearrange("(d p) t -> p d t", p=128)[:, :, i0:i0 + 512]
                          ).then_inc(sems["cL"], 16)
            return None
        S.step("sync", ld_q, wait=("cT",), inc="cL", dma_n=1)

        for jt in range(it + 1):
            j0 = jt * JT
            kb = kTj[jt % 2]

            def ld_k(eng, j0=j0, kb=kb):
                eng.dma_start(out=kb.rearrange("p (d j) -> p d j", j=512),
                              in_=kT_d.rearrange("(d p) t -> p d t", p=128)[:, :, j0:j0 + 512]
                              ).then_inc(sems["cL"], 16)
                return None
            S.step("sync", ld_k, wait=("cT",), inc="cL", dma_n=1)

            for isub in range(4):
                ti = it * 4 + isub
                sbank = PS[0] if isub % 2 == 0 else PS[1]

                def s_mm(eng, isub=isub, kb=kb, sbank=sbank):
                    last = None
                    for dd in range(ND):
                        last = eng.matmul(
                            sbank[:, :],
                            qTi[:, dd * 512 + isub * 128:dd * 512 + (isub + 1) * 128],
                            kb[:, dd * 512:(dd + 1) * 512],
                            start=(dd == 0), stop=(dd == ND - 1))
                    return last
                S.step("tensor", s_mm, wait=("cL", "cS", "cV"), inc="cT")

                def m_exp(eng, sbank=sbank):
                    return eng.activation(out=P0t[:], in_=sbank[:, :], func=AF.Exp,
                                          scale=1.0 / 32.0)
                S.step("scalar", m_exp, wait=("cT",), inc="cS")

                def m_sc(eng, ti=ti, j0=j0):
                    eng.tensor_scalar(out=ms[0][:], in0=QPXR[:, j0:j0 + 512],
                                      scalar1=qpN[:, 2 * ti:2 * ti + 1], scalar2=None,
                                      op0=ALU.subtract)
                    return eng.tensor_scalar(out=ms[1][:], in0=QPYR[:, j0:j0 + 512],
                                             scalar1=qpN[:, 2 * ti + 1:2 * ti + 2],
                                             scalar2=None, op0=ALU.subtract)
                S.step("vector", m_sc, wait=("cV", "cT", "cS"), inc="cV")

                def m_sq(eng):
                    eng.activation(out=ms[2][:], in_=ms[0][:], func=AF.Square)
                    eng.activation(out=ms[3][:], in_=ms[1][:], func=AF.Square)
                    return eng.activation(out=ms[4][:], in_=ms[0][:], func=AF.Identity,
                                          scale=CST[:, 3:4])
                S.step("scalar", m_sq, wait=("cV",), inc="cS")

                def m_neg(eng):
                    return eng.activation(out=ms[0][:], in_=ms[1][:], func=AF.Identity,
                                          scale=-1.0)
                S.step("scalar", m_neg, inc="cS")

                def m_v(eng, it=it, jt=jt, isub=isub):
                    eng.tensor_add(ms[5][:], ms[2][:], ms[3][:])
                    eng.tensor_max(ms[2][:], ms[0][:], ms[1][:])
                    eng.tensor_scalar(out=ms[3][:], in0=ms[5][:],
                                      scalar1=CST[:, 6:7], scalar2=None,
                                      op0=ALU.is_le)
                    eng.tensor_tensor(ms[5][:], ms[4][:], ms[2][:], ALU.is_gt)
                    eng.tensor_mul(ms[0][:], ms[3][:], ms[5][:])
                    eng.tensor_sub(ms[1][:], ms[3][:], ms[0][:])
                    if jt == it:
                        eng.tensor_mul(ms[2][:], ms[1][:],
                                       tril[:, isub * JT:(isub + 1) * JT])
                        mfin = ms[2]
                    else:
                        mfin = ms[1]
                    return eng.scalar_tensor_tensor(
                        out=Pmt[:], in0=mfin[:], scalar=1.0, in1=P0t[:],
                        op0=ALU.mult, op1=ALU.mult,
                        accum_out=rs_parts[:, isub * 8 + jt:isub * 8 + jt + 1])
                S.step("vector", m_v, wait=("cS", "cT"), inc="cV")

                for jj4 in range(4):
                    def p_tr(eng, jj4=jj4):
                        return eng.transpose(PSB[:, 0:128],
                                             Pmt[:, jj4 * 128:(jj4 + 1) * 128],
                                             identb[:])
                    S.step("tensor", p_tr, wait=("cV", "cG"), inc="cT")

                    def p_ev(eng, jj4=jj4, isub=isub):
                        return eng.tensor_copy(
                            PTev[:, jj4 * 512 + isub * 128:jj4 * 512 + (isub + 1) * 128],
                            PSB[:, 0:128])
                    S.step("vector", p_ev, wait=("cT", "cG"), inc="cV")

            def st_pt(eng, jt=jt):
                for jj4 in range(4):
                    js = jt * 4 + jj4
                    eng.dma_start(out=pT_d[js * 128:(js + 1) * 128, :],
                                  in_=PTev[:, jj4 * 512:(jj4 + 1) * 512]
                                  ).then_inc(sems["cG"], 16)
                return None
            S.step("gpsimd", st_pt, wait=("cV",), inc="cG", dma_n=4)

        def rs_red(eng, it=it):
            last = None
            for isub in range(4):
                last = eng.reduce_sum(rs4[:, isub:isub + 1],
                                      rs_parts[:, isub * 8:isub * 8 + it + 1],
                                      axis=AX.X)
            return last
        S.step("vector", rs_red, wait=("cV",), inc="cV")

        def rs_rec(eng):
            vgap(eng)
            return eng.reciprocal(rr4[:], rs4[:])
        S.step("vector", rs_rec, inc="cV", pad=True)

        njs = (it + 1) * 4
        for dhalf in range(2):
            for js in range(njs):
                vb = Vbuf[js % 2]
                pb = PTrd[js % 2]

                def ld_v(eng, js=js, dhalf=dhalf, vb=vb, pb=pb):
                    eng.dma_start(out=vb[:],
                                  in_=V_d[js * 128:(js + 1) * 128,
                                          dhalf * 512:(dhalf + 1) * 512]
                                  ).then_inc(sems["cL"], 16)
                    eng.dma_start(out=pb[:], in_=pT_d[js * 128:(js + 1) * 128, :]
                                  ).then_inc(sems["cL"], 16)
                    return None
                S.step("sync", ld_v, wait=("cT", "cG"), inc="cL", dma_n=2)

                def c_mm(eng, js=js, vb=vb, pb=pb, first=(js == 0),
                         last_js=(js == njs - 1)):
                    lasti = None
                    for ds in range(4):
                        lasti = eng.matmul(
                            PS[2 + ds][:, :],
                            vb[:, ds * 128:(ds + 1) * 128],
                            pb[:],
                            start=first, stop=last_js)
                    return lasti
                S.step("tensor", c_mm, wait=("cL", "cV"), inc="cT")

            def c_ev(eng, dhalf=dhalf):
                lasti = None
                for ds in range(4):
                    dd = dhalf * 4 + ds
                    lasti = eng.tensor_copy(ctxT[:, dd * 512:(dd + 1) * 512],
                                            PS[2 + ds][:, :])
                return lasti
            S.step("vector", c_ev, wait=("cT",), inc="cV")

        for isub in range(4):
            ti = it * 4 + isub

            def ao_mm(eng, isub=isub):
                lasti = None
                for half in range(2):
                    for dd in range(ND):
                        lasti = eng.matmul(
                            PS[0 + half][:, :],
                            ctxT[:, dd * 512 + isub * 128:dd * 512 + (isub + 1) * 128],
                            WoTs[:, dd * D + half * 512:dd * D + (half + 1) * 512],
                            start=(dd == 0), stop=(dd == ND - 1))
                return lasti
            S.step("tensor", ao_mm, wait=("cV", "cS"), inc="cT")

            def qo_tr(eng, ti=ti):
                return eng.transpose(PS[4][0:2, 0:128],
                                     rembN[:, 2 * ti:2 * ti + 2], identf[:])
            S.step("tensor", qo_tr, wait=("cV",), inc="cT")

            def qo_ev(eng):
                return eng.tensor_copy(remb2[:], PS[4][0:2, 0:128])
            S.step("vector", qo_ev, wait=("cT",), inc="cV", pad=True)

            def qo_mm(eng, ti=ti):
                eng.matmul(PS[2][:, :], remb2[:], WfqT[:, 0:512],
                           start=True, stop=True)
                return eng.matmul(PS[3][:, :], remb2[:], WfqT[:, 512:1024],
                                  start=True, stop=True)
            S.step("tensor", qo_mm, wait=("cV",), inc="cT")

            def ld_xf(eng, ti=ti, isub=isub):
                eng.dma_start(out=xq_buf[isub % 2][:],
                              in_=x_in[ti * 128:(ti + 1) * 128, :]
                              ).then_inc(sems["cL"], 16)
                eng.dma_start(out=fq_buf[isub % 2][:],
                              in_=ffn0_d[ti * 128:(ti + 1) * 128, :]
                              ).then_inc(sems["cL"], 16)
                return None
            S.step("sync", ld_xf, wait=("cV",), inc="cL", dma_n=2)

            def comb_v(eng, ti=ti, isub=isub):
                eng.tensor_scalar(out=cb[0][:, 0:512], in0=PS[0][:, :],
                                  scalar1=rr4[:, isub:isub + 1], scalar2=None,
                                  op0=ALU.mult)
                eng.tensor_scalar(out=cb[0][:, 512:1024], in0=PS[1][:, :],
                                  scalar1=rr4[:, isub:isub + 1], scalar2=None,
                                  op0=ALU.mult)
                eng.tensor_add(cb[1][:], cb[0][:], boc_bc[:])
                eng.scalar_tensor_tensor(out=cb[2][:, 0:512], in0=PS[2][:, :],
                                         scalar=0.1, in1=cb[1][:, 0:512],
                                         op0=ALU.mult, op1=ALU.add)
                eng.scalar_tensor_tensor(out=cb[2][:, 512:1024], in0=PS[3][:, :],
                                         scalar=0.1, in1=cb[1][:, 512:1024],
                                         op0=ALU.mult, op1=ALU.add)
                return eng.tensor_add(cb[0][:], cb[2][:], bfqc_bc[:])
            S.step("vector", comb_v, wait=("cT", "cL"), inc="cV")

            def comb_s(eng, ti=ti, isub=isub):
                return eng.activation(out=cb[3][:], in_=xq_buf[isub % 2][:],
                                      func=AF.Identity, scale=stat_r[:, ti:ti + 1],
                                      bias=stat_mb[:, ti:ti + 1])
            S.step("scalar", comb_s, wait=("cL",), inc="cS")

            def comb_v2(eng, ti=ti, isub=isub):
                eng.tensor_add(cb[1][:], cb[0][:], cb[3][:])
                eng.tensor_add(cb[2][:], cb[1][:], xq_buf[isub % 2][:])
                return eng.tensor_add(x1N[:], cb[2][:], fq_buf[isub % 2][:])
            S.step("vector", comb_v2, wait=("cS", "cV", "cG"), inc="cV")

            def st_x1(eng, ti=ti):
                eng.dma_start(out=x1_d[ti * 128:(ti + 1) * 128, :], in_=x1N[:]
                              ).then_inc(sems["cG"], 16)
                return None
            S.step("gpsimd", st_x1, wait=("cV",), inc="cG", dma_n=1)

            ic = isub % 4

            def l1_red(eng, ic=ic):
                nop8 = lambda: vgap(eng)
                eng.reduce_sum(l1tmp[:, ic:ic + 1], x1N[:], axis=AX.X, negate=True)
                nop8()
                return eng.tensor_scalar_mul(l1mb[:, ic:ic + 1],
                                             l1tmp[:, ic:ic + 1], 1.0 / D)
            S.step("vector", l1_red, wait=("cV",), inc="cV", pad=True)

            def l1_a(eng, ic=ic):
                eng.activation(out=ln1_sc[0][:], in_=x1N[:], func=AF.Identity,
                               bias=l1mb[:, ic:ic + 1])
                return eng.activation(out=q6s[0][:], in_=ln1_sc[0][:],
                                      func=AF.Square,
                                      accum_out=l1tmp[:, ic:ic + 1])
            S.step("scalar", l1_a, wait=("cV",), inc="cS", pad=True)

            def l1_v2(eng, ic=ic):
                return eng.tensor_scalar(out=l1r[:, ic:ic + 1],
                                         in0=l1tmp[:, ic:ic + 1],
                                         scalar1=1.0 / D, scalar2=EPS_LN,
                                         op0=ALU.mult, op1=ALU.add)
            S.step("vector", l1_v2, wait=("cS",), inc="cV", pad=True)

            def l1_s2(eng, ic=ic):
                return eng.activation(out=l1tmp[:, ic:ic + 1],
                                      in_=l1r[:, ic:ic + 1], func=AF.Sqrt)
            S.step("scalar", l1_s2, wait=("cV",), inc="cS", pad=True)

            def l1_v3(eng, ic=ic):
                nop8 = lambda: vgap(eng)
                eng.reciprocal(l1r[:, ic:ic + 1], l1tmp[:, ic:ic + 1])
                nop8()
                eng.tensor_mul(l1tmp[:, ic:ic + 1], l1mb[:, ic:ic + 1],
                               l1r[:, ic:ic + 1])
                nop8()
                return eng.tensor_scalar_mul(l1mb[:, ic:ic + 1],
                                             l1tmp[:, ic:ic + 1], 1.0)
            S.step("vector", l1_v3, wait=("cS",), inc="cV", pad=True)

            def l1_b(eng, ic=ic):
                return eng.activation(out=xn1N[:], in_=x1N[:], func=AF.Identity,
                                      scale=l1r[:, ic:ic + 1],
                                      bias=l1mb[:, ic:ic + 1])
            S.step("scalar", l1_b, wait=("cV",), inc="cS")

            def st_xn1(eng, ti=ti):
                eng.dma_start(out=xn1_d[ti * 128:(ti + 1) * 128, :], in_=xn1N[:]
                              ).then_inc(sems["cG"], 16)
                return None
            S.step("gpsimd", st_xn1, wait=("cS",), inc="cG", dma_n=1)

            def q6a(eng):
                nop8 = lambda: vgap(eng)
                eng.tensor_reduce(out=q6sc[0].rearrange("p (g o) -> p g o", o=1),
                                  in_=xn1N.rearrange("p (g k) -> p g k", k=32),
                                  axis=AX.X, op=ALU.max, apply_absolute_value=True)
                nop8()
                eng.tensor_scalar(out=q6sc[1][:], in0=q6sc[0][:],
                                  scalar1=CST[:, 7:8], scalar2=None, op0=ALU.mult)
                nop8()
                eng.tensor_scalar_max(q6sc[2][:], q6sc[1][:], 1e-30)
                nop8()
                eng.reciprocal(q6sc[3][:], q6sc[2][:])
                nop8()
                return eng.tensor_mul(
                    q6s[0][:],
                    xn1N.rearrange("p (g k) -> p g k", k=32),
                    q6sc[3].rearrange("p (g o) -> p g o", o=1).broadcast_to((128, 32, 32)))
            S.step("vector", q6a, wait=("cS",), inc="cV")

            def q6b(eng):
                M0 = CST[:, 1:2]
                w1_, w2_, w3_, w4_ = q6s[1], q6s[2], q6s[3], q6s[4]
                q = q6s[0]
                eng.tensor_scalar(out=w1_[:], in0=q[:], scalar1=-1.0, scalar2=None,
                                  op0=ALU.mult)
                eng.tensor_max(w2_[:], w1_[:], q[:])
                eng.tensor_single_scalar(out=w1_[:], in_=w2_[:], scalar=2.0,
                                         op=ALU.is_lt)
                eng.tensor_single_scalar(out=w3_[:], in_=w2_[:], scalar=4.0,
                                         op=ALU.is_lt)
                eng.tensor_scalar(out=w4_[:], in0=w3_[:], scalar1=2.0, scalar2=2.0,
                                  op0=ALU.mult, op1=ALU.add)
                eng.scalar_tensor_tensor(out=w2_[:], in0=w1_[:], scalar=4.0,
                                         in1=w4_[:], op0=ALU.mult, op1=ALU.add)
                eng.tensor_scalar(out=w4_[:], in0=w3_[:], scalar1=-0.25, scalar2=0.5,
                                  op0=ALU.mult, op1=ALU.add)
                eng.scalar_tensor_tensor(out=w3_[:], in0=w1_[:], scalar=-0.125,
                                         in1=w4_[:], op0=ALU.mult, op1=ALU.add)
                eng.tensor_mul(w1_[:], q[:], w2_[:])
                eng.tensor_scalar(out=w4_[:], in0=w1_[:], scalar1=M0, scalar2=None,
                                  op0=ALU.add)
                eng.tensor_scalar(out=w2_[:], in0=w4_[:], scalar1=M0, scalar2=None,
                                  op0=ALU.subtract)
                eng.tensor_sub(w4_[:], w2_[:], w1_[:])
                eng.tensor_single_scalar(out=w1_[:], in_=w4_[:], scalar=0.5,
                                         op=ALU.is_equal)
                eng.tensor_sub(w4_[:], w2_[:], w1_[:])
                eng.tensor_mul(q6s[0][:], w4_[:], w3_[:])
                return eng.tensor_mul(
                    xdN.rearrange("p (g k) -> p g k", k=32),
                    q6s[0].rearrange("p (g k) -> p g k", k=32),
                    q6sc[2].rearrange("p (g o) -> p g o", o=1).broadcast_to((128, 32, 32)))
            S.step("vector", q6b, inc="cV")

            for d0 in range(0, ND, 2):
                def t1(eng, d0=d0):
                    eng.transpose(PS[4][:, 0:128], xn1N[:, d0 * 128:(d0 + 1) * 128],
                                  identf[:])
                    return eng.transpose(PS[5][:, 0:128],
                                         xn1N[:, (d0 + 1) * 128:(d0 + 2) * 128],
                                         identf[:])
                S.step("tensor", t1, wait=("cV", "cS", "cG"), inc="cT")

                def e1(eng, d0=d0):
                    eng.tensor_copy(pev3[:, 0:128], PS[4][:, 0:128])
                    return eng.tensor_copy(pev3[:, 128:256], PS[5][:, 0:128])
                S.step("vector", e1, wait=("cT",), inc="cV")

                def s1(eng, d0=d0, ti=ti):
                    eng.dma_start(out=xn1T_d[d0 * 128:(d0 + 1) * 128,
                                             ti * 128:(ti + 1) * 128],
                                  in_=pev3[:, 0:128]).then_inc(sems["cG"], 16)
                    eng.dma_start(out=xn1T_d[(d0 + 1) * 128:(d0 + 2) * 128,
                                             ti * 128:(ti + 1) * 128],
                                  in_=pev3[:, 128:256]).then_inc(sems["cG"], 16)
                    return None
                S.step("gpsimd", s1, wait=("cV",), inc="cG", dma_n=2)

                def t2(eng, d0=d0):
                    eng.transpose(PSB[:, 0:128], xdN[:, d0 * 128:(d0 + 1) * 128],
                                  identb[:])
                    return eng.transpose(PSB[:, 128:256],
                                         xdN[:, (d0 + 1) * 128:(d0 + 2) * 128],
                                         identb[:])
                S.step("tensor", t2, wait=("cV", "cG"), inc="cT")

                def e2(eng, d0=d0):
                    eng.tensor_copy(pev3[:, 0:128], PSB[:, 0:128])
                    return eng.tensor_copy(pev3[:, 128:256], PSB[:, 128:256])
                S.step("vector", e2, wait=("cT",), inc="cV")

                def s2(eng, d0=d0, ti=ti):
                    eng.dma_start(out=xdT_d[d0 * 128:(d0 + 1) * 128,
                                            ti * 128:(ti + 1) * 128],
                                  in_=pev3[:, 0:128]).then_inc(sems["cG"], 16)
                    eng.dma_start(out=xdT_d[(d0 + 1) * 128:(d0 + 2) * 128,
                                            ti * 128:(ti + 1) * 128],
                                  in_=pev3[:, 128:256]).then_inc(sems["cG"], 16)
                    return None
                S.step("gpsimd", s2, wait=("cV",), inc="cG", dma_n=2)

    for it in range(T // IT):
        flash_itile(it)

    barrier()
    pop_scope()

    push_scope()
    bL1c_bc = sb("bL1c_bc", (128, D))
    mk_bcast(bL1, bL1c_bc, 1.0)
    wdT = sb("wdT", (128, ND * D), BF16)
    wqN = [sb(f"wqN{i}", (128, D)) for i in range(2)]
    wqb = sb("wqb", (128, D), BF16)
    p4sc = [sb(f"p4sc{i}", (128, D)) for i in range(4)]
    p4bk = [sb(f"p4bk{i}", (128, 32)) for i in range(4)]
    xdTb = [sb(f"xdTb{i}", (128, ND * 128), BF16) for i in range(2)]
    f1ev = [sb(f"f1ev{i}", (128, D)) for i in range(2)]

    def q6_pipeline(src, dstbf, sc, bk, eng):
        nop8 = lambda: vgap(eng)
        M0 = CST[:, 1:2]
        eng.tensor_reduce(out=bk[0].rearrange("p (g o) -> p g o", o=1),
                          in_=src.rearrange("p (g k) -> p g k", k=32),
                          axis=AX.X, op=ALU.max, apply_absolute_value=True)
        nop8()
        eng.tensor_scalar(out=bk[1][:], in0=bk[0][:],
                          scalar1=CST[:, 7:8], scalar2=None, op0=ALU.mult)
        nop8()
        eng.tensor_scalar_max(bk[2][:], bk[1][:], 1e-30)
        nop8()
        eng.reciprocal(bk[3][:], bk[2][:])
        nop8()
        eng.tensor_mul(sc[0][:],
                       src.rearrange("p (g k) -> p g k", k=32),
                       bk[3].rearrange("p (g o) -> p g o", o=1).broadcast_to((128, 32, 32)))
        eng.tensor_scalar(out=sc[1][:], in0=sc[0][:], scalar1=-1.0, scalar2=None,
                          op0=ALU.mult)
        eng.tensor_max(sc[2][:], sc[1][:], sc[0][:])
        eng.tensor_single_scalar(out=sc[1][:], in_=sc[2][:], scalar=2.0, op=ALU.is_lt)
        eng.tensor_single_scalar(out=sc[3][:], in_=sc[2][:], scalar=4.0, op=ALU.is_lt)
        eng.tensor_scalar(out=sc[2][:], in0=sc[3][:], scalar1=2.0, scalar2=2.0,
                          op0=ALU.mult, op1=ALU.add)
        eng.scalar_tensor_tensor(out=sc[2][:], in0=sc[1][:], scalar=4.0, in1=sc[2][:],
                                 op0=ALU.mult, op1=ALU.add) \
            if False else eng.scalar_tensor_tensor(
                out=p4_k_tmp[:], in0=sc[1][:], scalar=4.0, in1=sc[2][:],
                op0=ALU.mult, op1=ALU.add)
        eng.tensor_scalar(out=sc[2][:], in0=sc[3][:], scalar1=-0.25, scalar2=0.5,
                          op0=ALU.mult, op1=ALU.add)
        eng.scalar_tensor_tensor(out=sc[3][:], in0=sc[1][:], scalar=-0.125,
                                 in1=sc[2][:], op0=ALU.mult, op1=ALU.add)
        eng.tensor_mul(sc[1][:], sc[0][:], p4_k_tmp[:])
        eng.tensor_scalar(out=sc[2][:], in0=sc[1][:], scalar1=M0, scalar2=None,
                          op0=ALU.add)
        eng.tensor_scalar(out=sc[0][:], in0=sc[2][:], scalar1=M0, scalar2=None,
                          op0=ALU.subtract)
        eng.tensor_sub(sc[2][:], sc[0][:], sc[1][:])
        eng.tensor_single_scalar(out=sc[1][:], in_=sc[2][:], scalar=0.5,
                                 op=ALU.is_equal)
        eng.tensor_sub(sc[2][:], sc[0][:], sc[1][:])
        eng.tensor_mul(sc[0][:], sc[2][:], sc[3][:])
        return eng.tensor_mul(
            dstbf.rearrange("p (g k) -> p g k", k=32),
            sc[0].rearrange("p (g k) -> p g k", k=32),
            bk[2].rearrange("p (g o) -> p g o", o=1).broadcast_to((128, 32, 32)))

    p4_k_tmp = sb("p4_k_tmp", (128, D))

    for rt in range(ND):
        def ldw1(eng, rt=rt):
            eng.dma_start(out=wqN[rt % 2][:], in_=wL1[rt * 128:(rt + 1) * 128, :]
                          ).then_inc(sems["cL"], 16)
            return None
        S.step("sync", ldw1, wait=("cT",), inc="cL", dma_n=1)

        def q6w(eng, rt=rt):
            return q6_pipeline(wqN[rt % 2], wqb, p4sc, p4bk, eng)
        S.step("vector", q6w, wait=("cL", "cT"), inc="cV")

        for d0 in range(0, ND, 2):
            def trw(eng, d0=d0):
                eng.transpose(PSB[:, 0:128], wqb[:, d0 * 128:(d0 + 1) * 128],
                              identb[:])
                return eng.transpose(PSB[:, 128:256],
                                     wqb[:, (d0 + 1) * 128:(d0 + 2) * 128],
                                     identb[:])
            S.step("tensor", trw, wait=("cV",), inc="cT")

            def evw(eng, d0=d0, rt=rt):
                eng.tensor_copy(wdT[:, d0 * D + rt * 128:d0 * D + (rt + 1) * 128],
                                PSB[:, 0:128])
                return eng.tensor_copy(
                    wdT[:, (d0 + 1) * D + rt * 128:(d0 + 1) * D + (rt + 1) * 128],
                    PSB[:, 128:256])
            S.step("vector", evw, wait=("cT",), inc="cV")

    for tt in range(NT):
        xb = xdTb[tt % 2]

        def ldxd(eng, tt=tt, xb=xb):
            eng.dma_start(out=xb.rearrange("p (d j) -> p d j", j=128),
                          in_=xdT_d.rearrange("(d p) t -> p d t", p=128)
                          [:, :, tt * 128:(tt + 1) * 128]).then_inc(sems["cL"], 16)
            return None
        S.step("sync", ldxd, wait=("cT",), inc="cL", dma_n=1)

        for half in range(2):
            def mmf1(eng, xb=xb, half=half):
                last = None
                for dd in range(ND):
                    last = eng.matmul(
                        PS[half][:, :],
                        xb[:, dd * 128:(dd + 1) * 128],
                        wdT[:, dd * D + half * 512:dd * D + (half + 1) * 512],
                        start=(dd == 0), stop=(dd == ND - 1))
                return last
            S.step("tensor", mmf1, wait=("cL", "cV", "cG"), inc="cT")

        def evf1(eng, tt=tt):
            eng.tensor_copy(f1ev[0][:, 0:512], PS[0][:, :])
            eng.tensor_copy(f1ev[0][:, 512:1024], PS[1][:, :])
            return eng.tensor_add(f1ev[1][:], f1ev[0][:], bL1c_bc[:])
        S.step("vector", evf1, wait=("cT", "cG"), inc="cV")

        def stf1(eng, tt=tt):
            eng.dma_start(out=ffn1_d[tt * 128:(tt + 1) * 128, :], in_=f1ev[1][:]
                          ).then_inc(sems["cG"], 16)
            return None
        S.step("gpsimd", stf1, wait=("cV",), inc="cG", dma_n=1)

    barrier()
    pop_scope()

    push_scope()
    inpT = sb("inpT", (128, ND * 2 * DI), BF16)
    weight_T(inpw, 2 * DI, D, inpT)
    xprT = sb("xprT", (128, NDI * 96), BF16)
    ddtT = sb("ddtT", (64, DI), BF16)

    def ldxpr(eng):
        eng.dma_start(out=wld_buf[0:96, 0:DI], in_=xprw[:, :]).then_inc(sems["cL"], 16)
        return None
    S.step("sync", ldxpr, wait=("cT", "cV"), inc="cL", dma_n=1)

    for c in range(NDI):
        def trxp(eng, c=c):
            return eng.transpose(PS7[:, 0:96],
                                 wld_buf[0:96, c * 128:(c + 1) * 128],
                                 identf[0:96, 0:96])
        S.step("tensor", trxp, wait=("cL", "cV"), inc="cT")

        def evxp(eng, c=c):
            return eng.tensor_copy(xprT[:, c * 96:(c + 1) * 96], PS7[:, 0:96])
        S.step("vector", evxp, wait=("cT",), inc="cV", pad=True)

    def lddtw(eng):
        eng.dma_start(out=wld_buf[:, 0:DTR * NDI].rearrange("p (j r) -> p j r", r=DTR),
                      in_=dtw.rearrange("(j p) r -> p j r", p=128)
                      ).then_inc(sems["cL"], 16)
        return None
    S.step("sync", lddtw, wait=("cT", "cV"), inc="cL", dma_n=1)

    for c in range(NDI):
        def trdt(eng, c=c):
            return eng.transpose(PS7[0:64, 0:128],
                                 wld_buf[:, c * DTR:(c + 1) * DTR], identf[:])
        S.step("tensor", trdt, wait=("cL", "cV"), inc="cT")

        def evdt(eng, c=c):
            return eng.tensor_copy(ddtT[:, c * 128:(c + 1) * 128], PS7[0:64, 0:128])
        S.step("vector", evdt, wait=("cT",), inc="cV", pad=True)

    xnTb = [sb(f"xnTb{i}", (128, ND * MC), BF16) for i in range(2)]
    u_raw = sb("u_raw", (128, NDI * MC))
    ustage = sb("ustage", (128, NDI * MC), BF16)
    u_ext = sb("u_ext", (128, MC + 3))
    cv_t = [sb(f"cv_t{i}", (128, MC)) for i in range(2)]
    halo = sb("halo", (128, NDI * 3))
    zev = sb("zev", (128, MC), BF16)
    xdbl = sb("xdbl", (96, MC))
    dtT_bf = sb("dtT_bf", (64, MC), BF16)
    dl_f = [sb(f"dl_f{i}", (128, MC)) for i in range(4)]
    dl_b = sb("dl_b", (128, MC), BF16)
    w_b = sb("w_b", (128, MC), BF16)

    def halo_z(eng):
        return eng.memset(halo[:], 0.0)
    S.step("vector", halo_z, inc="cV")

    for cc in range(NMC):
        ch0 = cc * MC
        xb = xnTb[cc % 2]

        def ldxn1(eng, ch0=ch0, xb=xb):
            eng.dma_start(out=xb.rearrange("p (d j) -> p d j", j=MC),
                          in_=xn1T_d.rearrange("(d p) t -> p d t", p=128)
                          [:, :, ch0:ch0 + MC]).then_inc(sems["cL"], 16)
            return None
        S.step("sync", ldxn1, wait=("cT",), inc="cL", dma_n=1)

        for m in range(32):
            bank = PS[m % 2]

            def mmxz(eng, m=m, xb=xb, bank=bank):
                last = None
                for c in range(ND):
                    last = eng.matmul(
                        bank[:, :],
                        inpT[:, c * (2 * DI) + m * 128:c * (2 * DI) + (m + 1) * 128],
                        xb[:, c * MC:(c + 1) * MC],
                        start=(c == 0), stop=(c == ND - 1))
                return last
            S.step("tensor", mmxz, wait=("cL", "cV", "cS", "cG"), inc="cT")

            if m < 16:
                def evu(eng, m=m, bank=bank):
                    return eng.tensor_copy(u_raw[:, m * MC:(m + 1) * MC], bank[:, :])
                S.step("vector", evu, wait=("cT",), inc="cV")
            else:
                def evz(eng, bank=bank):
                    return eng.activation(out=zev[:], in_=bank[:, :], func=AF.Silu)
                S.step("scalar", evz, wait=("cT", "cG"), inc="cS")

                def stz(eng, m=m, ch0=ch0):
                    dt = m - 16
                    eng.dma_start(out=sz_d[dt * 128:(dt + 1) * 128, ch0:ch0 + MC],
                                  in_=zev[:]).then_inc(sems["cG"], 16)
                    return None
                S.step("gpsimd", stz, wait=("cS",), inc="cG", dma_n=1)

        for dt in range(NDI):
            def cva(eng, dt=dt):
                eng.tensor_copy(u_ext[:, 0:3], halo[:, dt * 3:(dt + 1) * 3])
                eng.tensor_copy(u_ext[:, 3:MC + 3], u_raw[:, dt * MC:(dt + 1) * MC])
                eng.tensor_copy(halo[:, dt * 3:(dt + 1) * 3],
                                u_raw[:, dt * MC + MC - 3:dt * MC + MC])
                eng.tensor_scalar(out=cv_t[0][:], in0=u_ext[:, 0:MC],
                                  scalar1=convwc[:, dt * 4:dt * 4 + 1], scalar2=None,
                                  op0=ALU.mult)
                eng.scalar_tensor_tensor(out=cv_t[1][:], in0=u_ext[:, 1:MC + 1],
                                         scalar=convwc[:, dt * 4 + 1:dt * 4 + 2],
                                         in1=cv_t[0][:], op0=ALU.mult, op1=ALU.add)
                eng.scalar_tensor_tensor(out=cv_t[0][:], in0=u_ext[:, 2:MC + 2],
                                         scalar=convwc[:, dt * 4 + 2:dt * 4 + 3],
                                         in1=cv_t[1][:], op0=ALU.mult, op1=ALU.add)
                return eng.scalar_tensor_tensor(
                    out=cv_t[1][:], in0=u_ext[:, 3:MC + 3],
                    scalar=convwc[:, dt * 4 + 3:dt * 4 + 4],
                    in1=cv_t[0][:], op0=ALU.mult, op1=ALU.add)
            S.step("vector", cva, wait=("cS", "cG", "cV"), inc="cV")

            def cvs(eng, dt=dt):
                return eng.activation(out=ustage[:, dt * MC:(dt + 1) * MC],
                                      in_=cv_t[1][:], func=AF.Silu,
                                      bias=convbc[:, dt:dt + 1])
            S.step("scalar", cvs, wait=("cV",), inc="cS")

            def stu(eng, dt=dt, ch0=ch0):
                eng.dma_start(out=u_d[dt * 128:(dt + 1) * 128, ch0:ch0 + MC],
                              in_=ustage[:, dt * MC:(dt + 1) * MC]
                              ).then_inc(sems["cG"], 16)
                return None
            S.step("gpsimd", stu, wait=("cS",), inc="cG", dma_n=1)

        def mmxd(eng):
            last = None
            for c in range(NDI):
                last = eng.matmul(PS[2][0:96, :], xprT[:, c * 96:(c + 1) * 96],
                                  ustage[:, c * MC:(c + 1) * MC],
                                  start=(c == 0), stop=(c == NDI - 1))
            return last
        S.step("tensor", mmxd, wait=("cS", "cV"), inc="cT")

        def evxd(eng):
            eng.tensor_copy(xdbl[:], PS[2][0:96, :])
            return eng.tensor_copy(dtT_bf[:], PS[2][0:64, :])
        S.step("vector", evxd, wait=("cT", "cG"), inc="cV", pad=True)

        def stbc(eng, ch0=ch0):
            eng.dma_start(out=bm_d[:, ch0:ch0 + MC], in_=xdbl[64:80, :]
                          ).then_inc(sems["cG"], 16)
            eng.dma_start(out=cm_d[:, ch0:ch0 + MC], in_=xdbl[80:96, :]
                          ).then_inc(sems["cG"], 16)
            return None
        S.step("gpsimd", stbc, wait=("cV",), inc="cG", dma_n=2)

        for dt in range(NDI):
            def mmdt(eng, dt=dt):
                return eng.matmul(PS[3][:, :], ddtT[:, dt * 128:(dt + 1) * 128],
                                  dtT_bf[:], start=True, stop=True)
            S.step("tensor", mmdt, wait=("cV", "cS"), inc="cT")

            def spl(eng, dt=dt):
                eng.activation(out=dl_f[0][:], in_=PS[3][:, :], func=AF.Relu,
                               bias=dtbc[:, dt:dt + 1])
                eng.activation(out=dl_f[1][:], in_=PS[3][:, :], func=AF.Abs,
                               bias=dtbc[:, dt:dt + 1])
                eng.activation(out=dl_f[2][:], in_=dl_f[1][:], func=AF.Exp,
                               scale=-1.0)
                return eng.activation(out=dl_f[3][:], in_=dl_f[2][:], func=AF.Ln,
                                      bias=1.0)
            S.step("scalar", spl, wait=("cT", "cV"), inc="cS")

            def dl_v(eng, dt=dt):
                eng.tensor_add(dl_f[1][:], dl_f[0][:], dl_f[3][:])
                eng.tensor_copy(dl_b[:], dl_f[1][:])
                return eng.tensor_mul(w_b[:], dl_f[1][:],
                                      ustage[:, dt * MC:(dt + 1) * MC])
            S.step("vector", dl_v, wait=("cS", "cG"), inc="cV")

            def stdl(eng, dt=dt, ch0=ch0):
                eng.dma_start(out=dl_d[dt * 128:(dt + 1) * 128, ch0:ch0 + MC],
                              in_=dl_b[:]).then_inc(sems["cG"], 16)
                eng.dma_start(out=wsc_d[dt * 128:(dt + 1) * 128, ch0:ch0 + MC],
                              in_=w_b[:]).then_inc(sems["cG"], 16)
                return None
            S.step("gpsimd", stdl, wait=("cV",), inc="cG", dma_n=2)

    barrier()
    pop_scope()

    push_scope()
    M2C = 1024
    NM2 = T // M2C
    outT = sb("outT", (128, NDI * D), BF16)
    weight_T(outw, D, DI, outT)
    BBs = sb("BBs", (128, DST * M2C), BF16)
    CBs = sb("CBs", (128, DST * M2C), BF16)
    brow2 = sb("brow2", (1, M2C))
    crow2 = sb("crow2", (1, M2C))
    dlb2 = [sb(f"dlb2{i}", (128, M2C), BF16) for i in range(2)]
    wb2 = [sb(f"wb2{i}", (128, M2C), BF16) for i in range(2)]
    ub2 = [sb(f"ub2{i}", (128, M2C), BF16) for i in range(2)]
    szb2 = [sb(f"szb2{i}", (128, M2C), BF16) for i in range(2)]
    a_ring = [sb(f"a_ring{i}", (128, M2C), BF16) for i in range(4)]
    b_buf = [sb(f"b_buf{i}", (128, M2C), BF16) for i in range(2)]
    h_buf = [sb(f"h_buf{i}", (128, M2C), BF16) for i in range(2)]
    p_buf = [sb("p_buf0", (128, M2C), BF16)] * 2
    yacc = [sb(f"yacc{i}", (128, M2C), BF16) for i in range(2)]
    ud_f = sb("ud_f", (128, M2C), BF16)
    y3_f = sb("y3_f", (128, M2C), BF16)
    y2st = sb("y2st", (128, NDI * M2C), BF16)
    hm_ev = sb("hm_ev", (128, M2C), BF16)

    for cc in range(NM2):
        ch0 = cc * M2C

        for n in range(DST):
            def ldrow(eng, n=n, ch0=ch0):
                eng.dma_start(out=brow2[:], in_=bm_d[n:n + 1, ch0:ch0 + M2C]
                              ).then_inc(sems["cL"], 16)
                eng.dma_start(out=crow2[:], in_=cm_d[n:n + 1, ch0:ch0 + M2C]
                              ).then_inc(sems["cL"], 16)
                return None
            S.step("sync", ldrow, wait=("cV",), inc="cL", dma_n=2)

            def bbmm(eng, n=n):
                eng.matmul(PS[0][:, :], onesf[:], brow2[:, 0:512],
                           start=True, stop=True)
                eng.matmul(PS[1][:, :], onesf[:], brow2[:, 512:1024],
                           start=True, stop=True)
                eng.matmul(PS[2][:, :], onesf[:], crow2[:, 0:512],
                           start=True, stop=True)
                return eng.matmul(PS[3][:, :], onesf[:], crow2[:, 512:1024],
                                  start=True, stop=True)
            S.step("tensor", bbmm, wait=("cL", "cV"), inc="cT")

            def bbev(eng, n=n):
                eng.tensor_copy(BBs[:, n * M2C:n * M2C + 512], PS[0][:, :])
                eng.tensor_copy(BBs[:, n * M2C + 512:(n + 1) * M2C], PS[1][:, :])
                eng.tensor_copy(CBs[:, n * M2C:n * M2C + 512], PS[2][:, :])
                return eng.tensor_copy(CBs[:, n * M2C + 512:(n + 1) * M2C],
                                       PS[3][:, :])
            S.step("vector", bbev, wait=("cT",), inc="cV")

        for dt in range(NDI):
            dlb, wb = dlb2[dt % 2], wb2[dt % 2]

            def lddw(eng, dt=dt, ch0=ch0, dlb=dlb, wb=wb):
                eng.dma_start(out=dlb[:], in_=dl_d[dt * 128:(dt + 1) * 128,
                                                  ch0:ch0 + M2C]
                              ).then_inc(sems["cL"], 16)
                eng.dma_start(out=wb[:], in_=wsc_d[dt * 128:(dt + 1) * 128,
                                                   ch0:ch0 + M2C]
                              ).then_inc(sems["cL"], 16)
                return None
            S.step("sync", lddw, wait=("cV", "cS"), inc="cL", dma_n=2)

            for g4 in range(4):
                def a_g(eng, dt=dt, g4=g4, dlb=dlb):
                    last = None
                    for i in range(4):
                        n = g4 * 4 + i
                        last = eng.activation(out=a_ring[i][:], in_=dlb[:],
                                              func=AF.Exp,
                                              scale=Ac[:, dt * DST + n:dt * DST + n + 1])
                    return last
                S.step("scalar", a_g, wait=("cL", "cV"), inc="cS")

                def scan_g(eng, dt=dt, g4=g4, wb=wb):
                    last = None
                    for i in range(4):
                        n = g4 * 4 + i
                        col = dt * DST + n
                        eng.tensor_mul(b_buf[i % 2][:], wb[:],
                                       BBs[:, n * M2C:(n + 1) * M2C])
                        eng.tensor_tensor_scan(
                            out=h_buf[i % 2][:], data0=a_ring[i][:],
                            data1=b_buf[i % 2][:],
                            initial=scan_st[:, col:col + 1],
                            op0=ALU.mult, op1=ALU.add)
                        eng.tensor_mul(p_buf[i % 2][:], h_buf[i % 2][:],
                                       CBs[:, n * M2C:(n + 1) * M2C])
                        if n == 0:
                            eng.tensor_copy(yacc[1][:], p_buf[i % 2][:])
                        else:
                            eng.tensor_add(yacc[(n + 1) % 2][:], yacc[n % 2][:],
                                           p_buf[i % 2][:])
                        last = eng.tensor_copy(scan_st[:, col:col + 1],
                                               h_buf[i % 2][:, M2C - 1:M2C])
                    return last
                S.step("vector", scan_g, wait=("cS",), inc="cV")

            def ldusz(eng, dt=dt, ch0=ch0):
                eng.dma_start(out=ub2[dt % 2][:], in_=u_d[dt * 128:(dt + 1) * 128,
                                                         ch0:ch0 + M2C]
                              ).then_inc(sems["cL"], 16)
                eng.dma_start(out=szb2[dt % 2][:], in_=sz_d[dt * 128:(dt + 1) * 128,
                                                           ch0:ch0 + M2C]
                              ).then_inc(sems["cL"], 16)
                return None
            S.step("sync", ldusz, wait=("cV", "cS"), inc="cL", dma_n=2)

            def gate_s(eng, dt=dt):
                return eng.activation(out=ud_f[:], in_=ub2[dt % 2][:], func=AF.Identity,
                                      scale=Dc[:, dt:dt + 1])
            S.step("scalar", gate_s, wait=("cL", "cV"), inc="cS")

            def gate_v(eng, dt=dt):
                eng.tensor_add(y3_f[:], yacc[0][:], ud_f[:])
                return eng.tensor_mul(y2st[:, dt * M2C:(dt + 1) * M2C], y3_f[:],
                                      szb2[dt % 2][:])
            S.step("vector", gate_v, wait=("cS",), inc="cV")

        for m in range(ND):
            bank2 = (PS[2], PS[3]) if m % 2 == 0 else (PS[4], PS[5])

            def mmop(eng, m=m, bank2=bank2):
                last = None
                for k in range(NDI):
                    for half in range(2):
                        last = eng.matmul(
                            bank2[half][:, :],
                            outT[:, k * D + m * 128:k * D + (m + 1) * 128],
                            y2st[:, k * M2C + half * 512:k * M2C + (half + 1) * 512],
                            start=(k == 0), stop=(k == NDI - 1))
                return last
            S.step("tensor", mmop, wait=("cV", "cG"), inc="cT")

            def evop(eng, m=m, bank2=bank2):
                eng.tensor_copy(hm_ev[:, 0:512], bank2[0][:, :])
                return eng.tensor_copy(hm_ev[:, 512:1024], bank2[1][:, :])
            S.step("vector", evop, wait=("cT", "cG"), inc="cV")

            def stop_(eng, m=m, ch0=ch0):
                eng.dma_start(out=hm_d[m * 128:(m + 1) * 128, ch0:ch0 + M2C],
                              in_=hm_ev[:]).then_inc(sems["cG"], 16)
                return None
            S.step("gpsimd", stop_, wait=("cV",), inc="cG", dma_n=1)

    barrier()
    pop_scope()

    push_scope()
    b2c_bc = sb("b2c_bc", (128, D))
    nw_bc = sb("nw_bc", (128, D))
    fnw_bc = sb("fnw_bc", (128, D))
    mk_bcast(b2, b2c_bc, 1.0)
    mk_bcast(nw, nw_bc, 1.0)
    mk_bcast(fnw, fnw_bc, 1.0)
    w2T_dd = dsc("w2T_dd", (DFF, D), BF16)
    w1Ts = sb("w1Ts", (128, ND * DFF), BF16)
    weight_T(w1, DFF, D, w1Ts)

    w2ev = sb("w2ev", (128, 128), BF16)
    for rt in range(ND):
        def ldw2(eng, rt=rt):
            eng.dma_start(out=wld_buf[:, 0:DFF], in_=w2[rt * 128:(rt + 1) * 128, :]
                          ).then_inc(sems["cL"], 16)
            return None
        S.step("sync", ldw2, wait=("cT",), inc="cL", dma_n=1)
        for c in range(NDF):
            def trw2(eng, c=c):
                return eng.transpose(PS7[:, 0:128],
                                     wld_buf[:, c * 128:(c + 1) * 128], identf[:])
            S.step("tensor", trw2, wait=("cL", "cG"), inc="cT")

            def evw2(eng, c=c):
                return eng.tensor_copy(w2ev[:], PS7[:, 0:128])
            S.step("vector", evw2, wait=("cT", "cG"), inc="cV")

            def stw2(eng, c=c, rt=rt):
                eng.dma_start(out=w2T_dd[c * 128:(c + 1) * 128,
                                         rt * 128:(rt + 1) * 128],
                              in_=w2ev[:]).then_inc(sems["cG"], 16)
                return None
            S.step("gpsimd", stw2, wait=("cV",), inc="cG", dma_n=1)

    M3C = 256
    NS3 = M3C // 128
    hTb = [sb(f"hTb{i}", (128, ND * M3C), BF16) for i in range(2)]
    xn1b = [sb(f"xn1b{i}", (128, D)) for i in range(2)]
    vN = sb("vN", (128, D))
    h2a = sb("h2a", (128, D))
    h2N = sb("h2N", (128, NS3 * D))
    h2bf = sb("h2bf", (128, D), BF16)
    h2T = sb("h2T", (128, ND * M3C), BF16)
    gm = [sb(f"gm{i}", (128, M3C), BF16) for i in range(2)]
    w2Tb = [sb(f"w2Tb{i}", (128, D), BF16) for i in range(2)]
    m3r = sb("m3r", (128, 8))
    ffv = [sb(f"ffv{i}", (128, D)) for i in range(3)]
    x1b = [sb(f"x1b{i}", (128, D)) for i in range(2)]
    f1b = [sb(f"f1b{i}", (128, D)) for i in range(2)]
    x2o = sb("x2o", (128, D))

    for cc in range(T // M3C):
        ch0 = cc * M3C
        hb = hTb[cc % 2]

        def ldh(eng, ch0=ch0, hb=hb):
            eng.dma_start(out=hb.rearrange("p (d j) -> p d j", j=M3C),
                          in_=hm_d.rearrange("(d p) t -> p d t", p=128)
                          [:, :, ch0:ch0 + M3C]).then_inc(sems["cL"], 16)
            return None
        S.step("sync", ldh, wait=("cT",), inc="cL", dma_n=1)

        for tsub in range(NS3):
            ti = cc * NS3 + tsub

            def ldxn1c(eng, ti=ti, tsub=tsub):
                eng.dma_start(out=xn1b[tsub % 2][:],
                              in_=xn1_d[ti * 128:(ti + 1) * 128, :]
                              ).then_inc(sems["cL"], 16)
                return None
            S.step("sync", ldxn1c, wait=("cV",), inc="cL", dma_n=1)

            for d0 in range(0, ND, 2):
                def trh(eng, d0=d0, tsub=tsub, hb=hb):
                    eng.transpose(PSB[:, 0:128],
                                  hb[:, d0 * M3C + tsub * 128:d0 * M3C + (tsub + 1) * 128],
                                  identb[:])
                    return eng.transpose(
                        PSB[:, 128:256],
                        hb[:, (d0 + 1) * M3C + tsub * 128:(d0 + 1) * M3C + (tsub + 1) * 128],
                        identb[:])
                S.step("tensor", trh, wait=("cL", "cV"), inc="cT")

                def evh(eng, d0=d0, tsub=tsub):
                    eng.tensor_add(vN[:, d0 * 128:(d0 + 1) * 128], PSB[:, 0:128],
                                   xn1b[tsub % 2][:, d0 * 128:(d0 + 1) * 128])
                    return eng.tensor_add(vN[:, (d0 + 1) * 128:(d0 + 2) * 128],
                                          PSB[:, 128:256],
                                          xn1b[tsub % 2][:, (d0 + 1) * 128:(d0 + 2) * 128])
                S.step("vector", evh, wait=("cT", "cL"), inc="cV")

            def rms1_s(eng, tsub=tsub):
                return eng.activation(out=h2a[:], in_=vN[:], func=AF.Square,
                                      accum_out=m3r[:, tsub:tsub + 1])
            S.step("scalar", rms1_s, wait=("cV",), inc="cS", pad=True)

            def rms1_v2(eng, tsub=tsub):
                return eng.tensor_scalar(out=m3r[:, tsub + 4:tsub + 5],
                                         in0=m3r[:, tsub:tsub + 1],
                                         scalar1=1.0 / D, scalar2=EPS_RMS,
                                         op0=ALU.mult, op1=ALU.add)
            S.step("vector", rms1_v2, wait=("cS",), inc="cV", pad=True)

            def rms1_sq(eng, tsub=tsub):
                return eng.activation(out=m3r[:, tsub:tsub + 1],
                                      in_=m3r[:, tsub + 4:tsub + 5], func=AF.Sqrt)
            S.step("scalar", rms1_sq, wait=("cV",), inc="cS", pad=True)

            def rms1_r(eng, tsub=tsub):
                [eng.engine_nop() for _ in range(8)]
                return eng.reciprocal(m3r[:, tsub + 4:tsub + 5],
                                      m3r[:, tsub:tsub + 1])
            S.step("vector", rms1_r, wait=("cS",), inc="cV", pad=True)

            def rms1_f(eng, tsub=tsub):
                return eng.activation(out=h2a[:], in_=vN[:], func=AF.Identity,
                                      scale=m3r[:, tsub + 4:tsub + 5])
            S.step("scalar", rms1_f, wait=("cV",), inc="cS")

            def rms1_v(eng, tsub=tsub):
                eng.tensor_mul(h2N[:, tsub * D:(tsub + 1) * D], h2a[:], nw_bc[:])
                return eng.tensor_copy(h2bf[:], h2N[:, tsub * D:(tsub + 1) * D])
            S.step("vector", rms1_v, wait=("cS",), inc="cV")

            for d0 in range(0, ND, 2):
                def trh2(eng, d0=d0):
                    eng.transpose(PSB[:, 0:128], h2bf[:, d0 * 128:(d0 + 1) * 128],
                                  identb[:])
                    return eng.transpose(PSB[:, 128:256],
                                         h2bf[:, (d0 + 1) * 128:(d0 + 2) * 128],
                                         identb[:])
                S.step("tensor", trh2, wait=("cV",), inc="cT")

                def evh2(eng, d0=d0, tsub=tsub):
                    eng.tensor_copy(
                        h2T[:, d0 * M3C + tsub * 128:d0 * M3C + (tsub + 1) * 128],
                        PSB[:, 0:128])
                    return eng.tensor_copy(
                        h2T[:, (d0 + 1) * M3C + tsub * 128:(d0 + 1) * M3C + (tsub + 1) * 128],
                        PSB[:, 128:256])
                S.step("vector", evh2, wait=("cT",), inc="cV")

        def ldw2b(eng, m=0):
            eng.dma_start(out=w2Tb[m % 2][:], in_=w2T_dd[m * 128:(m + 1) * 128, :]
                          ).then_inc(sems["cL"], 16)
            return None

        def mmg(eng, m=0):
            last = None
            for k in range(ND):
                last = eng.matmul(PS[4 + m % 2][:, 0:M3C],
                                  w1Ts[:, k * DFF + m * 128:k * DFF + (m + 1) * 128],
                                  h2T[:, k * M3C:(k + 1) * M3C],
                                  start=(k == 0), stop=(k == ND - 1))
            return last

        def evg(eng, m=0):
            return eng.activation(out=gm[m % 2][:], in_=PS[4 + m % 2][:, 0:M3C],
                                  func=AF.Gelu, bias=b1c[:, m:m + 1])

        def mmff(eng, m=0):
            last = None
            for tsub in range(NS3):
                for half in range(2):
                    last = eng.matmul(
                        PS[tsub * 2 + half][:, :],
                        gm[m % 2][:, tsub * 128:(tsub + 1) * 128],
                        w2Tb[m % 2][:, half * 512:(half + 1) * 512],
                        start=(m == 0), stop=(m == NDF - 1))
            return last

        import functools as _ft
        evg_mark = {}
        ldw_mark = {}
        for m in range(NDF + 1):
            if m < NDF:
                S.step("sync", _ft.partial(ldw2b, m=m), wait=("cT",),
                       inc="cL", dma_n=1)
                ldw_mark[m] = dict(cL=S.counts.get("cL", 0))
                S.step("tensor", _ft.partial(mmg, m=m), wait=("cV", "cS"),
                       inc="cT",
                       wait_at=(evg_mark[m - 2] if m >= 2 else None))
                S.step("scalar", _ft.partial(evg, m=m), wait=("cT",), inc="cS")
                evg_mark[m] = dict(cS=S.counts.get("cS", 0))
            if m >= 1:
                w = ("cS", "cL", "cV") if m - 1 < 2 else ("cS", "cL")
                S.step("tensor", _ft.partial(mmff, m=m - 1), wait=w, inc="cT",
                       wait_at=dict(evg_mark[m - 1], **ldw_mark[m - 1]))

        for tsub in range(NS3):
            ti = cc * NS3 + tsub

            def ldx1f(eng, ti=ti, tsub=tsub):
                eng.dma_start(out=x1b[tsub % 2][:],
                              in_=x1_d[ti * 128:(ti + 1) * 128, :]
                              ).then_inc(sems["cL"], 16)
                eng.dma_start(out=f1b[tsub % 2][:],
                              in_=ffn1_d[ti * 128:(ti + 1) * 128, :]
                              ).then_inc(sems["cL"], 16)
                return None
            S.step("sync", ldx1f, wait=("cG",), inc="cL", dma_n=2)

            def ffev(eng, tsub=tsub):
                eng.tensor_add(ffv[0][:, 0:512], PS[tsub * 2][:, :], b2c_bc[:, 0:512])
                eng.tensor_add(ffv[0][:, 512:1024], PS[tsub * 2 + 1][:, :],
                               b2c_bc[:, 512:1024])
                return eng.tensor_add(ffv[1][:], ffv[0][:],
                                      h2N[:, tsub * D:(tsub + 1) * D])
            S.step("vector", ffev, wait=("cT",), inc="cV")

            def rms2_s(eng, tsub=tsub):
                return eng.activation(out=ffv[2][:], in_=ffv[1][:], func=AF.Square,
                                      accum_out=m3r[:, tsub:tsub + 1])
            S.step("scalar", rms2_s, wait=("cV", "cG"), inc="cS", pad=True)

            def rms2_v2(eng, tsub=tsub):
                return eng.tensor_scalar(out=m3r[:, tsub + 4:tsub + 5],
                                         in0=m3r[:, tsub:tsub + 1],
                                         scalar1=1.0 / D, scalar2=EPS_RMS,
                                         op0=ALU.mult, op1=ALU.add)
            S.step("vector", rms2_v2, wait=("cS",), inc="cV", pad=True)

            def rms2_sq(eng, tsub=tsub):
                return eng.activation(out=m3r[:, tsub:tsub + 1],
                                      in_=m3r[:, tsub + 4:tsub + 5], func=AF.Sqrt)
            S.step("scalar", rms2_sq, wait=("cV",), inc="cS", pad=True)

            def rms2_r(eng, tsub=tsub):
                [eng.engine_nop() for _ in range(8)]
                return eng.reciprocal(m3r[:, tsub + 4:tsub + 5],
                                      m3r[:, tsub:tsub + 1])
            S.step("vector", rms2_r, wait=("cS",), inc="cV", pad=True)

            def rms2_f(eng, tsub=tsub):
                return eng.activation(out=ffv[2][:], in_=ffv[1][:], func=AF.Identity,
                                      scale=m3r[:, tsub + 4:tsub + 5])
            S.step("scalar", rms2_f, wait=("cV",), inc="cS")

            def fin_v(eng, tsub=tsub):
                eng.tensor_mul(ffv[0][:], ffv[2][:], fnw_bc[:])
                eng.tensor_add(x2o[:], x1b[tsub % 2][:], ffv[0][:])
                return eng.tensor_add(ffv[2][:], x2o[:], f1b[tsub % 2][:])
            S.step("vector", fin_v, wait=("cS", "cL", "cG"), inc="cV")

            def st_out(eng, ti=ti):
                eng.dma_start(out=out[ti * 128:(ti + 1) * 128, :], in_=ffv[2][:]
                              ).then_inc(sems["cG"], 16)
                return None
            S.step("gpsimd", st_out, wait=("cV",), inc="cG", dma_n=1)

    barrier()
    pop_scope()

    scratch_map = dict(xn_dbg=xn_dbg, stats_dbg=stats_dbg, qpx_d=qpx_d, qpy_d=qpy_d,
                       qT_d=qT_d, kT_d=kT_d, V_d=V_d, ffn0_d=ffn0_d, x1_d=x1_d,
                       xn1_d=xn1_d, xn1T_d=xn1T_d, xdT_d=xdT_d, u_d=u_d,
                       dl_d=dl_d, wsc_d=wsc_d, sz_d=sz_d, hm_d=hm_d,
                       ffn1_d=ffn1_d)
    for name, shape, dt in dbg:
        src = scratch_map[name]

        def dmp(eng, src=src, dst=dbg_out[name]):
            eng.dma_start(out=dst, in_=src).then_inc(sems["cG"], 16)
            return None
        S.step("gpsimd", dmp, inc="cG", dma_n=1)
    if dbg:
        barrier()

    with nc.allow_non_contiguous_dma(reason="small one-time bias/const gathers"), \
            nc.Block() as block:
        @block.sync
        def _(eng):
            S.emit(eng, "sync", sems)

        @block.gpsimd
        def _(eng):
            S.emit(eng, "gpsimd", sems)

        @block.vector
        def _(eng):
            S.emit(eng, "vector", sems)

        @block.scalar
        def _(eng):
            S.emit(eng, "scalar", sems)

        @block.tensor
        def _(eng):
            S.emit(eng, "tensor", sems)

    ctx.close()
    return nc


def _shard_inputs(inputs):
    hc = _host_consts()
    x = np.ascontiguousarray(np.asarray(inputs["L0_attn_hidden_states"],
                                        dtype=np.float32))
    common = {}
    for k, v in inputs.items():
        if k == "L0_attn_hidden_states":
            continue
        a = np.ascontiguousarray(np.asarray(v, dtype=np.float32))
        if a.ndim == 1:
            a = a[None, :]
        common[k] = a
    common.update(hc)
    return [dict(common, x_in=np.ascontiguousarray(x[b])) for b in range(NCORES)]


def kernel(**inputs):
    maps = _shard_inputs(inputs)
    nc = build_nc()
    res = run_bass_kernel_spmd(nc, maps, core_ids=list(range(NCORES)))
    return np.stack([np.asarray(r["out"]) for r in res.results], axis=0)


# version 12
# speedup vs baseline: 1.2959x; 1.1707x over previous
import numpy as np
import ml_dtypes
from contextlib import ExitStack

import concourse.bass as bass
import concourse.mybir as mybir
from concourse.bass_utils import run_bass_kernel_spmd

F32 = mybir.dt.float32
BF16 = mybir.dt.bfloat16
AF = mybir.ActivationFunctionType
ALU = mybir.AluOpType
AX = mybir.AxisListType

T = 4096
D = 1024
DI = 2048
DST = 16
DTR = 64
DFF = 4096
NT = T // 128
ND = D // 128
NDI = DI // 128
NDF = DFF // 128
IT = 512
JT = 512
MC = 512
NMC = T // MC
EPS_LN = 1e-5
EPS_RMS = 1e-12
NCORES = 8
CVALS = [EPS_LN, EPS_RMS, 1.0 / D, 1.0 / 32.0, 1.0, -1.0, 0.0, 2.0**-9, 7.5]


class Sched:
    ENGINES = ("sync", "gpsimd", "vector", "scalar", "tensor")

    def __init__(self):
        self.steps = {e: [] for e in self.ENGINES}
        self.counts = {}

    pad_emitters = {}

    def step(self, eng, fn, wait=(), inc=None, inc_n=1, dma_n=0, pad=False,
             wait_at=None):
        waits = tuple((c, (wait_at or {}).get(c, self.counts.get(c, 0)))
                      for c in wait
                      if (wait_at or {}).get(c, self.counts.get(c, 0)) > 0)
        if inc is not None and dma_n:
            self.counts[inc] = self.counts.get(inc, 0) + 16 * dma_n
            self.steps[eng].append((fn, waits, None, 0, False))
        elif inc is not None:
            self.counts[inc] = self.counts.get(inc, 0) + inc_n
            self.steps[eng].append((fn, waits, inc, inc_n, pad))
        else:
            self.steps[eng].append((fn, waits, None, 0, False))

    def emit(self, eng_obj, eng_name, sems):
        last_wait = {}
        for fn, waits, inc, inc_n, pad in self.steps[eng_name]:
            for chn, val in waits:
                if last_wait.get(chn, -1) < val:
                    eng_obj.wait_ge(sems[chn], val)
                    last_wait[chn] = val
            instr = fn(eng_obj)
            if pad and eng_name in self.pad_emitters:
                instr = self.pad_emitters[eng_name](eng_obj)
            if inc is not None:
                assert instr is not None, f"step on {eng_name} returned None"
                instr.then_inc(sems[inc], inc_n)


def _host_consts():
    identf = np.eye(128, dtype=np.float32)
    identb = np.eye(128).astype(ml_dtypes.bfloat16)
    onesf = np.ones((1, 128), dtype=np.float32)
    tril = np.zeros((4, 128, JT), dtype=np.float32)
    jj = np.arange(JT)[None, :]
    pp = np.arange(128)[:, None]
    for k in range(4):
        tril[k] = (jj <= 128 * k + pp).astype(np.float32)
    probe = np.array([[2.0**-30, 1.0, 2.0**30, 2.0**49, 2.0**20,
                       1.5 * 2.0**52, 1.5 * 2.0**23,
                       float(np.sqrt(2.0) * 2.0**-27), float(2.0**-12),
                       0.0, float(T * T), float(T * T - 1), 1.25,
                       0.0, 0.0, 0.0]], dtype=np.float32)
    cvals = np.broadcast_to(np.array(CVALS, dtype=np.float32)[None, :],
                            (128, len(CVALS))).copy()
    return dict(identf_h=identf, identb_h=identb, onesf_h=onesf,
                tril_h=np.ascontiguousarray(tril.reshape(4 * 128, JT)),
                probe_h=probe, cvals_h=cvals)


def build_nc(dbg=()):
    nc = bass.Bass("TRN2", target_bir_lowering=False,
                   detect_race_conditions=False)
    S = Sched()
    ctx = ExitStack()

    def din(name, shape, dt=F32):
        return nc.dram_tensor(name, list(shape), dt, kind="ExternalInput").ap()

    x_in = din("x_in", (T, D))
    Wq = din("L0_attn_Wq", (D, D)); bq = din("L0_attn_bq", (1, D))
    Wk = din("L0_attn_Wk", (D, D)); bk = din("L0_attn_bk", (1, D))
    Wv = din("L0_attn_Wv", (D, D)); bv = din("L0_attn_bv", (1, D))
    Wo = din("L0_attn_Wo", (D, D)); bo = din("L0_attn_bo", (1, D))
    Wtq = din("L0_attn_Wtq", (2, D)); btq = din("L0_attn_btq", (1, 2))
    Wfq = din("L0_attn_Wfq", (D, 2)); bfq = din("L0_attn_bfq", (1, D))
    w8 = din("L0_ffn_weight_fp8", (D, D)); b8 = din("L0_ffn_bias", (1, D))
    s8in = din("L0_ffn_input_scale", (1, 1)); s8w = din("L0_ffn_weight_scale", (1, 1))
    inpw = din("L1_attn_in_proj_w", (2 * DI, D))
    convw = din("L1_attn_conv_w", (DI, 4)); convb = din("L1_attn_conv_b", (1, DI))
    xprw = din("L1_attn_x_proj_w", (DTR + 2 * DST, DI))
    dtw = din("L1_attn_dt_proj_w", (DI, DTR)); dtb = din("L1_attn_dt_proj_b", (1, DI))
    alog = din("L1_attn_A_log", (DI, DST)); dD = din("L1_attn_D", (1, DI))
    outw = din("L1_attn_out_proj_w", (D, DI))
    nw = din("L1_attn_norm_w", (1, D))
    w1 = din("L1_attn_ffn_w1", (DFF, D)); b1 = din("L1_attn_ffn_b1", (1, DFF))
    w2 = din("L1_attn_ffn_w2", (D, DFF)); b2 = din("L1_attn_ffn_b2", (1, D))
    fnw = din("L1_attn_ffn_norm_w", (1, D))
    wL1 = din("L1_ffn_weight", (D, D)); bL1 = din("L1_ffn_bias", (1, D))
    identf_h = din("identf_h", (128, 128))
    identb_h = din("identb_h", (128, 128), BF16)
    cvals_h = din("cvals_h", (128, len(CVALS)))
    onesf_h = din("onesf_h", (1, 128))
    tril_h = din("tril_h", (4 * 128, JT))
    probe_h = din("probe_h", (1, 16))

    out = nc.dram_tensor("out", [T, D], F32, kind="ExternalOutput").ap()
    dbg_out = {}
    for name, shape, dt in dbg:
        dbg_out[name] = nc.dram_tensor("dbg_" + name, list(shape), dt,
                                       kind="ExternalOutput").ap()

    def dsc(name, shape, dt):
        return nc.dram_tensor(name, list(shape), dt).ap()

    qT_d = dsc("qT_d", (D, T), BF16)
    kT_d = dsc("kT_d", (D, T), BF16)
    V_d = dsc("V_d", (T, D), BF16)
    ffn0_d = dsc("ffn0_d", (T, D), F32)
    x1_d = dsc("x1_d", (T, D), F32)
    xn1_d = dsc("xn1_d", (T, D), F32)
    xn1T_d = dsc("xn1T_d", (D, T), BF16)
    xdT_d = dsc("xdT_d", (D, T), BF16)
    u_d = dsc("u_d", (DI, T), BF16)
    dl_d = dsc("dl_d", (DI, T), BF16)
    wsc_d = dsc("wsc_d", (DI, T), BF16)
    sz_d = dsc("sz_d", (DI, T), BF16)
    hm_d = dsc("hm_d", (D, T), BF16)
    ffn1_d = dsc("ffn1_d", (T, D), F32)
    xn_dbg = dsc("xn_dbg", (T, D), F32)
    stats_dbg = dsc("stats_dbg", (128, 96), F32)
    qpx_d = dsc("qpx_d", (1, T), F32)
    qpy_d = dsc("qpy_d", (1, T), F32)
    bm_d = dsc("bm_d", (DST, T), F32)
    cm_d = dsc("cm_d", (DST, T), F32)
    pT_d = dsc("pT_d", (T, IT), BF16)

    scopes = [ctx]

    def sb(name, shape, dt=F32):
        return scopes[-1].enter_context(nc.sbuf_tensor(name, list(shape), dt))

    def push_scope():
        scopes.append(ExitStack())

    def pop_scope():
        scopes.pop().close()

    def barrier():
        for e in Sched.ENGINES:
            S.step(e, lambda eng: None, wait=("cL", "cG", "cT", "cV", "cS"))

    def psum(name, shape, dt=F32):
        return ctx.enter_context(nc.psum_tensor(name, list(shape), dt))

    sems = {}
    for chn in ("cL", "cG", "cT", "cV", "cS"):
        sems[chn] = ctx.enter_context(nc.semaphore(chn))

    identf = sb("identf", (128, 128))
    identb = sb("identb", (128, 128), BF16)
    onesf = sb("onesf", (1, 128))
    probe_sb = sb("probe_sb", (1, 16))
    CST = sb("CST", (128, 16))
    W_ = sb("W_", (1, 32))
    stat_r = sb("stat_r", (128, NT))
    stat_mb = sb("stat_mb", (128, NT))
    stat_tmp = sb("stat_tmp", (128, NT))
    stat_nm = sb("stat_nm", (128, NT))
    stat_ve = sb("stat_ve", (128, NT))
    stat_s = sb("stat_s", (128, NT))
    qpN = sb("qpN", (128, NT * 2))
    rembN = sb("rembN", (128, NT * 2))
    bqc = sb("bqc", (128, ND)); bkc = sb("bkc", (128, ND))
    btq_sb = sb("btq_sb", (2, 1))
    b1c = sb("b1c", (128, NDF))
    dtbc = sb("dtbc", (128, NDI))
    convbc = sb("convbc", (128, NDI))
    convwc = sb("convwc", (128, NDI * 4))
    Dc = sb("Dc", (128, NDI))
    Ac = sb("Ac", (128, NDI * DST))
    scan_st = sb("scan_st", (128, NDI * DST))
    wld_buf = sb("wld_buf", (128, 4096))
    brow = sb("brow", (1, D))
    cvals_sb = sb("cvals_sb", (128, len(CVALS)))
    vdump = sb("vdump", (128, 128))
    sdump = sb("sdump", (128, 128))
    S.pad_emitters = {
        "vector": lambda eng: eng.tensor_scalar_add(vdump[:], identf[:], 0.0),
        "scalar": lambda eng: eng.activation(out=sdump[:], in_=identf[:],
                                             func=AF.Identity),
    }

    def vgap(eng):
        return eng.tensor_scalar_add(vdump[:], identf[:], 0.0)
    for _i, _v in enumerate(CVALS):
        nc.const_aps.register(F32, float(_v), cvals_sb[:, _i:_i + 1])
    PS = [psum(f"ps{i}", (128, 512)) for i in range(6)]
    PSB = psum("psb", (128, 1024), BF16)
    PS7 = psum("ps7", (128, 512))

    def mk_bcast(src_ap, dst_sb, scl):
        def ldr(eng, src=src_ap):
            eng.dma_start(out=brow[:], in_=src).then_inc(sems["cL"], 16)
            return None
        S.step("sync", ldr, wait=("cT",), inc="cL", dma_n=1)

        def bc_t(eng):
            eng.matmul(PS[1][:, :], onesf[:], brow[:, 0:512], start=True, stop=True)
            return eng.matmul(PS[2][:, :], onesf[:], brow[:, 512:1024],
                              start=True, stop=True)
        S.step("tensor", bc_t, wait=("cL", "cV"), inc="cT")

        def bc_v(eng, dst=dst_sb, scl=scl):
            eng.tensor_scalar_mul(dst[:, 0:512], PS[1][:, :], scl)
            return eng.tensor_scalar_mul(dst[:, 512:1024], PS[2][:, :], scl)
        S.step("vector", bc_v, wait=("cT",), inc="cV")

    def ldc(eng):
        eng.dma_start(out=identf[:], in_=identf_h).then_inc(sems["cL"], 16)
        eng.dma_start(out=identb[:], in_=identb_h).then_inc(sems["cL"], 16)
        eng.dma_start(out=onesf[:], in_=onesf_h).then_inc(sems["cL"], 16)
        eng.dma_start(out=probe_sb[:], in_=probe_h).then_inc(sems["cL"], 16)
        eng.dma_start(out=cvals_sb[:], in_=cvals_h).then_inc(sems["cL"], 16)
        eng.dma_start(out=btq_sb[:], in_=btq.rearrange("o t -> t o")
                      ).then_inc(sems["cL"], 16)
        eng.dma_start(out=W_[:, 20:21], in_=s8in).then_inc(sems["cL"], 16)
        eng.dma_start(out=W_[:, 21:22], in_=s8w).then_inc(sems["cL"], 16)
        eng.dma_start(out=bqc[:], in_=bq.rearrange("o (j p) -> p (o j)", p=128)
                      ).then_inc(sems["cL"], 16)
        eng.dma_start(out=bkc[:], in_=bk.rearrange("o (j p) -> p (o j)", p=128)
                      ).then_inc(sems["cL"], 16)
        eng.dma_start(out=b1c[:], in_=b1.rearrange("o (j p) -> p (o j)", p=128)
                      ).then_inc(sems["cL"], 16)
        eng.dma_start(out=dtbc[:], in_=dtb.rearrange("o (j p) -> p (o j)", p=128)
                      ).then_inc(sems["cL"], 16)
        eng.dma_start(out=convbc[:], in_=convb.rearrange("o (j p) -> p (o j)", p=128)
                      ).then_inc(sems["cL"], 16)
        eng.dma_start(out=Dc[:], in_=dD.rearrange("o (j p) -> p (o j)", p=128)
                      ).then_inc(sems["cL"], 16)
        eng.dma_start(out=convwc.rearrange("p (j k) -> p j k", k=4),
                      in_=convw.rearrange("(j p) k -> p j k", p=128)
                      ).then_inc(sems["cL"], 16)
        eng.dma_start(out=Ac.rearrange("p (j n) -> p j n", n=DST),
                      in_=alog.rearrange("(j p) n -> p j n", p=128)
                      ).then_inc(sems["cL"], 16)
        return None
    S.step("sync", ldc, inc="cL", dma_n=16)

    def consts_v(eng):
        pc = probe_sb
        nop8 = lambda: vgap(eng)
        eng.tensor_add(W_[:, 0:1], pc[:, 0:1], pc[:, 1:2]); nop8()
        eng.tensor_sub(W_[:, 1:2], W_[:, 0:1], pc[:, 1:2]); nop8()
        eng.tensor_mul(W_[:, 2:3], W_[:, 1:2], pc[:, 2:3]); nop8()
        eng.tensor_sub(W_[:, 3:4], pc[:, 1:2], W_[:, 2:3]); nop8()
        eng.tensor_mul(W_[:, 4:5], W_[:, 2:3], pc[:, 3:4])
        eng.tensor_mul(W_[:, 5:6], W_[:, 3:4], pc[:, 4:5]); nop8()
        eng.tensor_add(W_[:, 6:7], W_[:, 4:5], W_[:, 5:6]); nop8()
        eng.tensor_add(W_[:, 7:8], W_[:, 6:7], pc[:, 1:2])
        eng.tensor_mul(W_[:, 8:9], W_[:, 2:3], pc[:, 5:6])
        eng.tensor_mul(W_[:, 9:10], W_[:, 3:4], pc[:, 6:7]); nop8()
        eng.tensor_add(W_[:, 10:11], W_[:, 8:9], W_[:, 9:10]); nop8()
        eng.tensor_sub(W_[:, 11:12], pc[:, 9:10], W_[:, 10:11])
        eng.tensor_mul(W_[:, 12:13], W_[:, 2:3], pc[:, 7:8])
        eng.tensor_mul(W_[:, 13:14], W_[:, 3:4], pc[:, 8:9]); nop8()
        eng.tensor_add(W_[:, 14:15], W_[:, 12:13], W_[:, 13:14]); nop8()
        eng.reciprocal(W_[:, 22:23], W_[:, 20:21]); nop8()
        return eng.tensor_mul(W_[:, 23:24], W_[:, 20:21], W_[:, 21:22])
    S.step("vector", consts_v, wait=("cL",), inc="cV", pad=True)

    cst_srcs = [(W_[:, 7:8], 0), (W_[:, 10:11], 1), (W_[:, 11:12], 2),
                (W_[:, 14:15], 3), (W_[:, 22:23], 4), (W_[:, 23:24], 5)]

    def cst_bc_t(eng):
        last = None
        for i, (src, col) in enumerate(cst_srcs):
            last = eng.matmul(PS[0][:, i:i + 1], onesf[:], src,
                              start=True, stop=True)
        return last
    S.step("tensor", cst_bc_t, wait=("cV",), inc="cT")

    def cst_bc_v(eng):
        last = None
        for i, (src, col) in enumerate(cst_srcs):
            last = eng.tensor_copy(CST[:, col:col + 1], PS[0][:, i:i + 1])
        return last
    S.step("vector", cst_bc_v, wait=("cT",), inc="cV", pad=True)

    def r75_v(eng):
        return eng.reciprocal(CST[:, 7:8], cvals_sb[:, 8:9])
    S.step("vector", r75_v, wait=("cL",), inc="cV", pad=True)

    atmp = sb("atmp", (128, NDI * DST))

    def a_exp(eng):
        return eng.activation(out=atmp[:], in_=Ac[:], func=AF.Exp)
    S.step("scalar", a_exp, wait=("cL",), inc="cS")

    def a_neg(eng):
        return eng.tensor_scalar_mul(Ac[:], atmp[:], -1.0)
    S.step("vector", a_neg, wait=("cS",), inc="cV")

    def st_zero(eng):
        return eng.memset(scan_st[:], 0.0)
    S.step("vector", st_zero, inc="cV")

    def weight_T(wap, R, C, dst_sb):
        nr, ncl = R // 128, C // 128
        for rt in range(nr):
            def ldw(eng, rt=rt, wap=wap, C=C):
                eng.dma_start(out=wld_buf[:, 0:C],
                              in_=wap[rt * 128:(rt + 1) * 128, :]
                              ).then_inc(sems["cL"], 16)
                return None
            S.step("sync", ldw, wait=("cV",), inc="cL", dma_n=1)
            for cg in range(0, ncl, 4):
                ng = min(4, ncl - cg)

                def trn_g(eng, cg=cg, ng=ng):
                    last = None
                    for i in range(ng):
                        last = eng.transpose(PS7[:, i * 128:(i + 1) * 128],
                                             wld_buf[:, (cg + i) * 128:(cg + i + 1) * 128],
                                             identf[:])
                    return last
                S.step("tensor", trn_g, wait=("cL", "cV"), inc="cT")

                def ev_g(eng, cg=cg, ng=ng, rt=rt, dst_sb=dst_sb, R=R):
                    last = None
                    for i in range(ng):
                        c = cg + i
                        last = eng.tensor_copy(
                            dst_sb[:, c * R + rt * 128:c * R + (rt + 1) * 128],
                            PS7[:, i * 128:(i + 1) * 128])
                    return last
                S.step("vector", ev_g, wait=("cT",), inc="cV")

    push_scope()
    bvc_bc = sb("bvc_bc", (128, D))
    b8c_bc = sb("b8c_bc", (128, D))
    mk_bcast(bv, bvc_bc, 1.0)
    mk_bcast(b8, b8c_bc, 1.0)
    WqTs = sb("WqTs", (128, D * ND), BF16)
    WkTs = sb("WkTs", (128, D * ND), BF16)
    WvTs = sb("WvTs", (128, D * ND), BF16)
    w8Ts = sb("w8Ts", (128, D * ND), BF16)
    WtqT = sb("WtqT", (128, ND * 2))

    weight_T(Wq, D, D, WqTs)
    weight_T(Wk, D, D, WkTs)
    weight_T(Wv, D, D, WvTs)
    weight_T(w8, D, D, w8Ts)

    def ldwtq(eng):
        for c in range(2):
            eng.dma_start(out=WtqT.rearrange("p (j c) -> p j c", c=2)[:, :, c],
                          in_=Wtq.rearrange("c (j p) -> c p j", p=128)[c]
                          ).then_inc(sems["cL"], 16)
        return None
    S.step("sync", ldwtq, inc="cL", dma_n=2)

    x_buf = [sb(f"x_buf{i}", (128, D)) for i in range(2)]
    ln_sc = [sb(f"ln_sc{i}", (128, D)) for i in range(3)]
    q8_sc = [sb(f"q8_sc{i}", (128, D)) for i in range(5)]
    aug = sb("aug", (128, 4))
    aug_sq = sb("aug_sq", (128, 2))

    def aug_ones(eng):
        return eng.memset(aug[:, 3:4], 1.0)
    S.step("vector", aug_ones, inc="cV")
    qp2 = sb("qp2", (2, 512))
    xnT_c = sb("xnT_c", (128, ND * 512), BF16)
    xnTf_c = sb("xnTf_c", (128, ND * 512))
    xdqT_c = sb("xdqT_c", (128, ND * 512), BF16)
    proj_ev = [sb(f"proj_ev{i}", (128, 512), BF16) for i in range(4)]
    vfe_ev = [sb(f"vfe_ev{i}", (128, D)) for i in range(2)]
    vfe_bf = [sb(f"vfe_bf{i}", (128, D), BF16) for i in range(2)]

    lnb_mark = {}

    def p1_tile(tt):
        xb = x_buf[tt % 2]

        def ldx(eng, tt=tt, xb=xb):
            eng.dma_start(out=xb[:], in_=x_in[tt * 128:(tt + 1) * 128, :]
                          ).then_inc(sems["cL"], 16)
            return None
        S.step("sync", ldx, wait=("cS",), inc="cL", dma_n=1,
               wait_at=dict(cS=lnb_mark.get(tt - 2, 0)))

        def red(eng, tt=tt, xb=xb):
            eng.reduce_sum(stat_tmp[:, tt:tt + 1], xb[:], axis=AX.X,
                           negate=True)
            vgap(eng)
            return eng.tensor_scalar_mul(stat_nm[:, tt:tt + 1],
                                         stat_tmp[:, tt:tt + 1], 1.0 / D)
        S.step("vector", red, wait=("cL",), inc="cV", pad=True)

        def ln_a(eng, tt=tt, xb=xb):
            eng.activation(out=ln_sc[0][:], in_=xb[:], func=AF.Identity,
                           bias=stat_nm[:, tt:tt + 1])
            return eng.activation(out=ln_sc[1][:], in_=ln_sc[0][:],
                                  func=AF.Square,
                                  accum_out=stat_tmp[:, tt:tt + 1])
        S.step("scalar", ln_a, wait=("cV",), inc="cS", pad=True)

        def ln_v2(eng, tt=tt):
            return eng.tensor_scalar(out=stat_ve[:, tt:tt + 1],
                                     in0=stat_tmp[:, tt:tt + 1],
                                     scalar1=1.0 / D, scalar2=EPS_LN,
                                     op0=ALU.mult, op1=ALU.add)
        S.step("vector", ln_v2, wait=("cS",), inc="cV", pad=True)

        def ln_s2(eng, tt=tt):
            return eng.activation(out=stat_s[:, tt:tt + 1],
                                  in_=stat_ve[:, tt:tt + 1], func=AF.Sqrt)
        S.step("scalar", ln_s2, wait=("cV",), inc="cS", pad=True)

        def ln_v3(eng, tt=tt):
            eng.reciprocal(stat_r[:, tt:tt + 1], stat_s[:, tt:tt + 1])
            vgap(eng)
            eng.tensor_mul(stat_ve[:, tt:tt + 1], stat_nm[:, tt:tt + 1],
                           stat_r[:, tt:tt + 1])
            vgap(eng)
            return eng.tensor_scalar_mul(stat_mb[:, tt:tt + 1],
                                         stat_ve[:, tt:tt + 1], 1.0)
        S.step("vector", ln_v3, wait=("cS",), inc="cV", pad=True)

        def ln_b(eng, tt=tt, xb=xb):
            return eng.activation(out=ln_sc[2][:], in_=xb[:], func=AF.Identity,
                                  scale=stat_r[:, tt:tt + 1],
                                  bias=stat_mb[:, tt:tt + 1])
        S.step("scalar", ln_b, wait=("cV", "cG"), inc="cS")
        lnb_mark[tt] = S.counts.get("cS", 0)

        xn = ln_sc[2]

        if dbg:
            def st_xn_dbg(eng, tt=tt, xn=xn):
                eng.dma_start(out=xn_dbg[tt * 128:(tt + 1) * 128, :], in_=xn[:]
                              ).then_inc(sems["cG"], 16)
                return None
            S.step("gpsimd", st_xn_dbg, wait=("cS",), inc="cG", dma_n=1)

        def q8(eng, xn=xn):
            Cv, M0 = CST[:, 0:1], CST[:, 1:2]
            rsin = CST[:, 4:5]
            w1_, w2_, w3_, w4_, w5_ = q8_sc
            eng.tensor_scalar(out=w1_[:], in0=xn[:], scalar1=rsin, scalar2=448.0,
                              op0=ALU.mult, op1=ALU.min)
            eng.tensor_scalar(out=w5_[:], in0=w1_[:], scalar1=-448.0, scalar2=None,
                              op0=ALU.max)
            eng.tensor_scalar(out=w2_[:], in0=w5_[:], scalar1=Cv, scalar2=None,
                              op0=ALU.mult)
            eng.tensor_sub(w3_[:], w2_[:], w5_[:])
            eng.tensor_sub(w4_[:], w2_[:], w3_[:])
            eng.tensor_scalar(out=w2_[:], in0=w5_[:], scalar1=512.0, scalar2=None,
                              op0=ALU.mult)
            eng.tensor_scalar(out=w3_[:], in0=w2_[:], scalar1=M0, scalar2=None,
                              op0=ALU.add)
            eng.tensor_scalar(out=w2_[:], in0=w3_[:], scalar1=M0,
                              scalar2=float(2.0**-9),
                              op0=ALU.subtract, op1=ALU.mult)
            eng.tensor_scalar(out=w1_[:], in0=w5_[:], scalar1=-1.0, scalar2=None,
                              op0=ALU.mult)
            eng.tensor_max(w3_[:], w1_[:], w5_[:])
            eng.tensor_single_scalar(out=w1_[:], in_=w3_[:], scalar=float(2.0**-6),
                                     op=ALU.is_lt)
            eng.tensor_sub(w3_[:], w2_[:], w4_[:])
            eng.tensor_mul(w2_[:], w3_[:], w1_[:])
            return eng.tensor_add(w5_[:], w2_[:], w4_[:])
        S.step("vector", q8, wait=("cS",), inc="cV")
        q8v = q8_sc[4]

        ci = tt % 4
        for d0 in range(0, ND, 2):
            def tr2(eng, d0=d0, xn=xn):
                eng.transpose(PS[4][:, 0:128], xn[:, d0 * 128:(d0 + 1) * 128],
                              identf[:])
                return eng.transpose(PS[5][:, 0:128],
                                     xn[:, (d0 + 1) * 128:(d0 + 2) * 128],
                                     identf[:])
            S.step("tensor", tr2, wait=("cV", "cS"), inc="cT")

            def ev2(eng, d0=d0, ci=ci):
                c0 = d0 * 512 + ci * 128
                c1 = (d0 + 1) * 512 + ci * 128
                eng.tensor_copy(xnT_c[:, c0:c0 + 128], PS[4][:, 0:128])
                eng.tensor_copy(xnTf_c[:, c0:c0 + 128], PS[4][:, 0:128])
                eng.tensor_copy(xnT_c[:, c1:c1 + 128], PS[5][:, 0:128])
                return eng.tensor_copy(xnTf_c[:, c1:c1 + 128], PS[5][:, 0:128])
            S.step("vector", ev2, wait=("cT",), inc="cV")

        for d0 in range(0, ND, 2):
            def tr2q(eng, d0=d0, q8v=q8v):
                eng.transpose(PS[4][:, 0:128], q8v[:, d0 * 128:(d0 + 1) * 128],
                              identf[:])
                return eng.transpose(PS[5][:, 0:128],
                                     q8v[:, (d0 + 1) * 128:(d0 + 2) * 128],
                                     identf[:])
            S.step("tensor", tr2q, wait=("cV",), inc="cT")

            def ev2q(eng, d0=d0, ci=ci):
                c0 = d0 * 512 + ci * 128
                c1 = (d0 + 1) * 512 + ci * 128
                eng.tensor_copy(xdqT_c[:, c0:c0 + 128], PS[4][:, 0:128])
                return eng.tensor_copy(xdqT_c[:, c1:c1 + 128], PS[5][:, 0:128])
            S.step("vector", ev2q, wait=("cT",), inc="cV")

        def mkaug(eng, tt=tt):
            eng.memset(aug[:, 3:4], 1.0)
            return eng.tensor_copy(aug[:, 0:2], qpN_hold[tt % 2][:, 0:2]) \
                if False else eng.tensor_copy(aug[:, 0:2], ln_sc[0][:, 0:2])

    qpN_dummy = None

    def p1_chunk_proj(cc):
        ch0 = cc * 512

        def qp_mm(eng):
            last = None
            for d in range(ND):
                last = eng.matmul(PS[0][0:2, 0:512],
                                  WtqT[:, d * 2:(d + 1) * 2],
                                  xnTf_c[:, d * 512:(d + 1) * 512],
                                  start=(d == 0), stop=(d == ND - 1))
            return last
        S.step("tensor", qp_mm, wait=("cV", "cS"), inc="cT")

        def qp_ev(eng, ch0=ch0):
            return eng.activation(out=qp2[:], in_=PS[0][0:2, 0:512],
                                  func=AF.Identity, bias=btq_sb[:])
        S.step("scalar", qp_ev, wait=("cT", "cG"), inc="cS")

        def qp_rows(eng, ch0=ch0):
            eng.dma_start(out=qpx_d[:, ch0:ch0 + 512], in_=qp2[0:1, :]
                          ).then_inc(sems["cG"], 16)
            eng.dma_start(out=qpy_d[:, ch0:ch0 + 512], in_=qp2[1:2, :]
                          ).then_inc(sems["cG"], 16)
            return None
        S.step("gpsimd", qp_rows, wait=("cS",), inc="cG", dma_n=2)

        for si in range(4):
            tt = cc * 4 + si

            def qp_tr(eng, si=si):
                return eng.transpose(PS[4][0:128, 0:2],
                                     qp2[:, si * 128:(si + 1) * 128],
                                     identf[0:2, 0:2])
            S.step("tensor", qp_tr, wait=("cS", "cV"), inc="cT")

            def qp_trev(eng, tt=tt):
                eng.tensor_copy(qpN[:, tt * 2:tt * 2 + 2], PS[4][0:128, 0:2])
                return eng.tensor_copy(aug[:, 0:2], PS[4][0:128, 0:2])
            S.step("vector", qp_trev, wait=("cT",), inc="cV", pad=True)

            def sqa(eng, tt=tt):
                return eng.activation(out=aug_sq[:], in_=qpN[:, tt * 2:tt * 2 + 2],
                                      func=AF.Square)
            S.step("scalar", sqa, wait=("cV",), inc="cS", pad=True)

            def reda(eng):
                return eng.reduce_sum(aug[:, 2:3], aug_sq[:], axis=AX.X)
            S.step("vector", reda, wait=("cS",), inc="cV", pad=True)

            def gmm(eng, tt=tt):
                return eng.matmul(PS7[0:4, 0:4], aug[:], aug[:],
                                  start=(tt == 0), stop=(tt == NT - 1),
                                  skip_group_check=True)
            S.step("tensor", gmm, wait=("cV",), inc="cT")

        kq_base = dict(cV=S.counts.get("cV", 0), cS=S.counts.get("cS", 0),
                       cG=S.counts.get("cG", 0))
        ev_marks = {}
        st_marks = {}
        for ni, (name, Wsl, bcol, dst) in enumerate(
                (("k", WkTs, bkc, kT_d), ("q", WqTs, bqc, qT_d))):
            for m in range(ND):
                idx = ni * ND + m
                bank = PS[0] if m % 2 == 0 else PS[1]

                def mm(eng, Wsl=Wsl, m=m, bank=bank):
                    last = None
                    for c in range(ND):
                        last = eng.matmul(bank[:, :],
                                          Wsl[:, c * D + m * 128:c * D + (m + 1) * 128],
                                          xnT_c[:, c * 512:(c + 1) * 512],
                                          start=(c == 0), stop=(c == ND - 1))
                    return last
                S.step("tensor", mm, wait=("cV", "cS", "cG"), inc="cT",
                       wait_at=dict(kq_base,
                                    **(dict(cS=ev_marks[idx - 2])
                                       if idx >= 2 else {})))

                def ev(eng, m=m, bank=bank, bcol=bcol):
                    return eng.activation(out=proj_ev[m % 4][:], in_=bank[:, :],
                                          func=AF.Identity, bias=bcol[:, m:m + 1])
                S.step("scalar", ev, wait=("cT", "cG"), inc="cS",
                       wait_at=(dict(cG=st_marks[idx - 4]) if idx >= 4
                                else dict(cG=kq_base["cG"])))
                ev_marks[idx] = S.counts.get("cS", 0)

                def st(eng, m=m, dst=dst, ch0=ch0):
                    eng.dma_start(out=dst[m * 128:(m + 1) * 128, ch0:ch0 + 512],
                                  in_=proj_ev[m % 4][:]).then_inc(sems["cG"], 16)
                    return None
                S.step("gpsimd", st, wait=("cS",), inc="cG", dma_n=1)
                st_marks[idx] = S.counts.get("cG", 0)

        for tsub in range(4):
            tt = cc * 4 + tsub
            for half in range(2):
                bank = PS[2] if half == 0 else PS[3]

                def mmv(eng, tsub=tsub, half=half, bank=bank):
                    last = None
                    for c in range(ND):
                        last = eng.matmul(
                            bank[:, :],
                            xnT_c[:, c * 512 + tsub * 128:c * 512 + (tsub + 1) * 128],
                            WvTs[:, c * D + half * 512:c * D + (half + 1) * 512],
                            start=(c == 0), stop=(c == ND - 1))
                    return last
                S.step("tensor", mmv, wait=("cV", "cG"), inc="cT")

            def evv(eng, tsub=tsub):
                eng.tensor_add(vfe_ev[0][:, 0:512], PS[2][:, :], bvc_bc[:, 0:512])
                eng.tensor_add(vfe_ev[0][:, 512:1024], PS[3][:, :], bvc_bc[:, 512:1024])
                return eng.tensor_copy(vfe_bf[0][:], vfe_ev[0][:])
            S.step("vector", evv, wait=("cT", "cG"), inc="cV")

            def stv(eng, tt=tt):
                eng.dma_start(out=V_d[tt * 128:(tt + 1) * 128, :], in_=vfe_bf[0][:]
                              ).then_inc(sems["cG"], 16)
                return None
            S.step("gpsimd", stv, wait=("cV",), inc="cG", dma_n=1)

            for half in range(2):
                bank = PS[2] if half == 0 else PS[3]

                def mmf(eng, tsub=tsub, half=half, bank=bank):
                    last = None
                    for c in range(ND):
                        last = eng.matmul(
                            bank[:, :],
                            xdqT_c[:, c * 512 + tsub * 128:c * 512 + (tsub + 1) * 128],
                            w8Ts[:, c * D + half * 512:c * D + (half + 1) * 512],
                            start=(c == 0), stop=(c == ND - 1))
                    return last
                S.step("tensor", mmf, wait=("cV", "cG"), inc="cT")

            def evf(eng, tsub=tsub):
                s8p = CST[:, 5:6]
                eng.tensor_scalar(out=vfe_ev[0][:, 0:512], in0=PS[2][:, :],
                                  scalar1=s8p, scalar2=None, op0=ALU.mult)
                eng.tensor_scalar(out=vfe_ev[0][:, 512:1024], in0=PS[3][:, :],
                                  scalar1=s8p, scalar2=None, op0=ALU.mult)
                return eng.tensor_add(vfe_ev[1][:], vfe_ev[0][:], b8c_bc[:])
            S.step("vector", evf, wait=("cT", "cG"), inc="cV")

            def stf(eng, tt=tt):
                eng.dma_start(out=ffn0_d[tt * 128:(tt + 1) * 128, :], in_=vfe_ev[1][:]
                              ).then_inc(sems["cG"], 16)
                return None
            S.step("gpsimd", stf, wait=("cV",), inc="cG", dma_n=1)

    for cc in range(NMC):
        for si in range(4):
            p1_tile(cc * 4 + si)
        p1_chunk_proj(cc)

    G_sb = sb("G_sb", (4, 4))
    G9 = sb("G9", (1, 16))

    def g_ev(eng):
        return eng.tensor_copy(G_sb[:], PS7[0:4, 0:4])
    S.step("vector", g_ev, wait=("cT",), inc="cV", pad=True)

    def g_gather(eng):
        for i in range(4):
            eng.dma_start(out=G9[0:1, 4 * i:4 * i + 4], in_=G_sb[i:i + 1, :]
                          ).then_inc(sems["cG"], 16)
        return None
    S.step("gpsimd", g_gather, wait=("cV",), inc="cG", dma_n=4)

    def thr_v(eng):
        g = lambda i: G9[:, i:i + 1]
        nop8 = lambda: vgap(eng)
        pc = probe_sb
        eng.tensor_mul(W_[:, 0:1], g(14), g(15))
        eng.tensor_mul(W_[:, 1:2], g(3), g(3))
        eng.tensor_mul(W_[:, 2:3], g(7), g(7)); nop8()
        eng.tensor_scalar_mul(W_[:, 3:4], W_[:, 0:1], 2.0)
        eng.tensor_add(W_[:, 4:5], W_[:, 1:2], W_[:, 2:3]); nop8()
        eng.tensor_scalar_mul(W_[:, 5:6], W_[:, 4:5], 2.0); nop8()
        eng.tensor_sub(W_[:, 6:7], W_[:, 3:4], W_[:, 5:6]); nop8()
        eng.tensor_scalar_mul(W_[:, 7:8], W_[:, 6:7], float(2.0**-24))
        eng.tensor_mul(W_[:, 8:9], g(15), g(10))
        eng.tensor_mul(W_[:, 9:10], g(14), g(14))
        eng.tensor_mul(W_[:, 10:11], g(2), g(3))
        eng.tensor_mul(W_[:, 11:12], g(6), g(7))
        eng.tensor_mul(W_[:, 12:13], g(0), g(0))
        eng.tensor_mul(W_[:, 13:14], g(1), g(1))
        eng.tensor_mul(W_[:, 14:15], g(5), g(5)); nop8()
        eng.tensor_scalar_mul(W_[:, 15:16], W_[:, 8:9], 2.0)
        eng.tensor_scalar_mul(W_[:, 16:17], W_[:, 9:10], 2.0)
        eng.tensor_add(W_[:, 17:18], W_[:, 10:11], W_[:, 11:12])
        eng.tensor_scalar_mul(W_[:, 18:19], W_[:, 13:14], 2.0); nop8()
        eng.tensor_scalar_mul(W_[:, 19:20], W_[:, 17:18], -8.0)
        eng.tensor_add(W_[:, 24:25], W_[:, 12:13], W_[:, 18:19]); nop8()
        eng.tensor_add(W_[:, 25:26], W_[:, 24:25], W_[:, 14:15]); nop8()
        eng.tensor_scalar_mul(W_[:, 26:27], W_[:, 25:26], 4.0)
        eng.tensor_add(W_[:, 27:28], W_[:, 15:16], W_[:, 16:17]); nop8()
        eng.tensor_add(W_[:, 28:29], W_[:, 27:28], W_[:, 19:20]); nop8()
        eng.tensor_add(W_[:, 29:30], W_[:, 28:29], W_[:, 26:27]); nop8()
        eng.tensor_mul(W_[:, 24:25], W_[:, 6:7], W_[:, 6:7]); nop8()
        eng.tensor_scalar_mul(W_[:, 25:26], W_[:, 24:25], float(2.0**-24)); nop8()
        eng.tensor_sub(W_[:, 26:27], W_[:, 29:30], W_[:, 25:26])
        eng.reciprocal(W_[:, 27:28], pc[:, 11:12]); nop8()
        return eng.tensor_mul(W_[:, 28:29], W_[:, 26:27], W_[:, 27:28])
    S.step("vector", thr_v, wait=("cG",), inc="cV", pad=True)

    def thr_s(eng):
        return eng.activation(out=W_[:, 29:30], in_=W_[:, 28:29], func=AF.Sqrt)
    S.step("scalar", thr_s, wait=("cV",), inc="cS", pad=True)

    def thr_v2(eng):
        nop8 = lambda: vgap(eng)
        eng.tensor_scalar_mul(W_[:, 30:31], W_[:, 29:30], 1.25); nop8()
        return eng.tensor_add(W_[:, 31:32], W_[:, 30:31], W_[:, 7:8])
    S.step("vector", thr_v2, wait=("cS",), inc="cV", pad=True)

    def thr_bc_t(eng):
        return eng.matmul(PS[0][:, 0:1], onesf[:], W_[:, 31:32],
                          start=True, stop=True)
    S.step("tensor", thr_bc_t, wait=("cV",), inc="cT")

    def thr_bc_v(eng):
        return eng.tensor_copy(CST[:, 6:7], PS[0][:, 0:1])
    S.step("vector", thr_bc_v, wait=("cT",), inc="cV")

    rn_a = sb("rn_a", (128, NT))
    rn_b = sb("rn_b", (128, NT))

    for tt in range(NT):
        def sq_t(eng, tt=tt):
            return eng.activation(out=aug_sq[:], in_=qpN[:, 2 * tt:2 * tt + 2],
                                  func=AF.Square)
        S.step("scalar", sq_t, wait=("cV",), inc="cS", pad=True)

        def red_t(eng, tt=tt):
            return eng.reduce_sum(rn_a[:, tt:tt + 1], aug_sq[:], axis=AX.X)
        S.step("vector", red_t, wait=("cS",), inc="cV", pad=True)

    def rn_sqrt(eng):
        return eng.activation(out=rn_b[:], in_=rn_a[:], func=AF.Sqrt)
    S.step("scalar", rn_sqrt, wait=("cV",), inc="cS", pad=True)

    def rn_max(eng):
        return eng.tensor_scalar_max(rn_a[:], rn_b[:], 1e-12)
    S.step("vector", rn_max, wait=("cS",), inc="cV")

    def rn_rec(eng):
        vgap(eng)
        return eng.reciprocal(rn_b[:], rn_a[:])
    S.step("vector", rn_rec, inc="cV", pad=True)

    for tt in range(NT):
        def remb_mk(eng, tt=tt):
            return eng.tensor_scalar(out=rembN[:, 2 * tt:2 * tt + 2],
                                     in0=qpN[:, 2 * tt:2 * tt + 2],
                                     scalar1=rn_b[:, tt:tt + 1], scalar2=None,
                                     op0=ALU.mult)
        S.step("vector", remb_mk, wait=("cV",), inc="cV", pad=True)

    if dbg:
        def st_stats_dbg(eng):
            eng.dma_start(out=stats_dbg[:, 0:32], in_=stat_r[:]
                          ).then_inc(sems["cG"], 16)
            eng.dma_start(out=stats_dbg[:, 32:64], in_=stat_mb[:]
                          ).then_inc(sems["cG"], 16)
            eng.dma_start(out=stats_dbg[:, 64:96], in_=stat_tmp[:]
                          ).then_inc(sems["cG"], 16)
            return None
        S.step("gpsimd", st_stats_dbg, wait=("cS", "cV"), inc="cG", dma_n=3)

    barrier()
    pop_scope()

    push_scope()
    boc_bc = sb("boc_bc", (128, D))
    bfqc_bc = sb("bfqc_bc", (128, D))
    mk_bcast(bo, boc_bc, 1.0)
    mk_bcast(bfq, bfqc_bc, 0.1)
    tril = sb("tril", (128, 4 * JT), BF16)

    def ldtril(eng):
        eng.dma_start(out=tril.rearrange("p (k j) -> p k j", j=JT),
                      in_=tril_h.rearrange("(k p) j -> p k j", p=128)
                      ).then_inc(sems["cG"], 16)
        return None
    S.step("gpsimd", ldtril, inc="cG", dma_n=1)

    WoTs = sb("WoTs", (128, D * ND), BF16)
    weight_T(Wo, D, D, WoTs)
    WfqT = sb("WfqT", (2, D))

    def ldwfq(eng):
        eng.dma_start(out=WfqT[:], in_=Wfq.rearrange("r c -> c r")
                      ).then_inc(sems["cL"], 16)
        return None
    S.step("sync", ldwfq, inc="cL", dma_n=1)

    QPXR = sb("QPXR", (128, T))
    QPYR = sb("QPYR", (128, T))
    push_scope()
    qprow = sb("qprow", (1, 1024))

    for ch in range(8):
        def ld_qrow(eng, ch=ch):
            eng.dma_start(out=qprow[:, 0:512], in_=qpx_d[:, ch * 512:(ch + 1) * 512]
                          ).then_inc(sems["cL"], 16)
            eng.dma_start(out=qprow[:, 512:1024], in_=qpy_d[:, ch * 512:(ch + 1) * 512]
                          ).then_inc(sems["cL"], 16)
            return None
        S.step("sync", ld_qrow, wait=("cT",), inc="cL", dma_n=2)

        def qpr_t(eng, ch=ch):
            eng.matmul(PS[0][:, :], onesf[:], qprow[:, 0:512],
                       start=True, stop=True)
            return eng.matmul(PS[1][:, :], onesf[:], qprow[:, 512:1024],
                              start=True, stop=True)
        S.step("tensor", qpr_t, wait=("cL", "cV"), inc="cT")

        def qpr_v(eng, ch=ch):
            eng.tensor_copy(QPXR[:, ch * 512:(ch + 1) * 512], PS[0][:, :])
            return eng.tensor_copy(QPYR[:, ch * 512:(ch + 1) * 512], PS[1][:, :])
        S.step("vector", qpr_v, wait=("cT",), inc="cV")

    barrier()
    pop_scope()
    qTi = sb("qTi", (128, ND * 512), BF16)
    kTj = [sb(f"kTj{i}", (128, ND * 512), BF16) for i in range(2)]
    PTev = sb("PTev", (128, 4 * 512), BF16)
    PTrd = [sb(f"PTrd{i}", (128, 512), BF16) for i in range(2)]
    remb2 = sb("remb2", (2, 128))
    ms = [sb(f"ms{i}", (128, 512)) for i in range(6)]
    P0t = sb("P0t", (128, 512), BF16)
    Pmt = sb("Pmt", (128, 512), BF16)
    rs_parts = sb("rs_parts", (128, 32))
    rs4 = sb("rs4", (128, 4))
    rr4 = sb("rr4", (128, 4))
    Vbuf = [sb(f"Vbuf{i}", (128, 512), BF16) for i in range(2)]
    ctxT = sb("ctxT", (128, ND * 512), BF16)
    xq_buf = [sb("xq_buf0", (128, D))] * 2
    fq_buf = [sb("fq_buf0", (128, D))] * 2
    cb = [sb(f"cb{i}", (128, D)) for i in range(4)]
    q6s = [sb(f"q6s{i}", (128, D)) for i in range(5)]
    q6sc = [sb(f"q6sc{i}", (128, 32)) for i in range(4)]
    pev3 = sb("pev3", (128, 256), BF16)
    xdN = sb("xdN", (128, D), BF16)
    x1N = sb("x1N", (128, D))
    xn1N = sb("xn1N", (128, D))
    ln1_sc = [sb("ln1_sc0", (128, D))]
    l1r = sb("l1r", (128, 4))
    l1tmp = sb("l1tmp", (128, 4))
    l1mb = sb("l1mb", (128, 4))

    def flash_itile(it):
        i0 = it * IT

        def ld_q(eng, i0=i0):
            eng.dma_start(out=qTi.rearrange("p (d j) -> p d j", j=512),
                          in_=qT_d.rearrange("(d p) t -> p d t", p=128)[:, :, i0:i0 + 512]
                          ).then_inc(sems["cL"], 16)
            return None
        S.step("sync", ld_q, wait=("cT",), inc="cL", dma_n=1)

        passa_base = dict(cS=S.counts.get("cS", 0), cV=S.counts.get("cV", 0))
        exp_mark = {}
        for jt in range(it + 1):
            j0 = jt * JT
            kb = kTj[jt % 2]

            def ld_k(eng, j0=j0, kb=kb):
                eng.dma_start(out=kb.rearrange("p (d j) -> p d j", j=512),
                              in_=kT_d.rearrange("(d p) t -> p d t", p=128)[:, :, j0:j0 + 512]
                              ).then_inc(sems["cL"], 16)
                return None
            S.step("sync", ld_k, wait=("cT",), inc="cL", dma_n=1)

            for isub in range(4):
                ti = it * 4 + isub
                sbank = PS[0] if isub % 2 == 0 else PS[1]

                flat = jt * 4 + isub

                def s_mm(eng, isub=isub, kb=kb, sbank=sbank):
                    last = None
                    for dd in range(ND):
                        last = eng.matmul(
                            sbank[:, :],
                            qTi[:, dd * 512 + isub * 128:dd * 512 + (isub + 1) * 128],
                            kb[:, dd * 512:(dd + 1) * 512],
                            start=(dd == 0), stop=(dd == ND - 1))
                    return last
                S.step("tensor", s_mm, wait=("cL", "cS", "cV"), inc="cT",
                       wait_at=dict(passa_base,
                                    **(dict(cS=exp_mark[flat - 2])
                                       if flat >= 2 else {})))

                def m_exp(eng, sbank=sbank):
                    return eng.activation(out=P0t[:], in_=sbank[:, :], func=AF.Exp,
                                          scale=1.0 / 32.0)
                S.step("scalar", m_exp, wait=("cT",), inc="cS")
                exp_mark[flat] = S.counts.get("cS", 0)

                def m_sc(eng, ti=ti, j0=j0):
                    eng.tensor_scalar(out=ms[0][:], in0=QPXR[:, j0:j0 + 512],
                                      scalar1=qpN[:, 2 * ti:2 * ti + 1], scalar2=None,
                                      op0=ALU.subtract)
                    return eng.tensor_scalar(out=ms[1][:], in0=QPYR[:, j0:j0 + 512],
                                             scalar1=qpN[:, 2 * ti + 1:2 * ti + 2],
                                             scalar2=None, op0=ALU.subtract)
                S.step("vector", m_sc, wait=("cV", "cT", "cS"), inc="cV")

                def m_sq(eng):
                    eng.activation(out=ms[2][:], in_=ms[0][:], func=AF.Square)
                    eng.activation(out=ms[3][:], in_=ms[1][:], func=AF.Square)
                    return eng.activation(out=ms[4][:], in_=ms[0][:], func=AF.Identity,
                                          scale=CST[:, 3:4])
                S.step("scalar", m_sq, wait=("cV",), inc="cS")

                def m_neg(eng):
                    return eng.activation(out=ms[0][:], in_=ms[1][:], func=AF.Identity,
                                          scale=-1.0)
                S.step("scalar", m_neg, inc="cS")

                def m_v(eng, it=it, jt=jt, isub=isub):
                    eng.tensor_add(ms[5][:], ms[2][:], ms[3][:])
                    eng.tensor_max(ms[2][:], ms[0][:], ms[1][:])
                    eng.tensor_scalar(out=ms[3][:], in0=ms[5][:],
                                      scalar1=CST[:, 6:7], scalar2=None,
                                      op0=ALU.is_le)
                    eng.tensor_tensor(ms[5][:], ms[4][:], ms[2][:], ALU.is_gt)
                    eng.tensor_mul(ms[0][:], ms[3][:], ms[5][:])
                    eng.tensor_sub(ms[1][:], ms[3][:], ms[0][:])
                    if jt == it:
                        eng.tensor_mul(ms[2][:], ms[1][:],
                                       tril[:, isub * JT:(isub + 1) * JT])
                        mfin = ms[2]
                    else:
                        mfin = ms[1]
                    return eng.scalar_tensor_tensor(
                        out=Pmt[:], in0=mfin[:], scalar=1.0, in1=P0t[:],
                        op0=ALU.mult, op1=ALU.mult,
                        accum_out=rs_parts[:, isub * 8 + jt:isub * 8 + jt + 1])
                S.step("vector", m_v, wait=("cS", "cT"), inc="cV")

                for jj4 in range(4):
                    def p_tr(eng, jj4=jj4):
                        return eng.transpose(PSB[:, 0:128],
                                             Pmt[:, jj4 * 128:(jj4 + 1) * 128],
                                             identb[:])
                    S.step("tensor", p_tr, wait=("cV", "cG"), inc="cT")

                    def p_ev(eng, jj4=jj4, isub=isub):
                        return eng.tensor_copy(
                            PTev[:, jj4 * 512 + isub * 128:jj4 * 512 + (isub + 1) * 128],
                            PSB[:, 0:128])
                    S.step("vector", p_ev, wait=("cT", "cG"), inc="cV")

            def st_pt(eng, jt=jt):
                for jj4 in range(4):
                    js = jt * 4 + jj4
                    eng.dma_start(out=pT_d[js * 128:(js + 1) * 128, :],
                                  in_=PTev[:, jj4 * 512:(jj4 + 1) * 512]
                                  ).then_inc(sems["cG"], 16)
                return None
            S.step("gpsimd", st_pt, wait=("cV",), inc="cG", dma_n=4)

        def rs_red(eng, it=it):
            last = None
            for isub in range(4):
                last = eng.reduce_sum(rs4[:, isub:isub + 1],
                                      rs_parts[:, isub * 8:isub * 8 + it + 1],
                                      axis=AX.X)
            return last
        S.step("vector", rs_red, wait=("cV",), inc="cV")

        def rs_rec(eng):
            vgap(eng)
            return eng.reciprocal(rr4[:], rs4[:])
        S.step("vector", rs_rec, inc="cV", pad=True)

        njs = (it + 1) * 4
        passb_cg = S.counts.get("cG", 0)
        for dhalf in range(2):
            passb_base = dict(cT=S.counts.get("cT", 0), cG=passb_cg)
            mm_mark = {}
            for js in range(njs):
                vb = Vbuf[js % 2]
                pb = PTrd[js % 2]

                def ld_v(eng, js=js, dhalf=dhalf, vb=vb, pb=pb):
                    eng.dma_start(out=vb[:],
                                  in_=V_d[js * 128:(js + 1) * 128,
                                          dhalf * 512:(dhalf + 1) * 512]
                                  ).then_inc(sems["cL"], 16)
                    eng.dma_start(out=pb[:], in_=pT_d[js * 128:(js + 1) * 128, :]
                                  ).then_inc(sems["cL"], 16)
                    return None
                S.step("sync", ld_v, wait=("cT", "cG"), inc="cL", dma_n=2,
                       wait_at=dict(passb_base,
                                    **(dict(cT=mm_mark[js - 2]) if js >= 2
                                       else {})))

                def c_mm(eng, js=js, vb=vb, pb=pb, first=(js == 0),
                         last_js=(js == njs - 1)):
                    lasti = None
                    for ds in range(4):
                        lasti = eng.matmul(
                            PS[2 + ds][:, :],
                            vb[:, ds * 128:(ds + 1) * 128],
                            pb[:],
                            start=first, stop=last_js)
                    return lasti
                S.step("tensor", c_mm, wait=("cL", "cV"), inc="cT")
                mm_mark[js] = S.counts.get("cT", 0)

            def c_ev(eng, dhalf=dhalf):
                lasti = None
                for ds in range(4):
                    dd = dhalf * 4 + ds
                    lasti = eng.tensor_copy(ctxT[:, dd * 512:(dd + 1) * 512],
                                            PS[2 + ds][:, :])
                return lasti
            S.step("vector", c_ev, wait=("cT",), inc="cV")

        for isub in range(4):
            ti = it * 4 + isub

            def ao_mm(eng, isub=isub):
                lasti = None
                for half in range(2):
                    for dd in range(ND):
                        lasti = eng.matmul(
                            PS[0 + half][:, :],
                            ctxT[:, dd * 512 + isub * 128:dd * 512 + (isub + 1) * 128],
                            WoTs[:, dd * D + half * 512:dd * D + (half + 1) * 512],
                            start=(dd == 0), stop=(dd == ND - 1))
                return lasti
            S.step("tensor", ao_mm, wait=("cV", "cS"), inc="cT")

            def qo_tr(eng, ti=ti):
                return eng.transpose(PS[4][0:2, 0:128],
                                     rembN[:, 2 * ti:2 * ti + 2], identf[:])
            S.step("tensor", qo_tr, wait=("cV",), inc="cT")

            def qo_ev(eng):
                return eng.tensor_copy(remb2[:], PS[4][0:2, 0:128])
            S.step("vector", qo_ev, wait=("cT",), inc="cV", pad=True)

            def qo_mm(eng, ti=ti):
                eng.matmul(PS[2][:, :], remb2[:], WfqT[:, 0:512],
                           start=True, stop=True)
                return eng.matmul(PS[3][:, :], remb2[:], WfqT[:, 512:1024],
                                  start=True, stop=True)
            S.step("tensor", qo_mm, wait=("cV",), inc="cT")

            def ld_xf(eng, ti=ti, isub=isub):
                eng.dma_start(out=xq_buf[isub % 2][:],
                              in_=x_in[ti * 128:(ti + 1) * 128, :]
                              ).then_inc(sems["cL"], 16)
                eng.dma_start(out=fq_buf[isub % 2][:],
                              in_=ffn0_d[ti * 128:(ti + 1) * 128, :]
                              ).then_inc(sems["cL"], 16)
                return None
            S.step("sync", ld_xf, wait=("cV",), inc="cL", dma_n=2)

            def comb_v(eng, ti=ti, isub=isub):
                eng.tensor_scalar(out=cb[0][:, 0:512], in0=PS[0][:, :],
                                  scalar1=rr4[:, isub:isub + 1], scalar2=None,
                                  op0=ALU.mult)
                eng.tensor_scalar(out=cb[0][:, 512:1024], in0=PS[1][:, :],
                                  scalar1=rr4[:, isub:isub + 1], scalar2=None,
                                  op0=ALU.mult)
                eng.tensor_add(cb[1][:], cb[0][:], boc_bc[:])
                eng.scalar_tensor_tensor(out=cb[2][:, 0:512], in0=PS[2][:, :],
                                         scalar=0.1, in1=cb[1][:, 0:512],
                                         op0=ALU.mult, op1=ALU.add)
                eng.scalar_tensor_tensor(out=cb[2][:, 512:1024], in0=PS[3][:, :],
                                         scalar=0.1, in1=cb[1][:, 512:1024],
                                         op0=ALU.mult, op1=ALU.add)
                return eng.tensor_add(cb[0][:], cb[2][:], bfqc_bc[:])
            S.step("vector", comb_v, wait=("cT", "cL"), inc="cV")

            def comb_s(eng, ti=ti, isub=isub):
                return eng.activation(out=cb[3][:], in_=xq_buf[isub % 2][:],
                                      func=AF.Identity, scale=stat_r[:, ti:ti + 1],
                                      bias=stat_mb[:, ti:ti + 1])
            S.step("scalar", comb_s, wait=("cL",), inc="cS")

            def comb_v2(eng, ti=ti, isub=isub):
                eng.tensor_add(cb[1][:], cb[0][:], cb[3][:])
                eng.tensor_add(cb[2][:], cb[1][:], xq_buf[isub % 2][:])
                return eng.tensor_add(x1N[:], cb[2][:], fq_buf[isub % 2][:])
            S.step("vector", comb_v2, wait=("cS", "cV", "cG"), inc="cV")

            def st_x1(eng, ti=ti):
                eng.dma_start(out=x1_d[ti * 128:(ti + 1) * 128, :], in_=x1N[:]
                              ).then_inc(sems["cG"], 16)
                return None
            S.step("gpsimd", st_x1, wait=("cV",), inc="cG", dma_n=1)

            ic = isub % 4

            def l1_red(eng, ic=ic):
                nop8 = lambda: vgap(eng)
                eng.reduce_sum(l1tmp[:, ic:ic + 1], x1N[:], axis=AX.X, negate=True)
                nop8()
                return eng.tensor_scalar_mul(l1mb[:, ic:ic + 1],
                                             l1tmp[:, ic:ic + 1], 1.0 / D)
            S.step("vector", l1_red, wait=("cV",), inc="cV", pad=True)

            def l1_a(eng, ic=ic):
                eng.activation(out=ln1_sc[0][:], in_=x1N[:], func=AF.Identity,
                               bias=l1mb[:, ic:ic + 1])
                return eng.activation(out=q6s[0][:], in_=ln1_sc[0][:],
                                      func=AF.Square,
                                      accum_out=l1tmp[:, ic:ic + 1])
            S.step("scalar", l1_a, wait=("cV",), inc="cS", pad=True)

            def l1_v2(eng, ic=ic):
                return eng.tensor_scalar(out=l1r[:, ic:ic + 1],
                                         in0=l1tmp[:, ic:ic + 1],
                                         scalar1=1.0 / D, scalar2=EPS_LN,
                                         op0=ALU.mult, op1=ALU.add)
            S.step("vector", l1_v2, wait=("cS",), inc="cV", pad=True)

            def l1_s2(eng, ic=ic):
                return eng.activation(out=l1tmp[:, ic:ic + 1],
                                      in_=l1r[:, ic:ic + 1], func=AF.Sqrt)
            S.step("scalar", l1_s2, wait=("cV",), inc="cS", pad=True)

            def l1_v3(eng, ic=ic):
                nop8 = lambda: vgap(eng)
                eng.reciprocal(l1r[:, ic:ic + 1], l1tmp[:, ic:ic + 1])
                nop8()
                eng.tensor_mul(l1tmp[:, ic:ic + 1], l1mb[:, ic:ic + 1],
                               l1r[:, ic:ic + 1])
                nop8()
                return eng.tensor_scalar_mul(l1mb[:, ic:ic + 1],
                                             l1tmp[:, ic:ic + 1], 1.0)
            S.step("vector", l1_v3, wait=("cS",), inc="cV", pad=True)

            def l1_b(eng, ic=ic):
                return eng.activation(out=xn1N[:], in_=x1N[:], func=AF.Identity,
                                      scale=l1r[:, ic:ic + 1],
                                      bias=l1mb[:, ic:ic + 1])
            S.step("scalar", l1_b, wait=("cV",), inc="cS")

            def st_xn1(eng, ti=ti):
                eng.dma_start(out=xn1_d[ti * 128:(ti + 1) * 128, :], in_=xn1N[:]
                              ).then_inc(sems["cG"], 16)
                return None
            S.step("gpsimd", st_xn1, wait=("cS",), inc="cG", dma_n=1)

            def q6a(eng):
                nop8 = lambda: vgap(eng)
                eng.tensor_reduce(out=q6sc[0].rearrange("p (g o) -> p g o", o=1),
                                  in_=xn1N.rearrange("p (g k) -> p g k", k=32),
                                  axis=AX.X, op=ALU.max, apply_absolute_value=True)
                nop8()
                eng.tensor_scalar(out=q6sc[1][:], in0=q6sc[0][:],
                                  scalar1=CST[:, 7:8], scalar2=None, op0=ALU.mult)
                nop8()
                eng.tensor_scalar_max(q6sc[2][:], q6sc[1][:], 1e-30)
                nop8()
                eng.reciprocal(q6sc[3][:], q6sc[2][:])
                nop8()
                return eng.tensor_mul(
                    q6s[0][:],
                    xn1N.rearrange("p (g k) -> p g k", k=32),
                    q6sc[3].rearrange("p (g o) -> p g o", o=1).broadcast_to((128, 32, 32)))
            S.step("vector", q6a, wait=("cS",), inc="cV")

            def q6b(eng):
                M0 = CST[:, 1:2]
                w1_, w2_, w3_, w4_ = q6s[1], q6s[2], q6s[3], q6s[4]
                q = q6s[0]
                eng.tensor_scalar(out=w1_[:], in0=q[:], scalar1=-1.0, scalar2=None,
                                  op0=ALU.mult)
                eng.tensor_max(w2_[:], w1_[:], q[:])
                eng.tensor_single_scalar(out=w1_[:], in_=w2_[:], scalar=2.0,
                                         op=ALU.is_lt)
                eng.tensor_single_scalar(out=w3_[:], in_=w2_[:], scalar=4.0,
                                         op=ALU.is_lt)
                eng.tensor_scalar(out=w4_[:], in0=w3_[:], scalar1=2.0, scalar2=2.0,
                                  op0=ALU.mult, op1=ALU.add)
                eng.scalar_tensor_tensor(out=w2_[:], in0=w1_[:], scalar=4.0,
                                         in1=w4_[:], op0=ALU.mult, op1=ALU.add)
                eng.tensor_scalar(out=w4_[:], in0=w3_[:], scalar1=-0.25, scalar2=0.5,
                                  op0=ALU.mult, op1=ALU.add)
                eng.scalar_tensor_tensor(out=w3_[:], in0=w1_[:], scalar=-0.125,
                                         in1=w4_[:], op0=ALU.mult, op1=ALU.add)
                eng.tensor_mul(w1_[:], q[:], w2_[:])
                eng.tensor_scalar(out=w4_[:], in0=w1_[:], scalar1=M0, scalar2=None,
                                  op0=ALU.add)
                eng.tensor_scalar(out=w2_[:], in0=w4_[:], scalar1=M0, scalar2=None,
                                  op0=ALU.subtract)
                eng.tensor_sub(w4_[:], w2_[:], w1_[:])
                eng.tensor_single_scalar(out=w1_[:], in_=w4_[:], scalar=0.5,
                                         op=ALU.is_equal)
                eng.tensor_sub(w4_[:], w2_[:], w1_[:])
                eng.tensor_mul(q6s[0][:], w4_[:], w3_[:])
                return eng.tensor_mul(
                    xdN.rearrange("p (g k) -> p g k", k=32),
                    q6s[0].rearrange("p (g k) -> p g k", k=32),
                    q6sc[2].rearrange("p (g o) -> p g o", o=1).broadcast_to((128, 32, 32)))
            S.step("vector", q6b, inc="cV")

            for d0 in range(0, ND, 2):
                def t1(eng, d0=d0):
                    eng.transpose(PS[4][:, 0:128], xn1N[:, d0 * 128:(d0 + 1) * 128],
                                  identf[:])
                    return eng.transpose(PS[5][:, 0:128],
                                         xn1N[:, (d0 + 1) * 128:(d0 + 2) * 128],
                                         identf[:])
                S.step("tensor", t1, wait=("cV", "cS", "cG"), inc="cT")

                def e1(eng, d0=d0):
                    eng.tensor_copy(pev3[:, 0:128], PS[4][:, 0:128])
                    return eng.tensor_copy(pev3[:, 128:256], PS[5][:, 0:128])
                S.step("vector", e1, wait=("cT",), inc="cV")

                def s1(eng, d0=d0, ti=ti):
                    eng.dma_start(out=xn1T_d[d0 * 128:(d0 + 1) * 128,
                                             ti * 128:(ti + 1) * 128],
                                  in_=pev3[:, 0:128]).then_inc(sems["cG"], 16)
                    eng.dma_start(out=xn1T_d[(d0 + 1) * 128:(d0 + 2) * 128,
                                             ti * 128:(ti + 1) * 128],
                                  in_=pev3[:, 128:256]).then_inc(sems["cG"], 16)
                    return None
                S.step("gpsimd", s1, wait=("cV",), inc="cG", dma_n=2)

                def t2(eng, d0=d0):
                    eng.transpose(PSB[:, 0:128], xdN[:, d0 * 128:(d0 + 1) * 128],
                                  identb[:])
                    return eng.transpose(PSB[:, 128:256],
                                         xdN[:, (d0 + 1) * 128:(d0 + 2) * 128],
                                         identb[:])
                S.step("tensor", t2, wait=("cV", "cG"), inc="cT")

                def e2(eng, d0=d0):
                    eng.tensor_copy(pev3[:, 0:128], PSB[:, 0:128])
                    return eng.tensor_copy(pev3[:, 128:256], PSB[:, 128:256])
                S.step("vector", e2, wait=("cT",), inc="cV")

                def s2(eng, d0=d0, ti=ti):
                    eng.dma_start(out=xdT_d[d0 * 128:(d0 + 1) * 128,
                                            ti * 128:(ti + 1) * 128],
                                  in_=pev3[:, 0:128]).then_inc(sems["cG"], 16)
                    eng.dma_start(out=xdT_d[(d0 + 1) * 128:(d0 + 2) * 128,
                                            ti * 128:(ti + 1) * 128],
                                  in_=pev3[:, 128:256]).then_inc(sems["cG"], 16)
                    return None
                S.step("gpsimd", s2, wait=("cV",), inc="cG", dma_n=2)

    for it in range(T // IT):
        flash_itile(it)

    barrier()
    pop_scope()

    push_scope()
    bL1c_bc = sb("bL1c_bc", (128, D))
    mk_bcast(bL1, bL1c_bc, 1.0)
    wdT = sb("wdT", (128, ND * D), BF16)
    wqN = [sb(f"wqN{i}", (128, D)) for i in range(2)]
    wqb = sb("wqb", (128, D), BF16)
    p4sc = [sb(f"p4sc{i}", (128, D)) for i in range(4)]
    p4bk = [sb(f"p4bk{i}", (128, 32)) for i in range(4)]
    xdTb = [sb(f"xdTb{i}", (128, ND * 128), BF16) for i in range(2)]
    f1ev = [sb(f"f1ev{i}", (128, D)) for i in range(2)]

    def q6_pipeline(src, dstbf, sc, bk, eng):
        nop8 = lambda: vgap(eng)
        M0 = CST[:, 1:2]
        eng.tensor_reduce(out=bk[0].rearrange("p (g o) -> p g o", o=1),
                          in_=src.rearrange("p (g k) -> p g k", k=32),
                          axis=AX.X, op=ALU.max, apply_absolute_value=True)
        nop8()
        eng.tensor_scalar(out=bk[1][:], in0=bk[0][:],
                          scalar1=CST[:, 7:8], scalar2=None, op0=ALU.mult)
        nop8()
        eng.tensor_scalar_max(bk[2][:], bk[1][:], 1e-30)
        nop8()
        eng.reciprocal(bk[3][:], bk[2][:])
        nop8()
        eng.tensor_mul(sc[0][:],
                       src.rearrange("p (g k) -> p g k", k=32),
                       bk[3].rearrange("p (g o) -> p g o", o=1).broadcast_to((128, 32, 32)))
        eng.tensor_scalar(out=sc[1][:], in0=sc[0][:], scalar1=-1.0, scalar2=None,
                          op0=ALU.mult)
        eng.tensor_max(sc[2][:], sc[1][:], sc[0][:])
        eng.tensor_single_scalar(out=sc[1][:], in_=sc[2][:], scalar=2.0, op=ALU.is_lt)
        eng.tensor_single_scalar(out=sc[3][:], in_=sc[2][:], scalar=4.0, op=ALU.is_lt)
        eng.tensor_scalar(out=sc[2][:], in0=sc[3][:], scalar1=2.0, scalar2=2.0,
                          op0=ALU.mult, op1=ALU.add)
        eng.scalar_tensor_tensor(out=sc[2][:], in0=sc[1][:], scalar=4.0, in1=sc[2][:],
                                 op0=ALU.mult, op1=ALU.add) \
            if False else eng.scalar_tensor_tensor(
                out=p4_k_tmp[:], in0=sc[1][:], scalar=4.0, in1=sc[2][:],
                op0=ALU.mult, op1=ALU.add)
        eng.tensor_scalar(out=sc[2][:], in0=sc[3][:], scalar1=-0.25, scalar2=0.5,
                          op0=ALU.mult, op1=ALU.add)
        eng.scalar_tensor_tensor(out=sc[3][:], in0=sc[1][:], scalar=-0.125,
                                 in1=sc[2][:], op0=ALU.mult, op1=ALU.add)
        eng.tensor_mul(sc[1][:], sc[0][:], p4_k_tmp[:])
        eng.tensor_scalar(out=sc[2][:], in0=sc[1][:], scalar1=M0, scalar2=None,
                          op0=ALU.add)
        eng.tensor_scalar(out=sc[0][:], in0=sc[2][:], scalar1=M0, scalar2=None,
                          op0=ALU.subtract)
        eng.tensor_sub(sc[2][:], sc[0][:], sc[1][:])
        eng.tensor_single_scalar(out=sc[1][:], in_=sc[2][:], scalar=0.5,
                                 op=ALU.is_equal)
        eng.tensor_sub(sc[2][:], sc[0][:], sc[1][:])
        eng.tensor_mul(sc[0][:], sc[2][:], sc[3][:])
        return eng.tensor_mul(
            dstbf.rearrange("p (g k) -> p g k", k=32),
            sc[0].rearrange("p (g k) -> p g k", k=32),
            bk[2].rearrange("p (g o) -> p g o", o=1).broadcast_to((128, 32, 32)))

    p4_k_tmp = sb("p4_k_tmp", (128, D))

    for rt in range(ND):
        def ldw1(eng, rt=rt):
            eng.dma_start(out=wqN[rt % 2][:], in_=wL1[rt * 128:(rt + 1) * 128, :]
                          ).then_inc(sems["cL"], 16)
            return None
        S.step("sync", ldw1, wait=("cT",), inc="cL", dma_n=1)

        def q6w(eng, rt=rt):
            return q6_pipeline(wqN[rt % 2], wqb, p4sc, p4bk, eng)
        S.step("vector", q6w, wait=("cL", "cT"), inc="cV")

        for d0 in range(0, ND, 2):
            def trw(eng, d0=d0):
                eng.transpose(PSB[:, 0:128], wqb[:, d0 * 128:(d0 + 1) * 128],
                              identb[:])
                return eng.transpose(PSB[:, 128:256],
                                     wqb[:, (d0 + 1) * 128:(d0 + 2) * 128],
                                     identb[:])
            S.step("tensor", trw, wait=("cV",), inc="cT")

            def evw(eng, d0=d0, rt=rt):
                eng.tensor_copy(wdT[:, d0 * D + rt * 128:d0 * D + (rt + 1) * 128],
                                PSB[:, 0:128])
                return eng.tensor_copy(
                    wdT[:, (d0 + 1) * D + rt * 128:(d0 + 1) * D + (rt + 1) * 128],
                    PSB[:, 128:256])
            S.step("vector", evw, wait=("cT",), inc="cV")

    for tt in range(NT):
        xb = xdTb[tt % 2]

        def ldxd(eng, tt=tt, xb=xb):
            eng.dma_start(out=xb.rearrange("p (d j) -> p d j", j=128),
                          in_=xdT_d.rearrange("(d p) t -> p d t", p=128)
                          [:, :, tt * 128:(tt + 1) * 128]).then_inc(sems["cL"], 16)
            return None
        S.step("sync", ldxd, wait=("cT",), inc="cL", dma_n=1)

        for half in range(2):
            def mmf1(eng, xb=xb, half=half):
                last = None
                for dd in range(ND):
                    last = eng.matmul(
                        PS[half][:, :],
                        xb[:, dd * 128:(dd + 1) * 128],
                        wdT[:, dd * D + half * 512:dd * D + (half + 1) * 512],
                        start=(dd == 0), stop=(dd == ND - 1))
                return last
            S.step("tensor", mmf1, wait=("cL", "cV", "cG"), inc="cT")

        def evf1(eng, tt=tt):
            eng.tensor_copy(f1ev[0][:, 0:512], PS[0][:, :])
            eng.tensor_copy(f1ev[0][:, 512:1024], PS[1][:, :])
            return eng.tensor_add(f1ev[1][:], f1ev[0][:], bL1c_bc[:])
        S.step("vector", evf1, wait=("cT", "cG"), inc="cV")

        def stf1(eng, tt=tt):
            eng.dma_start(out=ffn1_d[tt * 128:(tt + 1) * 128, :], in_=f1ev[1][:]
                          ).then_inc(sems["cG"], 16)
            return None
        S.step("gpsimd", stf1, wait=("cV",), inc="cG", dma_n=1)

    barrier()
    pop_scope()

    push_scope()
    inpT = sb("inpT", (128, ND * 2 * DI), BF16)
    weight_T(inpw, 2 * DI, D, inpT)
    xprT = sb("xprT", (128, NDI * 96), BF16)
    ddtT = sb("ddtT", (64, DI), BF16)

    def ldxpr(eng):
        eng.dma_start(out=wld_buf[0:96, 0:DI], in_=xprw[:, :]).then_inc(sems["cL"], 16)
        return None
    S.step("sync", ldxpr, wait=("cT", "cV"), inc="cL", dma_n=1)

    for c in range(NDI):
        def trxp(eng, c=c):
            return eng.transpose(PS7[:, 0:96],
                                 wld_buf[0:96, c * 128:(c + 1) * 128],
                                 identf[0:96, 0:96])
        S.step("tensor", trxp, wait=("cL", "cV"), inc="cT")

        def evxp(eng, c=c):
            return eng.tensor_copy(xprT[:, c * 96:(c + 1) * 96], PS7[:, 0:96])
        S.step("vector", evxp, wait=("cT",), inc="cV", pad=True)

    def lddtw(eng):
        eng.dma_start(out=wld_buf[:, 0:DTR * NDI].rearrange("p (j r) -> p j r", r=DTR),
                      in_=dtw.rearrange("(j p) r -> p j r", p=128)
                      ).then_inc(sems["cL"], 16)
        return None
    S.step("sync", lddtw, wait=("cT", "cV"), inc="cL", dma_n=1)

    for c in range(NDI):
        def trdt(eng, c=c):
            return eng.transpose(PS7[0:64, 0:128],
                                 wld_buf[:, c * DTR:(c + 1) * DTR], identf[:])
        S.step("tensor", trdt, wait=("cL", "cV"), inc="cT")

        def evdt(eng, c=c):
            return eng.tensor_copy(ddtT[:, c * 128:(c + 1) * 128], PS7[0:64, 0:128])
        S.step("vector", evdt, wait=("cT",), inc="cV", pad=True)

    xnTb = [sb(f"xnTb{i}", (128, ND * MC), BF16) for i in range(2)]
    u_raw = sb("u_raw", (128, NDI * MC))
    ustage = sb("ustage", (128, NDI * MC), BF16)
    u_ext = sb("u_ext", (128, MC + 3))
    cv_t = [sb(f"cv_t{i}", (128, MC)) for i in range(2)]
    halo = sb("halo", (128, NDI * 3))
    zev = sb("zev", (128, MC), BF16)
    xdbl = sb("xdbl", (96, MC))
    dtT_bf = sb("dtT_bf", (64, MC), BF16)
    dl_f = [sb(f"dl_f{i}", (128, MC)) for i in range(4)]
    dl_b = sb("dl_b", (128, MC), BF16)
    w_b = sb("w_b", (128, MC), BF16)

    def halo_z(eng):
        return eng.memset(halo[:], 0.0)
    S.step("vector", halo_z, inc="cV")

    for cc in range(NMC):
        ch0 = cc * MC
        xb = xnTb[cc % 2]

        def ldxn1(eng, ch0=ch0, xb=xb):
            eng.dma_start(out=xb.rearrange("p (d j) -> p d j", j=MC),
                          in_=xn1T_d.rearrange("(d p) t -> p d t", p=128)
                          [:, :, ch0:ch0 + MC]).then_inc(sems["cL"], 16)
            return None
        S.step("sync", ldxn1, wait=("cT",), inc="cL", dma_n=1)

        m1_base = dict(cV=S.counts.get("cV", 0), cS=S.counts.get("cS", 0),
                       cG=S.counts.get("cG", 0))
        ev_marks = {}
        for m in range(32):
            bank = PS[m % 2]

            def mmxz(eng, m=m, xb=xb, bank=bank):
                last = None
                for c in range(ND):
                    last = eng.matmul(
                        bank[:, :],
                        inpT[:, c * (2 * DI) + m * 128:c * (2 * DI) + (m + 1) * 128],
                        xb[:, c * MC:(c + 1) * MC],
                        start=(c == 0), stop=(c == ND - 1))
                return last
            S.step("tensor", mmxz, wait=("cL", "cV", "cS", "cG"), inc="cT",
                   wait_at=dict(m1_base, **(ev_marks.get(m - 2, {}))))

            if m < 16:
                def evu(eng, m=m, bank=bank):
                    return eng.tensor_copy(u_raw[:, m * MC:(m + 1) * MC], bank[:, :])
                S.step("vector", evu, wait=("cT",), inc="cV")
                ev_marks[m] = dict(cV=S.counts.get("cV", 0))
            else:
                def evz(eng, bank=bank):
                    return eng.activation(out=zev[:], in_=bank[:, :], func=AF.Silu)
                S.step("scalar", evz, wait=("cT", "cG"), inc="cS")
                ev_marks[m] = dict(cS=S.counts.get("cS", 0))

                def stz(eng, m=m, ch0=ch0):
                    dt = m - 16
                    eng.dma_start(out=sz_d[dt * 128:(dt + 1) * 128, ch0:ch0 + MC],
                                  in_=zev[:]).then_inc(sems["cG"], 16)
                    return None
                S.step("gpsimd", stz, wait=("cS",), inc="cG", dma_n=1)

        for dt in range(NDI):
            def cva(eng, dt=dt):
                eng.tensor_copy(u_ext[:, 0:3], halo[:, dt * 3:(dt + 1) * 3])
                eng.tensor_copy(u_ext[:, 3:MC + 3], u_raw[:, dt * MC:(dt + 1) * MC])
                eng.tensor_copy(halo[:, dt * 3:(dt + 1) * 3],
                                u_raw[:, dt * MC + MC - 3:dt * MC + MC])
                eng.tensor_scalar(out=cv_t[0][:], in0=u_ext[:, 0:MC],
                                  scalar1=convwc[:, dt * 4:dt * 4 + 1], scalar2=None,
                                  op0=ALU.mult)
                eng.scalar_tensor_tensor(out=cv_t[1][:], in0=u_ext[:, 1:MC + 1],
                                         scalar=convwc[:, dt * 4 + 1:dt * 4 + 2],
                                         in1=cv_t[0][:], op0=ALU.mult, op1=ALU.add)
                eng.scalar_tensor_tensor(out=cv_t[0][:], in0=u_ext[:, 2:MC + 2],
                                         scalar=convwc[:, dt * 4 + 2:dt * 4 + 3],
                                         in1=cv_t[1][:], op0=ALU.mult, op1=ALU.add)
                return eng.scalar_tensor_tensor(
                    out=cv_t[1][:], in0=u_ext[:, 3:MC + 3],
                    scalar=convwc[:, dt * 4 + 3:dt * 4 + 4],
                    in1=cv_t[0][:], op0=ALU.mult, op1=ALU.add)
            S.step("vector", cva, wait=("cS", "cG", "cV"), inc="cV",
               wait_at=dict(cG=m1_base["cG"]))

            def cvs(eng, dt=dt):
                return eng.activation(out=ustage[:, dt * MC:(dt + 1) * MC],
                                      in_=cv_t[1][:], func=AF.Silu,
                                      bias=convbc[:, dt:dt + 1])
            S.step("scalar", cvs, wait=("cV",), inc="cS")

            def stu(eng, dt=dt, ch0=ch0):
                eng.dma_start(out=u_d[dt * 128:(dt + 1) * 128, ch0:ch0 + MC],
                              in_=ustage[:, dt * MC:(dt + 1) * MC]
                              ).then_inc(sems["cG"], 16)
                return None
            S.step("gpsimd", stu, wait=("cS",), inc="cG", dma_n=1)

        def mmxd(eng):
            last = None
            for c in range(NDI):
                last = eng.matmul(PS[2][0:96, :], xprT[:, c * 96:(c + 1) * 96],
                                  ustage[:, c * MC:(c + 1) * MC],
                                  start=(c == 0), stop=(c == NDI - 1))
            return last
        S.step("tensor", mmxd, wait=("cS", "cV"), inc="cT")

        def evxd(eng):
            eng.tensor_copy(xdbl[:], PS[2][0:96, :])
            return eng.tensor_copy(dtT_bf[:], PS[2][0:64, :])
        S.step("vector", evxd, wait=("cT", "cG"), inc="cV", pad=True)

        def stbc(eng, ch0=ch0):
            eng.dma_start(out=bm_d[:, ch0:ch0 + MC], in_=xdbl[64:80, :]
                          ).then_inc(sems["cG"], 16)
            eng.dma_start(out=cm_d[:, ch0:ch0 + MC], in_=xdbl[80:96, :]
                          ).then_inc(sems["cG"], 16)
            return None
        S.step("gpsimd", stbc, wait=("cV",), inc="cG", dma_n=2)

        for dt in range(NDI):
            def mmdt(eng, dt=dt):
                return eng.matmul(PS[3][:, :], ddtT[:, dt * 128:(dt + 1) * 128],
                                  dtT_bf[:], start=True, stop=True)
            S.step("tensor", mmdt, wait=("cV", "cS"), inc="cT")

            def spl(eng, dt=dt):
                eng.activation(out=dl_f[0][:], in_=PS[3][:, :], func=AF.Relu,
                               bias=dtbc[:, dt:dt + 1])
                eng.activation(out=dl_f[1][:], in_=PS[3][:, :], func=AF.Abs,
                               bias=dtbc[:, dt:dt + 1])
                eng.activation(out=dl_f[2][:], in_=dl_f[1][:], func=AF.Exp,
                               scale=-1.0)
                return eng.activation(out=dl_f[3][:], in_=dl_f[2][:], func=AF.Ln,
                                      bias=1.0)
            S.step("scalar", spl, wait=("cT", "cV"), inc="cS")

            def dl_v(eng, dt=dt):
                eng.tensor_add(dl_f[1][:], dl_f[0][:], dl_f[3][:])
                eng.tensor_copy(dl_b[:], dl_f[1][:])
                return eng.tensor_mul(w_b[:], dl_f[1][:],
                                      ustage[:, dt * MC:(dt + 1) * MC])
            S.step("vector", dl_v, wait=("cS", "cG"), inc="cV")

            def stdl(eng, dt=dt, ch0=ch0):
                eng.dma_start(out=dl_d[dt * 128:(dt + 1) * 128, ch0:ch0 + MC],
                              in_=dl_b[:]).then_inc(sems["cG"], 16)
                eng.dma_start(out=wsc_d[dt * 128:(dt + 1) * 128, ch0:ch0 + MC],
                              in_=w_b[:]).then_inc(sems["cG"], 16)
                return None
            S.step("gpsimd", stdl, wait=("cV",), inc="cG", dma_n=2)

    barrier()
    pop_scope()

    push_scope()
    M2C = 1024
    NM2 = T // M2C
    outT = sb("outT", (128, NDI * D), BF16)
    weight_T(outw, D, DI, outT)
    BBs = sb("BBs", (128, DST * M2C), BF16)
    CBs = sb("CBs", (128, DST * M2C), BF16)
    brow2 = sb("brow2", (1, M2C))
    crow2 = sb("crow2", (1, M2C))
    dlb2 = [sb(f"dlb2{i}", (128, M2C), BF16) for i in range(2)]
    wb2 = [sb(f"wb2{i}", (128, M2C), BF16) for i in range(2)]
    ub2 = [sb(f"ub2{i}", (128, M2C), BF16) for i in range(2)]
    szb2 = [sb(f"szb2{i}", (128, M2C), BF16) for i in range(2)]
    a_ring = [sb(f"a_ring{i}", (128, M2C), BF16) for i in range(4)]
    b_buf = [sb(f"b_buf{i}", (128, M2C), BF16) for i in range(2)]
    h_buf = [sb(f"h_buf{i}", (128, M2C), BF16) for i in range(2)]
    p_buf = [sb("p_buf0", (128, M2C), BF16)] * 2
    yacc = [sb(f"yacc{i}", (128, M2C), BF16) for i in range(2)]
    ud_f = sb("ud_f", (128, M2C), BF16)
    y3_f = sb("y3_f", (128, M2C), BF16)
    y2st = sb("y2st", (128, NDI * M2C), BF16)
    hm_ev = sb("hm_ev", (128, M2C), BF16)

    for cc in range(NM2):
        ch0 = cc * M2C

        for n in range(DST):
            def ldrow(eng, n=n, ch0=ch0):
                eng.dma_start(out=brow2[:], in_=bm_d[n:n + 1, ch0:ch0 + M2C]
                              ).then_inc(sems["cL"], 16)
                eng.dma_start(out=crow2[:], in_=cm_d[n:n + 1, ch0:ch0 + M2C]
                              ).then_inc(sems["cL"], 16)
                return None
            S.step("sync", ldrow, wait=("cV",), inc="cL", dma_n=2)

            def bbmm(eng, n=n):
                eng.matmul(PS[0][:, :], onesf[:], brow2[:, 0:512],
                           start=True, stop=True)
                eng.matmul(PS[1][:, :], onesf[:], brow2[:, 512:1024],
                           start=True, stop=True)
                eng.matmul(PS[2][:, :], onesf[:], crow2[:, 0:512],
                           start=True, stop=True)
                return eng.matmul(PS[3][:, :], onesf[:], crow2[:, 512:1024],
                                  start=True, stop=True)
            S.step("tensor", bbmm, wait=("cL", "cV"), inc="cT")

            def bbev(eng, n=n):
                eng.tensor_copy(BBs[:, n * M2C:n * M2C + 512], PS[0][:, :])
                eng.tensor_copy(BBs[:, n * M2C + 512:(n + 1) * M2C], PS[1][:, :])
                eng.tensor_copy(CBs[:, n * M2C:n * M2C + 512], PS[2][:, :])
                return eng.tensor_copy(CBs[:, n * M2C + 512:(n + 1) * M2C],
                                       PS[3][:, :])
            S.step("vector", bbev, wait=("cT",), inc="cV")

        for dt in range(NDI):
            dlb, wb = dlb2[dt % 2], wb2[dt % 2]

            def lddw(eng, dt=dt, ch0=ch0, dlb=dlb, wb=wb):
                eng.dma_start(out=dlb[:], in_=dl_d[dt * 128:(dt + 1) * 128,
                                                  ch0:ch0 + M2C]
                              ).then_inc(sems["cL"], 16)
                eng.dma_start(out=wb[:], in_=wsc_d[dt * 128:(dt + 1) * 128,
                                                   ch0:ch0 + M2C]
                              ).then_inc(sems["cL"], 16)
                return None
            S.step("sync", lddw, wait=("cV", "cS"), inc="cL", dma_n=2)

            def a_pair(eng, dt=dt, g2=0, dlb=dlb):
                last = None
                for i in range(2):
                    n = g2 * 2 + i
                    last = eng.activation(
                        out=a_ring[(g2 % 2) * 2 + i][:], in_=dlb[:],
                        func=AF.Exp,
                        scale=Ac[:, dt * DST + n:dt * DST + n + 1])
                return last

            def scan_pair(eng, dt=dt, g2=0, wb=wb):
                last = None
                for i in range(2):
                    n = g2 * 2 + i
                    col = dt * DST + n
                    eng.tensor_mul(b_buf[i % 2][:], wb[:],
                                   BBs[:, n * M2C:(n + 1) * M2C])
                    eng.tensor_tensor_scan(
                        out=h_buf[i % 2][:], data0=a_ring[(g2 % 2) * 2 + i][:],
                        data1=b_buf[i % 2][:],
                        initial=scan_st[:, col:col + 1],
                        op0=ALU.mult, op1=ALU.add)
                    eng.tensor_mul(p_buf[i % 2][:], h_buf[i % 2][:],
                                   CBs[:, n * M2C:(n + 1) * M2C])
                    if n == 0:
                        eng.tensor_copy(yacc[1][:], p_buf[i % 2][:])
                    else:
                        eng.tensor_add(yacc[(n + 1) % 2][:], yacc[n % 2][:],
                                       p_buf[i % 2][:])
                    last = eng.tensor_copy(scan_st[:, col:col + 1],
                                           h_buf[i % 2][:, M2C - 1:M2C])
                return last

            import functools as _ft2
            amark = {}
            S.step("scalar", _ft2.partial(a_pair, g2=0),
                   wait=("cL", "cV"), inc="cS")
            amark[0] = dict(cS=S.counts.get("cS", 0))
            for g2 in range(8):
                if g2 + 1 < 8:
                    S.step("scalar", _ft2.partial(a_pair, g2=g2 + 1),
                           wait=("cL", "cV"), inc="cS")
                    amark[g2 + 1] = dict(cS=S.counts.get("cS", 0))
                S.step("vector", _ft2.partial(scan_pair, g2=g2),
                       wait=("cS",), inc="cV", wait_at=amark[g2])

            def ldusz(eng, dt=dt, ch0=ch0):
                eng.dma_start(out=ub2[dt % 2][:], in_=u_d[dt * 128:(dt + 1) * 128,
                                                         ch0:ch0 + M2C]
                              ).then_inc(sems["cL"], 16)
                eng.dma_start(out=szb2[dt % 2][:], in_=sz_d[dt * 128:(dt + 1) * 128,
                                                           ch0:ch0 + M2C]
                              ).then_inc(sems["cL"], 16)
                return None
            S.step("sync", ldusz, wait=("cV", "cS"), inc="cL", dma_n=2)

            def gate_s(eng, dt=dt):
                return eng.activation(out=ud_f[:], in_=ub2[dt % 2][:], func=AF.Identity,
                                      scale=Dc[:, dt:dt + 1])
            S.step("scalar", gate_s, wait=("cL", "cV"), inc="cS")

            def gate_v(eng, dt=dt):
                eng.tensor_add(y3_f[:], yacc[0][:], ud_f[:])
                return eng.tensor_mul(y2st[:, dt * M2C:(dt + 1) * M2C], y3_f[:],
                                      szb2[dt % 2][:])
            S.step("vector", gate_v, wait=("cS",), inc="cV")

        for m in range(ND):
            bank2 = (PS[2], PS[3]) if m % 2 == 0 else (PS[4], PS[5])

            def mmop(eng, m=m, bank2=bank2):
                last = None
                for k in range(NDI):
                    for half in range(2):
                        last = eng.matmul(
                            bank2[half][:, :],
                            outT[:, k * D + m * 128:k * D + (m + 1) * 128],
                            y2st[:, k * M2C + half * 512:k * M2C + (half + 1) * 512],
                            start=(k == 0), stop=(k == NDI - 1))
                return last
            S.step("tensor", mmop, wait=("cV", "cG"), inc="cT")

            def evop(eng, m=m, bank2=bank2):
                eng.tensor_copy(hm_ev[:, 0:512], bank2[0][:, :])
                return eng.tensor_copy(hm_ev[:, 512:1024], bank2[1][:, :])
            S.step("vector", evop, wait=("cT", "cG"), inc="cV")

            def stop_(eng, m=m, ch0=ch0):
                eng.dma_start(out=hm_d[m * 128:(m + 1) * 128, ch0:ch0 + M2C],
                              in_=hm_ev[:]).then_inc(sems["cG"], 16)
                return None
            S.step("gpsimd", stop_, wait=("cV",), inc="cG", dma_n=1)

    barrier()
    pop_scope()

    push_scope()
    b2c_bc = sb("b2c_bc", (128, D))
    nw_bc = sb("nw_bc", (128, D))
    fnw_bc = sb("fnw_bc", (128, D))
    mk_bcast(b2, b2c_bc, 1.0)
    mk_bcast(nw, nw_bc, 1.0)
    mk_bcast(fnw, fnw_bc, 1.0)
    w2T_dd = dsc("w2T_dd", (DFF, D), BF16)
    w1Ts = sb("w1Ts", (128, ND * DFF), BF16)
    weight_T(w1, DFF, D, w1Ts)

    w2ev = sb("w2ev", (128, 128), BF16)
    for rt in range(ND):
        def ldw2(eng, rt=rt):
            eng.dma_start(out=wld_buf[:, 0:DFF], in_=w2[rt * 128:(rt + 1) * 128, :]
                          ).then_inc(sems["cL"], 16)
            return None
        S.step("sync", ldw2, wait=("cT",), inc="cL", dma_n=1)
        for c in range(NDF):
            def trw2(eng, c=c):
                return eng.transpose(PS7[:, 0:128],
                                     wld_buf[:, c * 128:(c + 1) * 128], identf[:])
            S.step("tensor", trw2, wait=("cL", "cG"), inc="cT")

            def evw2(eng, c=c):
                return eng.tensor_copy(w2ev[:], PS7[:, 0:128])
            S.step("vector", evw2, wait=("cT", "cG"), inc="cV")

            def stw2(eng, c=c, rt=rt):
                eng.dma_start(out=w2T_dd[c * 128:(c + 1) * 128,
                                         rt * 128:(rt + 1) * 128],
                              in_=w2ev[:]).then_inc(sems["cG"], 16)
                return None
            S.step("gpsimd", stw2, wait=("cV",), inc="cG", dma_n=1)

    M3C = 256
    NS3 = M3C // 128
    hTb = [sb(f"hTb{i}", (128, ND * M3C), BF16) for i in range(2)]
    xn1b = [sb(f"xn1b{i}", (128, D)) for i in range(2)]
    vN = sb("vN", (128, D))
    h2a = sb("h2a", (128, D))
    h2N = sb("h2N", (128, NS3 * D))
    h2bf = sb("h2bf", (128, D), BF16)
    h2T = sb("h2T", (128, ND * M3C), BF16)
    gm = [sb(f"gm{i}", (128, M3C), BF16) for i in range(2)]
    w2Tb = [sb(f"w2Tb{i}", (128, D), BF16) for i in range(2)]
    m3r = sb("m3r", (128, 8))
    ffv = [sb(f"ffv{i}", (128, D)) for i in range(3)]
    x1b = [sb(f"x1b{i}", (128, D)) for i in range(2)]
    f1b = [sb(f"f1b{i}", (128, D)) for i in range(2)]
    x2o = sb("x2o", (128, D))

    for cc in range(T // M3C):
        ch0 = cc * M3C
        hb = hTb[cc % 2]

        def ldh(eng, ch0=ch0, hb=hb):
            eng.dma_start(out=hb.rearrange("p (d j) -> p d j", j=M3C),
                          in_=hm_d.rearrange("(d p) t -> p d t", p=128)
                          [:, :, ch0:ch0 + M3C]).then_inc(sems["cL"], 16)
            return None
        S.step("sync", ldh, wait=("cT",), inc="cL", dma_n=1)

        for tsub in range(NS3):
            ti = cc * NS3 + tsub

            def ldxn1c(eng, ti=ti, tsub=tsub):
                eng.dma_start(out=xn1b[tsub % 2][:],
                              in_=xn1_d[ti * 128:(ti + 1) * 128, :]
                              ).then_inc(sems["cL"], 16)
                return None
            S.step("sync", ldxn1c, wait=("cV",), inc="cL", dma_n=1)

            for d0 in range(0, ND, 2):
                def trh(eng, d0=d0, tsub=tsub, hb=hb):
                    eng.transpose(PSB[:, 0:128],
                                  hb[:, d0 * M3C + tsub * 128:d0 * M3C + (tsub + 1) * 128],
                                  identb[:])
                    return eng.transpose(
                        PSB[:, 128:256],
                        hb[:, (d0 + 1) * M3C + tsub * 128:(d0 + 1) * M3C + (tsub + 1) * 128],
                        identb[:])
                S.step("tensor", trh, wait=("cL", "cV"), inc="cT")

                def evh(eng, d0=d0, tsub=tsub):
                    eng.tensor_add(vN[:, d0 * 128:(d0 + 1) * 128], PSB[:, 0:128],
                                   xn1b[tsub % 2][:, d0 * 128:(d0 + 1) * 128])
                    return eng.tensor_add(vN[:, (d0 + 1) * 128:(d0 + 2) * 128],
                                          PSB[:, 128:256],
                                          xn1b[tsub % 2][:, (d0 + 1) * 128:(d0 + 2) * 128])
                S.step("vector", evh, wait=("cT", "cL"), inc="cV")

            def rms1_s(eng, tsub=tsub):
                return eng.activation(out=h2a[:], in_=vN[:], func=AF.Square,
                                      accum_out=m3r[:, tsub:tsub + 1])
            S.step("scalar", rms1_s, wait=("cV",), inc="cS", pad=True)

            def rms1_v2(eng, tsub=tsub):
                return eng.tensor_scalar(out=m3r[:, tsub + 4:tsub + 5],
                                         in0=m3r[:, tsub:tsub + 1],
                                         scalar1=1.0 / D, scalar2=EPS_RMS,
                                         op0=ALU.mult, op1=ALU.add)
            S.step("vector", rms1_v2, wait=("cS",), inc="cV", pad=True)

            def rms1_sq(eng, tsub=tsub):
                return eng.activation(out=m3r[:, tsub:tsub + 1],
                                      in_=m3r[:, tsub + 4:tsub + 5], func=AF.Sqrt)
            S.step("scalar", rms1_sq, wait=("cV",), inc="cS", pad=True)

            def rms1_r(eng, tsub=tsub):
                [eng.engine_nop() for _ in range(8)]
                return eng.reciprocal(m3r[:, tsub + 4:tsub + 5],
                                      m3r[:, tsub:tsub + 1])
            S.step("vector", rms1_r, wait=("cS",), inc="cV", pad=True)

            def rms1_f(eng, tsub=tsub):
                return eng.activation(out=h2a[:], in_=vN[:], func=AF.Identity,
                                      scale=m3r[:, tsub + 4:tsub + 5])
            S.step("scalar", rms1_f, wait=("cV",), inc="cS")

            def rms1_v(eng, tsub=tsub):
                eng.tensor_mul(h2N[:, tsub * D:(tsub + 1) * D], h2a[:], nw_bc[:])
                return eng.tensor_copy(h2bf[:], h2N[:, tsub * D:(tsub + 1) * D])
            S.step("vector", rms1_v, wait=("cS",), inc="cV")

            for d0 in range(0, ND, 2):
                def trh2(eng, d0=d0):
                    eng.transpose(PSB[:, 0:128], h2bf[:, d0 * 128:(d0 + 1) * 128],
                                  identb[:])
                    return eng.transpose(PSB[:, 128:256],
                                         h2bf[:, (d0 + 1) * 128:(d0 + 2) * 128],
                                         identb[:])
                S.step("tensor", trh2, wait=("cV",), inc="cT")

                def evh2(eng, d0=d0, tsub=tsub):
                    eng.tensor_copy(
                        h2T[:, d0 * M3C + tsub * 128:d0 * M3C + (tsub + 1) * 128],
                        PSB[:, 0:128])
                    return eng.tensor_copy(
                        h2T[:, (d0 + 1) * M3C + tsub * 128:(d0 + 1) * M3C + (tsub + 1) * 128],
                        PSB[:, 128:256])
                S.step("vector", evh2, wait=("cT",), inc="cV")

        def ldw2b(eng, m=0):
            eng.dma_start(out=w2Tb[m % 2][:], in_=w2T_dd[m * 128:(m + 1) * 128, :]
                          ).then_inc(sems["cL"], 16)
            return None

        def mmg(eng, m=0):
            last = None
            for k in range(ND):
                last = eng.matmul(PS[4 + m % 2][:, 0:M3C],
                                  w1Ts[:, k * DFF + m * 128:k * DFF + (m + 1) * 128],
                                  h2T[:, k * M3C:(k + 1) * M3C],
                                  start=(k == 0), stop=(k == ND - 1))
            return last

        def evg(eng, m=0):
            return eng.activation(out=gm[m % 2][:], in_=PS[4 + m % 2][:, 0:M3C],
                                  func=AF.Gelu, bias=b1c[:, m:m + 1])

        def mmff(eng, m=0):
            last = None
            for tsub in range(NS3):
                for half in range(2):
                    last = eng.matmul(
                        PS[tsub * 2 + half][:, :],
                        gm[m % 2][:, tsub * 128:(tsub + 1) * 128],
                        w2Tb[m % 2][:, half * 512:(half + 1) * 512],
                        start=(m == 0), stop=(m == NDF - 1))
            return last

        import functools as _ft
        evg_mark = {}
        ldw_mark = {}
        for m in range(NDF + 1):
            if m < NDF:
                S.step("sync", _ft.partial(ldw2b, m=m), wait=("cT",),
                       inc="cL", dma_n=1)
                ldw_mark[m] = dict(cL=S.counts.get("cL", 0))
                S.step("tensor", _ft.partial(mmg, m=m), wait=("cV", "cS"),
                       inc="cT",
                       wait_at=(evg_mark[m - 2] if m >= 2 else None))
                S.step("scalar", _ft.partial(evg, m=m), wait=("cT",), inc="cS")
                evg_mark[m] = dict(cS=S.counts.get("cS", 0))
            if m >= 1:
                w = ("cS", "cL", "cV") if m - 1 < 2 else ("cS", "cL")
                S.step("tensor", _ft.partial(mmff, m=m - 1), wait=w, inc="cT",
                       wait_at=dict(evg_mark[m - 1], **ldw_mark[m - 1]))

        for tsub in range(NS3):
            ti = cc * NS3 + tsub

            def ldx1f(eng, ti=ti, tsub=tsub):
                eng.dma_start(out=x1b[tsub % 2][:],
                              in_=x1_d[ti * 128:(ti + 1) * 128, :]
                              ).then_inc(sems["cL"], 16)
                eng.dma_start(out=f1b[tsub % 2][:],
                              in_=ffn1_d[ti * 128:(ti + 1) * 128, :]
                              ).then_inc(sems["cL"], 16)
                return None
            S.step("sync", ldx1f, wait=("cG",), inc="cL", dma_n=2)

            def ffev(eng, tsub=tsub):
                eng.tensor_add(ffv[0][:, 0:512], PS[tsub * 2][:, :], b2c_bc[:, 0:512])
                eng.tensor_add(ffv[0][:, 512:1024], PS[tsub * 2 + 1][:, :],
                               b2c_bc[:, 512:1024])
                return eng.tensor_add(ffv[1][:], ffv[0][:],
                                      h2N[:, tsub * D:(tsub + 1) * D])
            S.step("vector", ffev, wait=("cT",), inc="cV")

            def rms2_s(eng, tsub=tsub):
                return eng.activation(out=ffv[2][:], in_=ffv[1][:], func=AF.Square,
                                      accum_out=m3r[:, tsub:tsub + 1])
            S.step("scalar", rms2_s, wait=("cV", "cG"), inc="cS", pad=True)

            def rms2_v2(eng, tsub=tsub):
                return eng.tensor_scalar(out=m3r[:, tsub + 4:tsub + 5],
                                         in0=m3r[:, tsub:tsub + 1],
                                         scalar1=1.0 / D, scalar2=EPS_RMS,
                                         op0=ALU.mult, op1=ALU.add)
            S.step("vector", rms2_v2, wait=("cS",), inc="cV", pad=True)

            def rms2_sq(eng, tsub=tsub):
                return eng.activation(out=m3r[:, tsub:tsub + 1],
                                      in_=m3r[:, tsub + 4:tsub + 5], func=AF.Sqrt)
            S.step("scalar", rms2_sq, wait=("cV",), inc="cS", pad=True)

            def rms2_r(eng, tsub=tsub):
                [eng.engine_nop() for _ in range(8)]
                return eng.reciprocal(m3r[:, tsub + 4:tsub + 5],
                                      m3r[:, tsub:tsub + 1])
            S.step("vector", rms2_r, wait=("cS",), inc="cV", pad=True)

            def rms2_f(eng, tsub=tsub):
                return eng.activation(out=ffv[2][:], in_=ffv[1][:], func=AF.Identity,
                                      scale=m3r[:, tsub + 4:tsub + 5])
            S.step("scalar", rms2_f, wait=("cV",), inc="cS")

            def fin_v(eng, tsub=tsub):
                eng.tensor_mul(ffv[0][:], ffv[2][:], fnw_bc[:])
                eng.tensor_add(x2o[:], x1b[tsub % 2][:], ffv[0][:])
                return eng.tensor_add(ffv[2][:], x2o[:], f1b[tsub % 2][:])
            S.step("vector", fin_v, wait=("cS", "cL", "cG"), inc="cV")

            def st_out(eng, ti=ti):
                eng.dma_start(out=out[ti * 128:(ti + 1) * 128, :], in_=ffv[2][:]
                              ).then_inc(sems["cG"], 16)
                return None
            S.step("gpsimd", st_out, wait=("cV",), inc="cG", dma_n=1)

    barrier()
    pop_scope()

    scratch_map = dict(xn_dbg=xn_dbg, stats_dbg=stats_dbg, qpx_d=qpx_d, qpy_d=qpy_d,
                       qT_d=qT_d, kT_d=kT_d, V_d=V_d, ffn0_d=ffn0_d, x1_d=x1_d,
                       xn1_d=xn1_d, xn1T_d=xn1T_d, xdT_d=xdT_d, u_d=u_d,
                       dl_d=dl_d, wsc_d=wsc_d, sz_d=sz_d, hm_d=hm_d,
                       ffn1_d=ffn1_d)
    for name, shape, dt in dbg:
        src = scratch_map[name]

        def dmp(eng, src=src, dst=dbg_out[name]):
            eng.dma_start(out=dst, in_=src).then_inc(sems["cG"], 16)
            return None
        S.step("gpsimd", dmp, inc="cG", dma_n=1)
    if dbg:
        barrier()

    with nc.allow_non_contiguous_dma(reason="small one-time bias/const gathers"), \
            nc.Block() as block:
        @block.sync
        def _(eng):
            S.emit(eng, "sync", sems)

        @block.gpsimd
        def _(eng):
            S.emit(eng, "gpsimd", sems)

        @block.vector
        def _(eng):
            S.emit(eng, "vector", sems)

        @block.scalar
        def _(eng):
            S.emit(eng, "scalar", sems)

        @block.tensor
        def _(eng):
            S.emit(eng, "tensor", sems)

    ctx.close()
    return nc


def _shard_inputs(inputs):
    hc = _host_consts()
    x = np.ascontiguousarray(np.asarray(inputs["L0_attn_hidden_states"],
                                        dtype=np.float32))
    common = {}
    for k, v in inputs.items():
        if k == "L0_attn_hidden_states":
            continue
        a = np.ascontiguousarray(np.asarray(v, dtype=np.float32))
        if a.ndim == 1:
            a = a[None, :]
        common[k] = a
    common.update(hc)
    return [dict(common, x_in=np.ascontiguousarray(x[b])) for b in range(NCORES)]


def kernel(**inputs):
    maps = _shard_inputs(inputs)
    nc = build_nc()
    res = run_bass_kernel_spmd(nc, maps, core_ids=list(range(NCORES)))
    return np.stack([np.asarray(r["out"]) for r in res.results], axis=0)
